# version 13
# baseline (speedup 1.0000x reference)
import math
from collections import defaultdict
from contextlib import ExitStack

import numpy as np
import ml_dtypes

import concourse.bass as bass
import concourse.mybir as mybir
from concourse.alu_op_type import AluOpType
from concourse.bass_utils import run_bass_kernel_spmd

F32 = mybir.dt.float32
BF16 = mybir.dt.bfloat16
AF = mybir.ActivationFunctionType
AX = mybir.AxisListType
OP = AluOpType

B, S, D = 2, 4096, 4096
NH, NKV, HD = 32, 8, 128
TPC = 1024
NC = 32
SCALE = HD ** -0.5
EPS = 1e-5
LORA_SCALING = 2.0
CH_SAFETY = 1.01

ORDER = [6, 4, 2, 0, 7, 5, 3, 1]
SLOT_OF = {m: s for s, m in enumerate(ORDER)}
BLOCKS = ([6, 4, 2, 0], [7, 5, 3, 1])


def att_sched(tiles):
    out = []
    for lg in range(tiles[0] + 1):
        act = [m for m in tiles if m >= lg]
        out.append((lg, 128 * len(act), lg in tiles, act))
    return out


def alibi_slopes(n):
    cp2 = 2 ** math.floor(math.log2(n))
    base = 2.0 ** (-8.0 / cp2)
    slopes = base ** np.arange(1, cp2 + 1, dtype=np.float32)
    if cp2 != n:
        extra_base = 2.0 ** (-4.0 / cp2)
        extra = extra_base ** np.arange(1, 1 + 2 * (n - cp2), 2, dtype=np.float32)
        slopes = np.concatenate([slopes, extra])
    return slopes.astype(np.float32)



def _tile128(a):
    Dk, Fc = a.shape[0] // 128, a.shape[1] // 128
    return np.ascontiguousarray(
        a.reshape(Dk, 128, Fc, 128).transpose(2, 1, 0, 3).reshape(Fc, 128, Dk * 128))


def _sbuf_img(a, ncol):
    Dk = a.shape[0] // 128
    assert a.shape[1] == ncol
    return np.ascontiguousarray(a.reshape(Dk, 128, ncol).transpose(1, 0, 2).reshape(128, Dk * ncol))


def host_prepare(inputs):
    f32 = np.float32

    def wq_split(w):
        return np.stack([_sbuf_img(np.ascontiguousarray(w[:, h * 128:(h + 1) * 128]), 128)
                         for h in range(NH)])

    shared = {}
    for L in range(2):
        if L == 0:
            wq = np.asarray(inputs["L0_attn_wq"], f32)
            wkv = np.concatenate([np.asarray(inputs["L0_attn_wk"], f32),
                                  np.asarray(inputs["L0_attn_wv"], f32)], axis=1)
            wo = np.asarray(inputs["L0_attn_wo"], f32)
            Wb = np.asarray(inputs["L0_ffn_W_base"], f32)
            bb = np.asarray(inputs["L0_ffn_b_base"], f32)
            Wr = np.asarray(inputs["L0_ffn_W_router"], f32)
            A = np.asarray(inputs["L0_ffn_A"], f32)
            Bm = np.asarray(inputs["L0_ffn_B"], f32)
        else:
            wqkv = np.asarray(inputs["L1_attn_w_qkv"], f32)
            wq, wkv = wqkv[:, :4096], np.ascontiguousarray(wqkv[:, 4096:6144])
            wo = np.asarray(inputs["L1_attn_w_o"], f32)
            Wb = np.asarray(inputs["L1_ffn_W_base"], f32)
            bb = np.asarray(inputs["L1_ffn_b_base"], f32)
            Wr = np.asarray(inputs["L1_ffn_W_router"], f32)
            A = np.asarray(inputs["L1_ffn_A"], f32)
            Bm = np.asarray(inputs["L1_ffn_B"], f32)
        shared[f"wq{L}"] = wq_split(wq)
        strips = [wkv[:, 0:512], wkv[:, 1024:1536], wkv[:, 512:1024], wkv[:, 1536:2048]]
        shared[f"wkv{L}"] = np.stack(
            [_sbuf_img(np.ascontiguousarray(st), 512) for st in strips])
        shared[f"wo{L}"] = _tile128(wo)
        shared[f"wb{L}"] = _tile128(np.ascontiguousarray(Wb.T))
        shared[f"aT{L}"] = _sbuf_img(np.ascontiguousarray(A.reshape(128, 4096).T), 128)
        shared[f"ball{L}"] = np.ascontiguousarray(Bm.transpose(0, 2, 1).reshape(128, 4096))
        shared[f"routT{L}"] = _sbuf_img(np.ascontiguousarray(Wr.T), 8)
        shared[f"bb{L}"] = np.ascontiguousarray(bb.reshape(32, 128).T)

    shared["idt"] = np.eye(128, dtype=f32)
    ind8 = np.zeros((8, 128), dtype=f32)
    for e in range(8):
        ind8[e, e * 16:(e + 1) * 16] = LORA_SCALING
    shared["ind8"] = ind8

    slopes = alibi_slopes(NH)
    x = np.asarray(inputs["L0_attn_x"], f32)

    in_maps, col_maps = [], []
    jj = np.arange(128)[:, None]
    ii = np.arange(128)[None, :]
    for core in range(8):
        b, r = core // 4, core % 4
        cols_seq = np.empty(TPC, dtype=np.int64)
        for s_ in range(8):
            m = ORDER[s_]
            cols_seq[s_ * 128:(s_ + 1) * 128] = 4 * (128 * m + np.arange(128)) + r
        col_maps.append((b, cols_seq))
        xs = np.ascontiguousarray(x[b].T[:, cols_seq])

        m0 = np.stack([((jj < ii) | ((jj == ii) & (rp <= r))).astype(f32) for rp in range(4)])
        mask0 = np.ascontiguousarray(m0.transpose(1, 0, 2).reshape(128, 512)).astype(ml_dtypes.bfloat16)

        dl = np.arange(8)[:, None, None, None]
        rp = np.arange(4)[None, :, None, None]
        jj4 = jj[None, None, :, :] * np.ones((1, 1, 1, 128))
        dmat = (512 * dl + 4 * (ii[None, None] - jj[None, None]) + (r - rp)).astype(np.int64)
        dmat_f = dmat.astype(f32)
        expb = np.empty((NH, 128, 4096), dtype=ml_dtypes.bfloat16)
        for h in range(NH):
            tab = np.where(dmat >= 0, np.exp(-slopes[h] * np.sqrt(np.maximum(dmat_f, 0.0))), 0.0)
            expb[h] = tab.transpose(2, 0, 1, 3).reshape(128, 4096).astype(ml_dtypes.bfloat16)

        m = {"x_in": xs, "mask0": mask0, "expb": expb}
        m.update(shared)
        in_maps.append(m)
    return in_maps, col_maps



class Plan:
    def __init__(self, cnt):
        self.cnt = cnt
        self.events = []

    def ev(self, eng, fn, waits=None, inc=None, n=1):
        self.events.append((eng, dict(waits or {}), fn, inc, n))
        if inc is not None:
            self.cnt[inc] += n

    def v(self, sem):
        return self.cnt[sem]


def emit_block(nc, plan, sems, name, trace=None):
    if trace is not None:
        trace.append(("BLOCK", name, list(plan.events)))
    with nc.Block(name, no_gpsimd_drain=True) as block:
        for eng_name in ("gpsimd", "sync", "scalar", "vector", "tensor"):
            evs = [e for e in plan.events if e[0] == eng_name]
            if not evs:
                continue

            def make(evs):
                def body(eng):
                    seen = {}
                    for _, waits, fn, inc, n in evs:
                        for semname, val in waits.items():
                            if val > 0 and seen.get(semname, -1) < val:
                                eng.wait_ge(sems[semname], val)
                                seen[semname] = val
                        instr = fn(eng) if fn is not None else None
                        if inc is not None:
                            if instr is None:
                                instr = eng.nop()
                            instr.then_inc(sems[inc], n)
                return body
            getattr(block, eng_name)(make(evs))



class Builder:
    _uid = 0

    @classmethod
    def uname(cls, name):
        cls._uid += 1
        return f"{name}_{cls._uid}"

    def __init__(self, debug=None, stop_after=None, att_parts="full", att_kh=NKV):
        self.dbg = debug or []
        self.stop_after = stop_after
        self.att_parts = att_parts
        self.att_kh = att_kh
        nc = self.nc = bass.Bass("TRN2", target_bir_lowering=False)
        self.C = defaultdict(int)

        self.x_in = nc.dram_tensor("x_in", [D, TPC], F32, kind="ExternalInput")
        self.out_fm = nc.dram_tensor("out_fm", [D, TPC], F32, kind="ExternalOutput")
        self.idt_d = nc.dram_tensor("idt", [128, 128], F32, kind="ExternalInput")
        self.ind8_d = nc.dram_tensor("ind8", [8, 128], F32, kind="ExternalInput")
        self.mask0_d = nc.dram_tensor("mask0", [128, 512], BF16, kind="ExternalInput")
        self.expb_d = nc.dram_tensor("expb", [NH, 128, 4096], BF16, kind="ExternalInput")
        self.W = {}
        for L in range(2):
            for nm, shape in [("wq", [NH, 128, 4096]), ("wkv", [4, 128, 16384]),
                              ("wo", [NC, 128, 4096]), ("wb", [NC, 128, 4096]),
                              ("aT", [128, 4096]), ("ball", [128, 4096]),
                              ("routT", [128, 256]), ("bb", [128, 32])]:
                self.W[f"{nm}{L}"] = nc.dram_tensor(f"{nm}{L}", shape, F32, kind="ExternalInput")
        self.x_cur = nc.dram_tensor("x_cur", [D, TPC], F32)
        self.g_buf = nc.dram_tensor("g_buf", [D, TPC], F32)
        self.kv_loc = [[nc.dram_tensor(f"kv_loc{L}_{s_}", [TPC, 512], BF16) for s_ in range(4)]
                       for L in range(2)]
        self.kv_full = [[nc.dram_tensor(f"kv_full{L}_{s_}", [4 * TPC, 512], BF16) for s_ in range(4)]
                        for L in range(2)]

        ctx = self.ctx = ExitStack()
        sbuf = lambda name, shape, dt: ctx.enter_context(nc.sbuf_tensor(name, shape, dt))
        semnames = (["W", "CC", "PE", "DVE", "ACT", "SM", "EXP", "MSK", "NPV",
                     "WQ", "EB", "KSQ", "RW", "GP"]
                    + [f"XB{i}" for i in range(4)]
                    + ["KVW0", "KVW1", "KST0", "KST1", "SLB0", "SLB1",
                       "OW0", "OW1", "OUT0", "OUT1", "GS0", "GS1", "GL0", "GL1"])
        self.sems = {s: ctx.enter_context(nc.semaphore(s)) for s in semnames}

        s = self
        s.sbuf = sbuf
        s.xn = sbuf("xn", [128, NC * TPC], BF16)
        s.onesb = sbuf("onesb", [128, 1], BF16)
        s.ones128 = sbuf("ones128", [128, 128], BF16)
        s.onesF = sbuf("onesF", [128, 1], F32)
        s.onesP = sbuf("onesP", [1, 128], F32)
        s.negP = sbuf("negP", [1, 128], F32)
        s.idt_s = sbuf("idt_s", [128, 128], F32)
        s.ind8_s = sbuf("ind8_s", [8, 128], F32)
        s.mask0_s = sbuf("mask0_s", [128, 512], BF16)
        s.bb_s = [sbuf(f"bb_s{L}", [128, 32], F32) for L in range(2)]
        s.hw_s = sbuf("hw_s", [128, TPC], BF16)
        s.constv = sbuf("constv", [128, 4], F32)
        self.const_vals = [EPS, math.log(SCALE * CH_SAFETY)]
        for i, v in enumerate(self.const_vals):
            nc.const_aps.register(F32, v, s.constv[:, i:i + 1])

        self.trace = []
        self.pending_negc = None
        self.pair_global = 0
        self.exp_of_pair = {}
        self.npv_pair_end = {}
        self.msk_pair_end = {}

    def phase_consts(self):
        C = self.C
        p = Plan(C)
        s = self
        p.ev("vector", lambda e: e.memset(s.onesb[:, :], 1.0))
        p.ev("vector", lambda e: e.memset(s.ones128[:, :], 1.0))
        p.ev("vector", lambda e: e.memset(s.onesF[:, :], 1.0))
        p.ev("vector", lambda e: e.memset(s.onesP[0:1, :], 1.0))
        p.ev("vector", lambda e: e.memset(s.negP[0:1, :], -1.0))
        for i, v in enumerate(self.const_vals):
            p.ev("vector", lambda e, i=i, v=v: e.memset(s.constv[:, i:i + 1], v))
        p.ev("vector", lambda e: e.drain(), inc="DVE")
        p.ev("gpsimd", lambda e: e.dma_start(out=s.idt_s[:, :], in_=s.idt_d[:, :]), inc="W", n=16)
        p.ev("gpsimd", lambda e: e.dma_start(out=s.ind8_s[:, :], in_=s.ind8_d[:, :]), inc="W", n=16)
        p.ev("gpsimd", lambda e: e.dma_start(out=s.mask0_s[:, :], in_=s.mask0_d[:, :]), inc="W", n=16)
        for L in range(2):
            p.ev("gpsimd", lambda e, L=L: e.dma_start(out=s.bb_s[L][:, :], in_=s.W[f"bb{L}"][:, :]), inc="W", n=16)
        p.ev("gpsimd", None, waits={"W": p.v("W")})
        emit_block(self.nc, p, self.sems, "consts", self.trace)

    def phase_ln_kv(self, L, xsrc, out_gate):
        nc, C, s = self.nc, self.C, self
        p = Plan(C)
        with ExitStack() as sc:
            sbuf = lambda name, shape, dt: sc.enter_context(nc.sbuf_tensor(Builder.uname(name), shape, dt))
            s.wkvs = [sbuf(f"wkvs{i}", [128, 16384], BF16) for i in range(2)]
            p.ev("gpsimd", lambda e, L=L: e.dma_start(out=s.wkvs[0][:, :], in_=s.W[f"wkv{L}"][0]), inc="KVW0", n=16)
            p.ev("gpsimd", lambda e, L=L: e.dma_start(out=s.wkvs[1][:, :], in_=s.W[f"wkv{L}"][1]), inc="KVW1", n=16)
            strip_ready = {0: ("KVW0", C["KVW0"]), 1: ("KVW1", C["KVW1"])}
            ln_ps = ExitStack()
            ps_s = ln_ps.enter_context(nc.psum_tensor(Builder.uname("ps_s"), [1, TPC], F32))
            ps_q = ln_ps.enter_context(nc.psum_tensor(Builder.uname("ps_q"), [1, TPC], F32))
            ps_A = ln_ps.enter_context(nc.psum_tensor(Builder.uname("ps_A"), [128, TPC], F32))
            ps_B = ln_ps.enter_context(nc.psum_tensor(Builder.uname("ps_B"), [128, TPC], F32))
            s.xbuf = [sbuf(f"xbuf{i}", [128, TPC], F32) for i in range(4)]
            s.xsqbuf = [sbuf(f"xsqbuf{i}", [128, TPC], F32) for i in range(2)]
            s.tmpf = [sbuf(f"tmpf{i}", [128, TPC], F32) for i in range(2)]
            st = [sbuf(f"st{i}", [1, TPC], F32) for i in range(4)]
            s.meanT, s.msqT, s.vqT, s.varT = st[0], st[1], st[2], st[3]
            s.lnvT, s.rstdT, s.mrstdT = st[1], st[2], st[3]
            s.AbcT = sbuf("AbcT", [128, TPC], F32)
            s.BbcT = sbuf("BbcT", [128, TPC], F32)
            base_PE, base_DVE, base_ACT = C["PE"], C["DVE"], C["ACT"]
            xb_ready = {}
            xn_dve = {}
            for c in range(NC):
                w = {}
                if c == 0 and out_gate:
                    w.update(out_gate)
                if c >= 4:
                    w.update({"ACT": base_ACT + (c - 3), "PE": base_PE + (c - 3)})
                p.ev("sync", lambda e, c=c: e.dma_start(
                    out=s.xbuf[c % 4][:, :], in_=xsrc[c * 128:(c + 1) * 128, :]),
                    waits=w, inc=f"XB{c % 4}", n=16)
                xb_ready[c] = C[f"XB{c % 4}"]
            for c in range(NC):
                w = {f"XB{c % 4}": xb_ready[c]}
                if c >= 2:
                    w["PE"] = base_PE + (c - 1)
                p.ev("scalar", lambda e, c=c: e.activation(s.xsqbuf[c % 2][:, :], s.xbuf[c % 4][:, :], AF.Square),
                     waits=w, inc="ACT")
            for c in range(NC):
                w = {"ACT": base_ACT + (c + 1)}
                if c >= 2:
                    w["PE"] = base_PE + (c - 1)
                for hf in range(2):
                    p.ev("tensor", lambda e, c=c, hf=hf: e.matmul(
                        ps_s[0:1, hf * 512:(hf + 1) * 512], s.onesF[:, :],
                        s.xbuf[c % 4][:, hf * 512:(hf + 1) * 512],
                        start=(c == 0), stop=(c == NC - 1)), waits=(w if hf == 0 else {}))
                    p.ev("tensor", lambda e, c=c, hf=hf: e.matmul(
                        ps_q[0:1, hf * 512:(hf + 1) * 512], s.onesF[:, :],
                        s.xsqbuf[c % 2][:, hf * 512:(hf + 1) * 512],
                        start=(c == 0), stop=(c == NC - 1)), inc=("PE" if hf == 1 else None))
            sq_act = base_ACT + NC
            p.ev("vector", lambda e: e.tensor_scalar(s.meanT[0:1, :], ps_s[0:1, :], 1.0 / D, None, OP.mult),
                 waits={"PE": p.v("PE")})
            p.ev("vector", lambda e: e.drain())
            p.ev("vector", lambda e: e.tensor_tensor(s.msqT[0:1, :], s.meanT[0:1, :], s.meanT[0:1, :], op=OP.mult))
            p.ev("vector", lambda e: e.drain())
            p.ev("vector", lambda e: e.tensor_scalar(s.vqT[0:1, :], ps_q[0:1, :], 1.0 / D, None, OP.mult))
            p.ev("vector", lambda e: e.drain())
            p.ev("vector", lambda e: e.tensor_tensor(s.varT[0:1, :], s.vqT[0:1, :], s.msqT[0:1, :], op=OP.subtract),
                 inc="DVE")
            p.ev("scalar", lambda e: e.activation(s.lnvT[0:1, :], s.varT[0:1, :], AF.Ln, bias=EPS),
                 waits={"DVE": p.v("DVE")})
            p.ev("scalar", lambda e: e.drain())
            p.ev("scalar", lambda e: e.activation(s.rstdT[0:1, :], s.lnvT[0:1, :], AF.Exp, scale=-0.5), inc="ACT")
            p.ev("vector", lambda e: e.tensor_tensor(s.mrstdT[0:1, :], s.meanT[0:1, :], s.rstdT[0:1, :], op=OP.mult),
                 waits={"ACT": p.v("ACT")})
            p.ev("vector", lambda e: e.drain(), inc="DVE")
            w = {"DVE": p.v("DVE")}
            for hf in range(2):
                p.ev("tensor", lambda e, hf=hf: e.matmul(
                    ps_A[:, hf * 512:(hf + 1) * 512], s.onesP[0:1, :], s.rstdT[0:1, hf * 512:(hf + 1) * 512],
                    start=True, stop=True), waits=(w if hf == 0 else {}))
                p.ev("tensor", lambda e, hf=hf: e.matmul(
                    ps_B[:, hf * 512:(hf + 1) * 512], s.negP[0:1, :], s.mrstdT[0:1, hf * 512:(hf + 1) * 512],
                    start=True, stop=True), inc=("PE" if hf == 1 else None))
            p.ev("vector", lambda e: e.tensor_copy(s.AbcT[:, :], ps_A[:, :]), waits={"PE": p.v("PE")})
            p.ev("vector", lambda e: e.tensor_copy(s.BbcT[:, :], ps_B[:, :]))
            p.ev("vector", lambda e: e.drain(), inc="DVE")
            dve_bc = p.v("DVE")
            for c in range(NC):
                xn_dve[c] = ("DVE", dve_bc + c + 1)
            for c in range(NC):
                w = {"DVE": dve_bc, "ACT": sq_act}
                if c >= 4:
                    sem4, val4 = xn_dve[c - 4]
                    w[sem4] = max(w.get(sem4, 0), val4)
                p.ev("sync", lambda e, c=c: e.dma_start(
                    out=s.xbuf[c % 4][:, :], in_=xsrc[c * 128:(c + 1) * 128, :]),
                    waits=w, inc=f"XB{c % 4}", n=16)
                xb_ready[c] = C[f"XB{c % 4}"]
            for c in range(NC):
                ww = {f"XB{c % 4}": xb_ready[c]}
                p.ev("vector", lambda e, c=c: e.tensor_tensor(
                    s.tmpf[c % 2][:, :], s.xbuf[c % 4][:, :], s.AbcT[:, :], op=OP.mult), waits=ww)
                p.ev("vector", lambda e: e.drain())
                p.ev("vector", lambda e, c=c: e.tensor_tensor(
                    s.xn[:, c * TPC:(c + 1) * TPC], s.tmpf[c % 2][:, :], s.BbcT[:, :], op=OP.add))
                p.ev("vector", lambda e: e.drain(), inc="DVE")
                assert xn_dve[c] == ("DVE", C["DVE"])
            ln_ps.close()
            pkv0 = sc.enter_context(nc.psum_tensor(Builder.uname("ps_kv0"), [128, 512], F32))
            pkv1 = sc.enter_context(nc.psum_tensor(Builder.uname("ps_kv1"), [128, 512], F32))
            s.kvst = [sbuf(f"kvst{i}", [128, 512], BF16) for i in range(2)]
            pkv = [pkv0, pkv1]
            base_PE, base_DVE = C["PE"], C["DVE"]
            kst_ready = {}
            for s_ in range(4):
                if 1 <= s_ < 3:
                    w = {"PE": base_PE + 8 * s_}
                    slot = f"KVW{(s_ + 1) % 2}"
                    p.ev("gpsimd", lambda e, s_=s_: e.dma_start(
                        out=s.wkvs[(s_ + 1) % 2][:, :], in_=s.W[f"wkv{L}"][s_ + 1]), waits=w, inc=slot, n=16)
                    strip_ready[s_ + 1] = (slot, C[slot])
                for tt in range(8):
                    it = s_ * 8 + tt
                    for c in range(NC):
                        semc, valc = xn_dve[c]
                        w = {semc: valc}
                        if c == 0:
                            sl, val = strip_ready[s_]
                            w[sl] = val
                            if it >= 2:
                                w["DVE"] = max(w.get("DVE", 0), base_DVE + (it - 1))
                        p.ev("tensor", lambda e, s_=s_, tt=tt, c=c: e.matmul(
                            pkv[tt % 2][:, :], s.xn[:, c * TPC + tt * 128: c * TPC + (tt + 1) * 128],
                            s.wkvs[s_ % 2][:, c * 512:(c + 1) * 512],
                            start=(c == 0), stop=(c == NC - 1)),
                            waits=w, inc=("PE" if c == NC - 1 else None))
                    w2 = {"PE": base_PE + it + 1}
                    if it >= 2:
                        w2[f"KST{it % 2}"] = kst_ready[it - 2]
                    p.ev("vector", lambda e, tt=tt: e.tensor_copy(s.kvst[tt % 2][:, :], pkv[tt % 2][:, :]),
                         waits=w2, inc="DVE")
                    p.ev("sync", lambda e, s_=s_, tt=tt, L=L: e.dma_start(
                        out=self.kv_loc[L][s_][tt * 128:(tt + 1) * 128, :],
                        in_=s.kvst[tt % 2][:, :]),
                        waits={"DVE": base_DVE + it + 1}, inc=f"KST{it % 2}", n=16)
                    kst_ready[it] = C[f"KST{it % 2}"]
                p.ev("gpsimd", lambda e, s_=s_, L=L: e.collective_compute(
                    "AllGather", OP.bypass, replica_groups=[[0, 1, 2, 3], [4, 5, 6, 7]],
                    ins=[self.kv_loc[L][s_].ap().opt()], outs=[self.kv_full[L][s_].ap().opt()]),
                    waits={"KST0": C["KST0"], "KST1": C["KST1"]}, inc="CC")
            emit_block(nc, p, self.sems, f"lnkv{L}", self.trace)

    def phase_att(self, L):
        nc, C, s = self.nc, self.C, self
        p = Plan(C)
        sm_head_end, msk_head_end, npv_head_end = {}, {}, {}
        npv_kh_end = {}

        with ExitStack() as sc:
            sbuf = lambda name, shape, dt: sc.enter_context(nc.sbuf_tensor(Builder.uname(name), shape, dt))
            pS0 = sc.enter_context(nc.psum_tensor(Builder.uname("ps_S0"), [128, 1024], F32))
            pS1 = sc.enter_context(nc.psum_tensor(Builder.uname("ps_S1"), [128, 1024], F32))
            pAcc = sc.enter_context(nc.psum_tensor(Builder.uname("ps_acc"), [128, 512], F32))
            pn = sc.enter_context(nc.psum_tensor(Builder.uname("ps_n"), [128, 512], F32))
            pq = sc.enter_context(nc.psum_tensor(Builder.uname("ps_pq"), [128, 512], F32))
            pnorm = sc.enter_context(nc.psum_tensor(Builder.uname("ps_norm"), [128, 512], F32))
            s.kfm = [sbuf(f"kfm{i}", [128, 4096], BF16) for i in range(2)]
            s.vsl = [sbuf(f"vsl{i}", [128, 4096], BF16) for i in range(2)]
            s.ksq = sbuf("ksq", [128, 4096], BF16)
            s.Ubuf = [sbuf(f"Ubuf{i}", [128, 1024], BF16) for i in range(2)]
            s.expb_s = [sbuf("expb_s0", [128, 4096], BF16)]
            s.wqb = [sbuf("wqb0", [128, 4096], BF16)]
            s.qh = [sbuf(f"qh{i}", [128, TPC], BF16) for i in range(2)]
            s.qsq = sbuf("qsq", [128, TPC], BF16)
            s.kaccT = sbuf("kaccT", [1, 16], F32)
            s.kmaxT = sbuf("kmaxT", [1, 2], F32)
            s.qaccT = sbuf("qaccT", [1, 16], F32)
            s.qmaxT = sbuf("qmaxT", [1, 1], F32)
            s.qkT = sbuf("qkT", [1, 1], F32)
            s.lnqkT = sbuf("lnqkT", [1, 1], F32)
            s.chT = sbuf("chT", [1, 1], F32)
            s.cbias = sbuf("cbias", [128, 2], F32)
            s.lnnT = sbuf("lnnT", [128, 512], F32)
            s.rbcT = sbuf("rbcT", [128, 512], F32)
            pS = [pS0, pS1]
            slab_ready = {}
            ksq_done = {}
            wq_ready = {}
            expb_ready = {}
            ch_gate = {}
            state = {"epi_dve": 0}

            cc_base = C["CC"] - 4
            def slab_load(kh):
                bufk, bufv = s.kfm[kh % 2], s.vsl[kh % 2]
                kstrip = 0 if kh < 4 else 2
                vstrip = 1 if kh < 4 else 3
                col = (kh % 4) * 128
                slot = f"SLB{kh % 2}"
                w = {"CC": cc_base + kstrip + 1}
                if kh >= 2:
                    w["NPV"] = npv_kh_end[kh - 2]
                    w["KSQ"] = ksq_done[kh - 2]
                p.ev("sync", lambda e, bufk=bufk, kstrip=kstrip, col=col, L=L: e.dma_start_transpose(
                    bufk[:, :], self.kv_full[L][kstrip][:, col:col + 128]), waits=w, inc=slot, n=16)
                p.ev("sync", None, waits={"CC": cc_base + vstrip + 1})
                for q4 in range(4):
                    p.ev("sync", lambda e, q4=q4, bufv=bufv, vstrip=vstrip, col=col, L=L: e.dma_start(
                        out=bufv[:, q4 * 1024:(q4 + 1) * 1024].rearrange("p (t d) -> p t d", t=8),
                        in_=self.kv_full[L][vstrip][q4 * 1024:(q4 + 1) * 1024, col:col + 128]
                        .rearrange("(t p) d -> p t d", p=128)), inc=slot, n=16)
                slab_ready[kh] = C[slot]

            def knorm(kh):
                p.ev("vector", lambda e, kh=kh: e.tensor_tensor(
                    s.ksq[:, :], s.kfm[kh % 2][:, :], s.kfm[kh % 2][:, :], op=OP.mult),
                    waits={f"SLB{kh % 2}": slab_ready[kh]}, inc="KSQ")
                ksq_done[kh] = C["KSQ"]
                base_dve_k = C["DVE"]
                for j8 in range(8):
                    tgt = pnorm if j8 % 2 == 0 else pq
                    w = {"KSQ": C["KSQ"]}
                    if j8 < 2:
                        w["DVE"] = base_dve_k
                        w["ACT"] = C["ACT"]
                    else:
                        w["DVE"] = base_dve_k + (j8 - 1)
                    p.ev("tensor", lambda e, j8=j8, tgt=tgt: e.matmul(
                        tgt[0:1, :], s.onesb[:, :], s.ksq[:, j8 * 512:(j8 + 1) * 512], start=True, stop=True),
                        waits=w, inc="PE")
                for j8 in range(8):
                    tgt = pnorm if j8 % 2 == 0 else pq
                    p.ev("vector", lambda e, j8=j8, tgt=tgt: e.reduce_max(
                        s.kaccT[0:1, j8:j8 + 1], tgt[0:1, :], axis=AX.X),
                        waits={"PE": C["PE"] - 7 + j8}, inc="DVE")
                p.ev("vector", lambda e: e.drain())
                p.ev("vector", lambda e, kh=kh: e.reduce_max(s.kmaxT[0:1, kh % 2:kh % 2 + 1], s.kaccT[0:1, 0:8], axis=AX.X))
                p.ev("vector", lambda e: e.drain(), inc="DVE")

            def prepare_head(h, qproj_only=False, skip_qproj=False):
                kh = h // 4
                if skip_qproj:
                    pass
                else:
                  for hf in range(2):
                    for c in range(NC):
                        ww = {}
                        if c == 0:
                            ww = {"WQ": wq_ready[h], "DVE": C["DVE"], "ACT": C["ACT"]}
                            if hf == 0 and h >= 2:
                                ww["NPV"] = npv_head_end[h - 2]
                        p.ev("tensor", lambda e, hf=hf, c=c, h=h: e.matmul(
                            pq[:, :], s.wqb[0][:, c * 128:(c + 1) * 128],
                            s.xn[:, c * TPC + hf * 512: c * TPC + hf * 512 + 512],
                            start=(c == 0), stop=(c == NC - 1)), waits=ww,
                            inc=("PE" if c == NC - 1 else None))
                    pe_now = C["PE"]
                    p.ev("vector", lambda e, hf=hf, h=h: e.tensor_copy(
                        s.qh[h % 2][:, hf * 512:(hf + 1) * 512], pq[:, :]), waits={"PE": pe_now}, inc="DVE")
                    p.ev("scalar", lambda e, hf=hf: e.activation(
                        s.qsq[:, hf * 512:(hf + 1) * 512], pq[:, :], AF.Square),
                        waits={"PE": pe_now, "DVE": C["DVE"]}, inc="ACT")
                if h + 1 < NH and not skip_qproj:
                    p.ev("gpsimd", lambda e, h=h, L=L: e.dma_start(
                        out=s.wqb[0][:, :], in_=s.W[f"wq{L}"][h + 1]),
                        waits={"PE": C["PE"]}, inc="WQ", n=16)
                    wq_ready[h + 1] = C["WQ"]
                if qproj_only:
                    return
                for hf in range(2):
                    p.ev("tensor", lambda e, hf=hf: e.matmul(
                        pnorm[hf:hf + 1, :] if False else pnorm[0:1, :] if hf == 0 else pnorm[32:33, :],
                        s.onesb[:, :], s.qsq[:, hf * 512:(hf + 1) * 512], start=True, stop=True),
                        waits=({"ACT": C["ACT"], "DVE": C["DVE"]} if hf == 0 else {"ACT": C["ACT"]}), inc="PE")
                pe_q = C["PE"]
                p.ev("vector", lambda e: e.reduce_max(s.qaccT[0:1, 0:1], pnorm[0:1, :], axis=AX.X),
                     waits={"PE": pe_q - 1})
                p.ev("vector", lambda e: e.reduce_max(s.qaccT[0:1, 1:2], pnorm[32:33, :], axis=AX.X),
                     waits={"PE": pe_q}, inc="DVE")
                p.ev("vector", lambda e: e.drain())
                p.ev("vector", lambda e: e.reduce_max(s.qmaxT[0:1, 0:1], s.qaccT[0:1, 0:2], axis=AX.X))
                p.ev("vector", lambda e: e.drain())
                p.ev("vector", lambda e, kh=kh: e.tensor_tensor(
                    s.qkT[0:1, 0:1], s.qmaxT[0:1, 0:1], s.kmaxT[0:1, kh % 2:kh % 2 + 1], op=OP.mult))
                p.ev("vector", lambda e: e.drain(), inc="DVE")
                p.ev("scalar", lambda e: e.activation(s.lnqkT[0:1, 0:1], s.qkT[0:1, 0:1], AF.Ln, bias=0.0),
                     waits={"DVE": C["DVE"]})
                p.ev("scalar", lambda e: e.drain())
                p.ev("scalar", lambda e: e.activation(
                    s.chT[0:1, 0:1], s.lnqkT[0:1, 0:1], AF.Exp, scale=0.5,
                    bias=math.log(SCALE * CH_SAFETY)), inc="ACT")
                act_ch = C["ACT"]
                dve_ch = C["DVE"]

                def emit_negc(h=h, act_ch=act_ch, dve_ch=dve_ch):
                    p.ev("tensor", lambda e: e.matmul(pnorm[:, 256:257], s.negP[0:1, :], s.chT[0:1, 0:1],
                                                      start=True, stop=True),
                         waits={"ACT": act_ch, "DVE": dve_ch}, inc="PE")
                    p.ev("vector", lambda e, h=h: e.tensor_copy(s.cbias[:, h % 2:h % 2 + 1], pnorm[:, 256:257]),
                         waits={"PE": C["PE"]})
                    p.ev("vector", lambda e: e.drain(), inc="DVE")
                    ch_gate[h] = C["DVE"]
                self.pending_negc = emit_negc

            def emit_attn_block(h, bi, tiles):
                kh = h // 4
                blk_col = bi * 512
                sched = att_sched(tiles)
                units = [(lg, Wd, bnd, act, rp) for (lg, Wd, bnd, act) in sched for rp in range(4)]
                first_unit = True
                pending = None
                gate = state["epi_dve"]

                def emit_npv(arg):
                    pid, items = arg
                    for (buf_, ucol_, Wd_, T_idx_, ww_, fu_, lu_) in items:
                        p.ev("tensor", lambda e, buf=buf_, ucol=ucol_, Wd=Wd_, fu=fu_, lu=lu_: e.matmul(
                            pn[:, 0:Wd], s.ones128[:, :], s.Ubuf[buf][:, ucol:ucol + Wd],
                            start=fu, stop=lu), waits=ww_)
                        p.ev("tensor", lambda e, buf=buf_, ucol=ucol_, Wd=Wd_, T_idx=T_idx_, kh=kh,
                             fu=fu_, lu=lu_: e.matmul(
                                 pAcc[:, 0:Wd], s.vsl[kh % 2][:, T_idx * 128:(T_idx + 1) * 128],
                                 s.Ubuf[buf][:, ucol:ucol + Wd],
                                 start=fu, stop=lu), inc="NPV")
                    self.npv_pair_end[pid] = C["NPV"]

                for pu in range(0, len(units), 2):
                    pair = self.pair_global
                    self.pair_global += 1
                    buf = pair % 2
                    for slot in range(2):
                        lg, Wd, bnd, act, rp = units[pu + slot]
                        T_idx = 8 * rp + SLOT_OF[lg]
                        ww = {}
                        if pu in (0, 2) and slot == 0:
                            ww["DVE"] = gate
                        if pu == 0 and slot == 0 and L == 1:
                            ww["EB"] = expb_ready[h]
                        if slot == 0 and pair >= 2:
                            ww["EXP"] = self.exp_of_pair[pair - 2]
                        p.ev("tensor", lambda e, buf=buf, slot=slot, Wd=Wd, T_idx=T_idx, h=h,
                             blk_col=blk_col, kh=kh: e.matmul(
                                 pS[buf][:, slot * 512: slot * 512 + Wd],
                                 s.kfm[kh % 2][:, T_idx * 128:(T_idx + 1) * 128],
                                 s.qh[h % 2][:, blk_col:blk_col + Wd],
                                 start=True, stop=True), waits=ww, inc="SM")
                    if pu == 0 and bi == 0 and self.pending_negc is not None:
                        self.pending_negc()
                        self.pending_negc = None
                    lg, Wd, bnd, act, _ = units[pu]
                    ww = {"SM": C["SM"]}
                    if pu == 0:
                        ww["DVE"] = max(gate, ch_gate[h])
                    if pair >= 2:
                        ww["NPV"] = self.npv_pair_end[pair - 2]
                        ww["MSK"] = self.msk_pair_end[pair - 2]
                    if Wd == 512:
                        p.ev("scalar", lambda e, buf=buf, h=h: e.activation(
                            s.Ubuf[buf][:, 0:1024], pS[buf][:, 0:1024], AF.Exp,
                            bias=s.cbias[:, h % 2:h % 2 + 1], scale=SCALE), waits=ww, inc="EXP")
                    else:
                        p.ev("scalar", lambda e, buf=buf, Wd=Wd, h=h: e.activation(
                            s.Ubuf[buf][:, :].rearrange("p (s w) -> p s w", s=2)[:, :, 0:Wd],
                            pS[buf][:, :].rearrange("p (s w) -> p s w", s=2)[:, :, 0:Wd],
                            AF.Exp, bias=s.cbias[:, h % 2:h % 2 + 1], scale=SCALE), waits=ww, inc="EXP")
                    self.exp_of_pair[pair] = C["EXP"]
                    items = []
                    for slot in range(2):
                        lg, Wd, bnd, act, rp = units[pu + slot]
                        last_unit = (pu + slot == len(units) - 1)
                        T_idx = 8 * rp + SLOT_OF[lg]
                        ucol = slot * 512
                        did_mask = False
                        if L == 0:
                            if bnd:
                                p.ev("vector", lambda e, buf=buf, ucol=ucol, Wd=Wd, rp=rp: e.tensor_tensor(
                                    s.Ubuf[buf][:, ucol + Wd - 128: ucol + Wd],
                                    s.Ubuf[buf][:, ucol + Wd - 128: ucol + Wd],
                                    s.mask0_s[:, rp * 128:(rp + 1) * 128], op=OP.mult),
                                    waits={"EXP": self.exp_of_pair[pair]}, inc="MSK")
                                did_mask = True
                        else:
                            for idx, m_ in enumerate(act):
                                dl = m_ - lg
                                p.ev("vector", lambda e, buf=buf, ucol=ucol, idx=idx, dl=dl, rp=rp,
                                     h=h: e.tensor_tensor(
                                         s.Ubuf[buf][:, ucol + idx * 128: ucol + (idx + 1) * 128],
                                         s.Ubuf[buf][:, ucol + idx * 128: ucol + (idx + 1) * 128],
                                         s.expb_s[0][:, (dl * 4 + rp) * 128:(dl * 4 + rp + 1) * 128],
                                         op=OP.mult),
                                     waits=({"EXP": self.exp_of_pair[pair]} if idx == 0 else {}),
                                     inc=("MSK" if idx == len(act) - 1 else None))
                            did_mask = True
                        ww = {"EXP": self.exp_of_pair[pair]}
                        if did_mask:
                            ww["MSK"] = C["MSK"]
                        if first_unit:
                            ww["DVE"] = max(gate, ch_gate[h])
                        items.append((buf, ucol, Wd, T_idx, ww, first_unit, last_unit))
                        self.msk_pair_end[pair] = C["MSK"]
                        first_unit = False
                    if pending is not None:
                        emit_npv(pending)
                    pending = (pair, items)
                emit_npv(pending)
                p.ev("scalar", lambda e: e.activation(s.lnnT[:, :], pn[:, 0:512], AF.Ln, bias=0.0),
                     waits={"NPV": C["NPV"]})
                p.ev("scalar", lambda e: e.drain())
                p.ev("scalar", lambda e: e.activation(s.rbcT[:, :], s.lnnT[:, :], AF.Exp, scale=-1.0),
                     inc="ACT")
                p.ev("vector", lambda e, h=h, blk_col=blk_col: e.tensor_tensor(
                    s.ao[:, h * TPC + blk_col: h * TPC + blk_col + 512],
                    pAcc[:, 0:512], s.rbcT[:, :], op=OP.mult),
                    waits={"ACT": C["ACT"]})
                p.ev("vector", lambda e: e.drain(), inc="DVE")
                state["epi_dve"] = C["DVE"]

            slab_load(0)
            if L == 1:
                p.ev("scalar", lambda e: e.dma_start(out=s.expb_s[0][:, :], in_=s.expb_d[0]), inc="EB", n=16)
                expb_ready[0] = C["EB"]
            p.ev("gpsimd", lambda e, L=L: e.dma_start(out=s.wqb[0][:, :], in_=s.W[f"wq{L}"][0]), inc="WQ", n=16)
            wq_ready[0] = C["WQ"]
            for kh in range(NKV):
                if kh == 0:
                    prepare_head(0, qproj_only=True)
                knorm(kh)
                if kh + 1 < NKV:
                    slab_load(kh + 1)
                for g_ in range(4):
                    h = 4 * kh + g_
                    prepare_head(h, skip_qproj=(h == 0))
                    emit_attn_block(h, 0, BLOCKS[0])
                    emit_attn_block(h, 1, BLOCKS[1])
                    sm_head_end[h] = C["SM"]
                    msk_head_end[h] = C["MSK"]
                    npv_head_end[h] = C["NPV"]
                    if L == 1 and h + 1 < NH:
                        p.ev("scalar", lambda e, h=h: e.dma_start(
                            out=s.expb_s[0][:, :], in_=s.expb_d[h + 1]),
                            waits={"MSK": C["MSK"]}, inc="EB", n=16)
                        expb_ready[h + 1] = C["EB"]
                npv_kh_end[kh] = C["NPV"]
            emit_block(nc, p, self.sems, f"att{L}", self.trace)
        return

    def phase_ffng(self, L):
        nc, C, s = self.nc, self.C, self
        p = Plan(C)
        with ExitStack() as sc:
            sbuf = lambda name, shape, dt: sc.enter_context(nc.sbuf_tensor(Builder.uname(name), shape, dt))
            rt_ps = ExitStack()
            plg = rt_ps.enter_context(nc.psum_tensor(Builder.uname("ps_lg"), [8, TPC], F32))
            pt8 = rt_ps.enter_context(nc.psum_tensor(Builder.uname("ps_t8"), [128, 64], F32))
            pcb = rt_ps.enter_context(nc.psum_tensor(Builder.uname("ps_cb"), [128, TPC], F32))
            ph = rt_ps.enter_context(nc.psum_tensor(Builder.uname("ps_h"), [128, TPC], F32))
            s.aT_s = sbuf("aT_s", [128, 4096], BF16)
            s.routT_s = sbuf("routT_s", [128, 256], BF16)
            s.logit_s = sbuf("logit_s", [8, TPC], F32)
            s.v8_s = sbuf("v8_s", [128, 64], F32)
            s.m8_s = sbuf("m8_s", [128, 8], F32)
            s.nm12_s = sbuf("nm12_s", [128, 8], F32)
            s.ge_s = sbuf("ge_s", [128, 8], F32)
            s.sg_s = sbuf("sg_s", [128, 8], F32)
            s.comb_s = sbuf("comb_s", [128, 64], F32)
            s.combT_s = sbuf("combT_s", [8, TPC], F32)
            s.h_s = sbuf("h_s", [128, TPC], F32)
            s.ball_s = sbuf("ball_s", [128, 4096], BF16)
            s.wbb = [sbuf(f"wbb{i}", [128, 4096], BF16) for i in range(2)]
            s.g_s = [sbuf(f"g_s{i}", [128, TPC], F32) for i in range(2)]
            p.ev("gpsimd", lambda e, L=L: e.dma_start(out=s.routT_s[:, :], in_=s.W[f"routT{L}"][:, :]), inc="RW", n=16)
            w_rt = C["RW"]
            p.ev("gpsimd", lambda e, L=L: e.dma_start(out=s.aT_s[:, :], in_=s.W[f"aT{L}"][:, :]), inc="GP", n=16)
            w_at = C["GP"]
            p.ev("gpsimd", lambda e, L=L: e.dma_start(out=s.ball_s[:, :], in_=s.W[f"ball{L}"][:, :]), inc="W", n=16)
            w_ready = C["W"]
            p.ev("gpsimd", lambda e, L=L: e.dma_start(out=s.wbb[0][:, :], in_=s.W[f"wb{L}"][0]), inc="OW0", n=16)
            ow_ready = {0: C["OW0"]}
            for c in range(NC):
                for hf in range(2):
                    p.ev("tensor", lambda e, c=c, hf=hf: e.matmul(
                        plg[0:8, hf * 512:(hf + 1) * 512], s.routT_s[:, c * 8:(c + 1) * 8],
                        s.xn[:, c * TPC + hf * 512: c * TPC + hf * 512 + 512],
                        start=(c == 0), stop=(c == NC - 1)),
                        waits=({"RW": w_rt} if (c == 0 and hf == 0) else {}),
                        inc=("PE" if (c == NC - 1 and hf == 1) else None))
            pe_lg = C["PE"]
            for c in range(NC):
                for hf in range(2):
                    p.ev("tensor", lambda e, c=c, hf=hf: e.matmul(
                        ph[:, hf * 512:(hf + 1) * 512], s.aT_s[:, c * 128:(c + 1) * 128],
                        s.xn[:, c * TPC + hf * 512: c * TPC + hf * 512 + 512],
                        start=(c == 0), stop=(c == NC - 1)),
                        waits=({"GP": w_at} if (c == 0 and hf == 0) else {}),
                        inc=("PE" if (c == NC - 1 and hf == 1) else None))
            pe_h = C["PE"]
            p.ev("vector", lambda e: e.tensor_copy(s.logit_s[0:8, :], plg[0:8, :]),
                 waits={"PE": pe_lg}, inc="DVE")
            for tt in range(8):
                p.ev("tensor", lambda e, tt=tt: e.transpose(
                    pt8[:, tt * 8:(tt + 1) * 8], s.logit_s[0:8, tt * 128:(tt + 1) * 128], s.idt_s[0:8, 0:8]),
                    waits=({"DVE": C["DVE"]} if tt == 0 else {}), inc=("PE" if tt == 7 else None))
            p.ev("vector", lambda e: e.tensor_copy(s.v8_s[:, :], pt8[:, :]), waits={"PE": C["PE"]})
            p.ev("vector", lambda e: e.drain())
            for tt in range(8):
                p.ev("vector", lambda e, tt=tt: e.max(s.m8_s[:, :], s.v8_s[:, tt * 8:(tt + 1) * 8]))
                p.ev("vector", lambda e: e.drain())
                p.ev("vector", lambda e, tt=tt: e.tensor_scalar(
                    s.nm12_s[:, tt:tt + 1], s.m8_s[:, 0:1], s.m8_s[:, 1:2], -1.0, OP.add, OP.mult))
                p.ev("vector", lambda e: e.drain())
                p.ev("vector", lambda e, tt=tt: e.tensor_scalar(
                    s.ge_s[:, :], s.v8_s[:, tt * 8:(tt + 1) * 8], s.m8_s[:, 1:2], None, OP.is_ge))
                p.ev("vector", lambda e: e.drain(), inc="DVE")
                p.ev("scalar", lambda e, tt=tt: e.activation(
                    s.sg_s[:, :], s.v8_s[:, tt * 8:(tt + 1) * 8], AF.Sigmoid,
                    bias=s.nm12_s[:, tt:tt + 1], scale=2.0),
                    waits={"DVE": C["DVE"]}, inc="ACT")
                p.ev("vector", lambda e, tt=tt: e.tensor_tensor(
                    s.comb_s[:, tt * 8:(tt + 1) * 8], s.sg_s[:, :], s.ge_s[:, :], op=OP.mult),
                    waits={"ACT": C["ACT"]})
                p.ev("vector", lambda e: e.drain(), inc="DVE")
                p.ev("tensor", lambda e, tt=tt: e.transpose(
                    plg[0:8, tt * 128:(tt + 1) * 128], s.comb_s[:, tt * 8:(tt + 1) * 8], s.idt_s[:, :]),
                    waits={"DVE": C["DVE"]}, inc="PE")
                p.ev("vector", lambda e, tt=tt: e.tensor_copy(
                    s.combT_s[0:8, tt * 128:(tt + 1) * 128], plg[0:8, tt * 128:(tt + 1) * 128]),
                    waits={"PE": C["PE"]}, inc="DVE")
            p.ev("vector", lambda e: e.drain(), inc="DVE")
            for hf in range(2):
                p.ev("tensor", lambda e, hf=hf: e.matmul(
                    pcb[:, hf * 512:(hf + 1) * 512], s.ind8_s[0:8, :], s.combT_s[0:8, hf * 512:(hf + 1) * 512],
                    start=True, stop=True),
                    waits=({"DVE": C["DVE"]} if hf == 0 else {}), inc=("PE" if hf == 1 else None))
            p.ev("vector", lambda e: e.tensor_copy(s.h_s[:, :], ph[:, :]), waits={"PE": pe_h})
            p.ev("vector", lambda e: e.drain())
            p.ev("vector", lambda e: e.tensor_tensor(s.hw_s[:, :], s.h_s[:, :], pcb[:, :], op=OP.mult),
                 waits={"PE": C["PE"]})
            p.ev("vector", lambda e: e.drain(), inc="DVE")
            hw_dve = C["DVE"]
            rt_ps.close()
            psB0 = sc.enter_context(nc.psum_tensor(Builder.uname("ps_B0"), [128, TPC], F32))
            psB1 = sc.enter_context(nc.psum_tensor(Builder.uname("ps_B1"), [128, TPC], F32))
            psB = [psB0, psB1]
            gs_ready = {}
            act_chunk_end = {}
            pe_chunk_end = {}
            for c in range(NC):
                if c + 1 < NC:
                    w = {}
                    if c >= 1:
                        w = {"PE": pe_chunk_end[c - 1]}
                    slot = f"OW{(c + 1) % 2}"
                    p.ev("gpsimd", lambda e, c=c, L=L: e.dma_start(
                        out=s.wbb[(c + 1) % 2][:, :], in_=s.W[f"wb{L}"][c + 1]), waits=w, inc=slot, n=16)
                    ow_ready[c + 1] = C[slot]
                ww = {f"OW{c % 2}": ow_ready[c], "DVE": hw_dve}
                if c >= 2:
                    ww["ACT"] = act_chunk_end[c - 2]
                for d_ in range(NC):
                    for hf in range(2):
                        p.ev("tensor", lambda e, c=c, d_=d_, hf=hf: e.matmul(
                            psB[c % 2][:, hf * 512:(hf + 1) * 512], s.wbb[c % 2][:, d_ * 128:(d_ + 1) * 128],
                            s.xn[:, d_ * TPC + hf * 512: d_ * TPC + hf * 512 + 512],
                            start=(d_ == 0), stop=False),
                            waits=(ww if (d_ == 0 and hf == 0) else {}))
                for hf in range(2):
                    p.ev("tensor", lambda e, c=c, hf=hf, L=L: e.matmul(
                        psB[c % 2][:, hf * 512:(hf + 1) * 512], s.ball_s[:, c * 128:(c + 1) * 128],
                        s.hw_s[:, hf * 512:(hf + 1) * 512], start=False, stop=True),
                        waits=({"W": w_ready} if (c == 0 and hf == 0) else {}),
                        inc=("PE" if hf == 1 else None))
                pe_chunk_end[c] = C["PE"]
                w = {"PE": C["PE"]}
                if c >= 2:
                    w[f"GS{c % 2}"] = gs_ready[c - 2]
                p.ev("scalar", lambda e, c=c, L=L: e.activation(
                    s.g_s[c % 2][:, :], psB[c % 2][:, :], AF.Silu, bias=s.bb_s[L][:, c:c + 1]),
                    waits=w, inc="ACT")
                act_chunk_end[c] = C["ACT"]
                p.ev("sync", lambda e, c=c: e.dma_start(
                    out=self.g_buf[c * 128:(c + 1) * 128, :], in_=s.g_s[c % 2][:, :]),
                    waits={"ACT": C["ACT"]}, inc=f"GS{c % 2}", n=16)
                gs_ready[c] = C[f"GS{c % 2}"]
            self.gs_done = (C["GS0"], C["GS1"])
            p.ev("sync", None, waits={"GS0": C["GS0"], "GS1": C["GS1"]})
            emit_block(nc, p, self.sems, f"ffng{L}", self.trace)

    def phase_out(self, L, xsrc, xdst, out_gate):
        nc, C, s = self.nc, self.C, self
        p = Plan(C)
        with ExitStack() as sc:
            sbuf = lambda name, shape, dt: sc.enter_context(nc.sbuf_tensor(Builder.uname(name), shape, dt))
            psA0 = sc.enter_context(nc.psum_tensor(Builder.uname("ps_A0"), [128, TPC], F32))
            psA1 = sc.enter_context(nc.psum_tensor(Builder.uname("ps_A1"), [128, TPC], F32))
            s.xbuf = [sbuf(f"xbuf{i}", [128, TPC], F32) for i in range(2)]
            s.gld = [sbuf(f"gld{i}", [128, TPC], F32) for i in range(2)]
            s.wob = [sbuf(f"wob{i}", [128, 4096], BF16) for i in range(2)]
            psA = [psA0, psA1]
            p.ev("gpsimd", lambda e, L=L: e.dma_start(out=s.wob[0][:, :], in_=s.W[f"wo{L}"][0]), inc="OW0", n=16)
            ow_ready = {0: C["OW0"]}
            xb_ready, gl_ready, out_ready = {}, {}, {}
            pe_chunk_end, dve_chunk_end = {}, {}
            gs0, gs1 = self.gs_done
            for c in range(NC):
                if c + 1 < NC:
                    w = {}
                    if c >= 1:
                        w = {"PE": pe_chunk_end[c - 1]}
                    slot = f"OW{(c + 1) % 2}"
                    p.ev("gpsimd", lambda e, c=c, L=L: e.dma_start(
                        out=s.wob[(c + 1) % 2][:, :], in_=s.W[f"wo{L}"][c + 1]), waits=w, inc=slot, n=16)
                    ow_ready[c + 1] = C[slot]
                w = {}
                if c == 0 and out_gate:
                    w.update(out_gate)
                if c >= 2:
                    w["DVE"] = dve_chunk_end[c - 2]
                    w[f"OUT{c % 2}"] = out_ready[c - 2]
                p.ev("sync", lambda e, c=c: e.dma_start(
                    out=s.xbuf[c % 2][:, :], in_=xsrc[c * 128:(c + 1) * 128, :]),
                    waits=w, inc=f"XB{c % 2}", n=16)
                xb_ready[c] = C[f"XB{c % 2}"]
                w = {f"GS{c % 2}": gs0 if c % 2 == 0 else gs1}
                if c >= 2:
                    w["DVE"] = dve_chunk_end[c - 2]
                p.ev("sync", lambda e, c=c: e.dma_start(
                    out=s.gld[c % 2][:, :], in_=self.g_buf[c * 128:(c + 1) * 128, :]),
                    waits=w, inc=f"GL{c % 2}", n=16)
                gl_ready[c] = C[f"GL{c % 2}"]
                ww = {f"OW{c % 2}": ow_ready[c]}
                if c >= 2:
                    ww["DVE"] = dve_chunk_end[c - 2]
                for d_ in range(NC):
                    for hf in range(2):
                        p.ev("tensor", lambda e, c=c, d_=d_, hf=hf: e.matmul(
                            psA[c % 2][:, hf * 512:(hf + 1) * 512], s.wob[c % 2][:, d_ * 128:(d_ + 1) * 128],
                            s.ao[:, d_ * TPC + hf * 512: d_ * TPC + hf * 512 + 512],
                            start=(d_ == 0), stop=(d_ == NC - 1)),
                            waits=(ww if (d_ == 0 and hf == 0) else {}),
                            inc=("PE" if (d_ == NC - 1 and hf == 1) else None))
                pe_chunk_end[c] = C["PE"]
                w = {"PE": C["PE"], f"GL{c % 2}": gl_ready[c], f"XB{c % 2}": xb_ready[c]}
                p.ev("vector", lambda e, c=c: e.tensor_tensor(
                    s.gld[c % 2][:, :], psA[c % 2][:, :], s.gld[c % 2][:, :], op=OP.mult), waits=w)
                p.ev("vector", lambda e: e.drain())
                p.ev("vector", lambda e, c=c: e.tensor_tensor(
                    s.xbuf[c % 2][:, :], s.gld[c % 2][:, :], s.xbuf[c % 2][:, :], op=OP.add))
                p.ev("vector", lambda e: e.drain(), inc="DVE")
                dve_chunk_end[c] = C["DVE"]
                p.ev("sync", lambda e, c=c: e.dma_start(
                    out=xdst[c * 128:(c + 1) * 128, :], in_=s.xbuf[c % 2][:, :]),
                    waits={"DVE": C["DVE"]}, inc=f"OUT{c % 2}", n=16)
                out_ready[c] = C[f"OUT{c % 2}"]
            p.ev("sync", None, waits={"OUT0": C["OUT0"], "OUT1": C["OUT1"]})
            emit_block(nc, p, self.sems, f"out{L}", self.trace)

    def phase_final(self):
        nc, C = self.nc, self.C
        p = Plan(C)
        dram_srcs = {"x_cur": self.x_cur}
        sbuf_srcs = {"xn": getattr(self, "xn", None), "ao": getattr(self, "ao", None),
                     "hw": getattr(self, "hw_s", None)}
        for name in self.dbg:
            if name in dram_srcs:
                src = dram_srcs[name]
                dd = nc.dram_tensor(f"dbg_{name}", list(src.shape), src.dtype, kind="ExternalOutput")
                p.ev("sync", lambda e, dd=dd, src=src: e.dma_start(out=dd[:, :], in_=src[:, :]),
                     waits={"OUT0": C["OUT0"], "OUT1": C["OUT1"], "CC": C["CC"]}, inc="OUT0", n=16)
            else:
                src = sbuf_srcs[name]
                shp = [int(x) for x in src.shape]
                dd = nc.dram_tensor(f"dbg_{name}", shp, src.dtype, kind="ExternalOutput")
                p.ev("sync", lambda e, dd=dd, src=src: e.dma_start(out=dd[:, :], in_=src[:, :]),
                     waits={"OUT0": C["OUT0"], "OUT1": C["OUT1"]}, inc="OUT0", n=16)
        p.ev("sync", None, waits={"OUT0": C["OUT0"], "OUT1": C["OUT1"]})
        emit_block(nc, p, self.sems, "fin", self.trace)

    def build(self):
        self.phase_consts()
        out_gate = None
        stop = self.stop_after

        def done(tag):
            return stop == tag

        for L in range(2):
            xsrc = self.x_in if L == 0 else self.x_cur
            xdst = self.x_cur if L == 0 else self.out_fm
            self.phase_ln_kv(L, xsrc, out_gate)
            if done(f"ln{L}") or done(f"kv{L}"):
                break
            self.phase_ffng(L)
            if done(f"rout{L}"):
                break
            with ExitStack() as lscope:
                self.ao = lscope.enter_context(
                    self.nc.sbuf_tensor(Builder.uname("ao"), [128, NC * TPC], BF16))
                self.phase_att(L)
                if done(f"att{L}"):
                    break
                self.phase_out(L, xsrc, xdst, out_gate)
            if done(f"out{L}"):
                break
            out_gate = {"OUT0": self.C["OUT0"], "OUT1": self.C["OUT1"]}
        self.phase_final()
        self.ctx.close()
        return self.nc


def build_program(debug=None, stop_after=None, att_parts="full", att_kh=NKV):
    return Builder(debug, stop_after, att_parts, att_kh).build()


def kernel(**inputs):
    in_maps, col_maps = host_prepare(inputs)
    nc = build_program()
    res = run_bass_kernel_spmd(nc, in_maps, core_ids=list(range(8)))
    out = np.empty((B, S, D), dtype=np.float32)
    for core in range(8):
        b, cols_seq = col_maps[core]
        out[b, cols_seq, :] = res.results[core]["out_fm"].T
    return out


# version 14
# speedup vs baseline: 1.0161x; 1.0161x over previous
import math
from collections import defaultdict
from contextlib import ExitStack

import numpy as np
import ml_dtypes

import concourse.bass as bass
import concourse.mybir as mybir
from concourse.alu_op_type import AluOpType
from concourse.bass_utils import run_bass_kernel_spmd

F32 = mybir.dt.float32
BF16 = mybir.dt.bfloat16
AF = mybir.ActivationFunctionType
AX = mybir.AxisListType
OP = AluOpType

B, S, D = 2, 4096, 4096
NH, NKV, HD = 32, 8, 128
TPC = 1024
NC = 32
SCALE = HD ** -0.5
EPS = 1e-5
LORA_SCALING = 2.0
CH_SAFETY = 1.01

ORDER = [6, 4, 2, 0, 7, 5, 3, 1]
SLOT_OF = {m: s for s, m in enumerate(ORDER)}
BLOCKS = ([6, 4, 2, 0], [7, 5, 3, 1])


def att_sched(tiles):
    out = []
    for lg in range(tiles[0] + 1):
        act = [m for m in tiles if m >= lg]
        out.append((lg, 128 * len(act), lg in tiles, act))
    return out


def alibi_slopes(n):
    cp2 = 2 ** math.floor(math.log2(n))
    base = 2.0 ** (-8.0 / cp2)
    slopes = base ** np.arange(1, cp2 + 1, dtype=np.float32)
    if cp2 != n:
        extra_base = 2.0 ** (-4.0 / cp2)
        extra = extra_base ** np.arange(1, 1 + 2 * (n - cp2), 2, dtype=np.float32)
        slopes = np.concatenate([slopes, extra])
    return slopes.astype(np.float32)



def _tile128(a):
    Dk, Fc = a.shape[0] // 128, a.shape[1] // 128
    return np.ascontiguousarray(
        a.reshape(Dk, 128, Fc, 128).transpose(2, 1, 0, 3).reshape(Fc, 128, Dk * 128))


def _sbuf_img(a, ncol):
    Dk = a.shape[0] // 128
    assert a.shape[1] == ncol
    return np.ascontiguousarray(a.reshape(Dk, 128, ncol).transpose(1, 0, 2).reshape(128, Dk * ncol))


def host_prepare(inputs):
    f32 = np.float32

    def wq_split(w):
        return np.stack([_sbuf_img(np.ascontiguousarray(w[:, h * 128:(h + 1) * 128]), 128)
                         for h in range(NH)])

    shared = {}
    for L in range(2):
        if L == 0:
            wq = np.asarray(inputs["L0_attn_wq"], f32)
            wkv = np.concatenate([np.asarray(inputs["L0_attn_wk"], f32),
                                  np.asarray(inputs["L0_attn_wv"], f32)], axis=1)
            wo = np.asarray(inputs["L0_attn_wo"], f32)
            Wb = np.asarray(inputs["L0_ffn_W_base"], f32)
            bb = np.asarray(inputs["L0_ffn_b_base"], f32)
            Wr = np.asarray(inputs["L0_ffn_W_router"], f32)
            A = np.asarray(inputs["L0_ffn_A"], f32)
            Bm = np.asarray(inputs["L0_ffn_B"], f32)
        else:
            wqkv = np.asarray(inputs["L1_attn_w_qkv"], f32)
            wq, wkv = wqkv[:, :4096], np.ascontiguousarray(wqkv[:, 4096:6144])
            wo = np.asarray(inputs["L1_attn_w_o"], f32)
            Wb = np.asarray(inputs["L1_ffn_W_base"], f32)
            bb = np.asarray(inputs["L1_ffn_b_base"], f32)
            Wr = np.asarray(inputs["L1_ffn_W_router"], f32)
            A = np.asarray(inputs["L1_ffn_A"], f32)
            Bm = np.asarray(inputs["L1_ffn_B"], f32)
        shared[f"wq{L}"] = wq_split(wq)
        strips = [wkv[:, 0:512], wkv[:, 1024:1536], wkv[:, 512:1024], wkv[:, 1536:2048]]
        shared[f"wkv{L}"] = np.stack(
            [_sbuf_img(np.ascontiguousarray(st), 512) for st in strips])
        shared[f"wo{L}"] = _tile128(wo)
        shared[f"wb{L}"] = _tile128(np.ascontiguousarray(Wb.T))
        shared[f"aT{L}"] = _sbuf_img(np.ascontiguousarray(A.reshape(128, 4096).T), 128)
        shared[f"ball{L}"] = np.ascontiguousarray(Bm.transpose(0, 2, 1).reshape(128, 4096))
        shared[f"routT{L}"] = _sbuf_img(np.ascontiguousarray(Wr.T), 8)
        shared[f"bb{L}"] = np.ascontiguousarray(bb.reshape(32, 128).T)

    shared["idt"] = np.eye(128, dtype=f32)
    ind8 = np.zeros((8, 128), dtype=f32)
    for e in range(8):
        ind8[e, e * 16:(e + 1) * 16] = LORA_SCALING
    shared["ind8"] = ind8

    slopes = alibi_slopes(NH)
    x = np.asarray(inputs["L0_attn_x"], f32)

    in_maps, col_maps = [], []
    jj = np.arange(128)[:, None]
    ii = np.arange(128)[None, :]
    for core in range(8):
        b, r = core // 4, core % 4
        cols_seq = np.empty(TPC, dtype=np.int64)
        for s_ in range(8):
            m = ORDER[s_]
            cols_seq[s_ * 128:(s_ + 1) * 128] = 4 * (128 * m + np.arange(128)) + r
        col_maps.append((b, cols_seq))
        xs = np.ascontiguousarray(x[b].T[:, cols_seq])

        m0 = np.stack([((jj < ii) | ((jj == ii) & (rp <= r))).astype(f32) for rp in range(4)])
        mask0 = np.ascontiguousarray(m0.transpose(1, 0, 2).reshape(128, 512)).astype(ml_dtypes.bfloat16)

        dl = np.arange(8)[:, None, None, None]
        rp = np.arange(4)[None, :, None, None]
        jj4 = jj[None, None, :, :] * np.ones((1, 1, 1, 128))
        dmat = (512 * dl + 4 * (ii[None, None] - jj[None, None]) + (r - rp)).astype(np.int64)
        dmat_f = dmat.astype(f32)
        expb = np.empty((NH, 128, 4096), dtype=ml_dtypes.bfloat16)
        for h in range(NH):
            tab = np.where(dmat >= 0, np.exp(-slopes[h] * np.sqrt(np.maximum(dmat_f, 0.0))), 0.0)
            expb[h] = tab.transpose(2, 0, 1, 3).reshape(128, 4096).astype(ml_dtypes.bfloat16)

        m = {"x_in": xs, "mask0": mask0, "expb": expb}
        m.update(shared)
        in_maps.append(m)
    return in_maps, col_maps



class Plan:
    def __init__(self, cnt):
        self.cnt = cnt
        self.events = []

    def ev(self, eng, fn, waits=None, inc=None, n=1):
        self.events.append((eng, dict(waits or {}), fn, inc, n))
        if inc is not None:
            self.cnt[inc] += n

    def v(self, sem):
        return self.cnt[sem]


def emit_block(nc, plan, sems, name, trace=None):
    if trace is not None:
        trace.append(("BLOCK", name, list(plan.events)))
    with nc.Block(name, no_gpsimd_drain=True) as block:
        for eng_name in ("gpsimd", "sync", "scalar", "vector", "tensor"):
            evs = [e for e in plan.events if e[0] == eng_name]
            if not evs:
                continue

            def make(evs):
                def body(eng):
                    seen = {}
                    for _, waits, fn, inc, n in evs:
                        for semname, val in waits.items():
                            if val > 0 and seen.get(semname, -1) < val:
                                eng.wait_ge(sems[semname], val)
                                seen[semname] = val
                        instr = fn(eng) if fn is not None else None
                        if inc is not None:
                            if instr is None:
                                instr = eng.nop()
                            instr.then_inc(sems[inc], n)
                return body
            getattr(block, eng_name)(make(evs))



class Builder:
    _uid = 0

    @classmethod
    def uname(cls, name):
        cls._uid += 1
        return f"{name}_{cls._uid}"

    def __init__(self, debug=None, stop_after=None, att_parts="full", att_kh=NKV):
        self.dbg = debug or []
        self.stop_after = stop_after
        self.att_parts = att_parts
        self.att_kh = att_kh
        nc = self.nc = bass.Bass("TRN2", target_bir_lowering=False)
        self.C = defaultdict(int)

        self.x_in = nc.dram_tensor("x_in", [D, TPC], F32, kind="ExternalInput")
        self.out_fm = nc.dram_tensor("out_fm", [D, TPC], F32, kind="ExternalOutput")
        self.idt_d = nc.dram_tensor("idt", [128, 128], F32, kind="ExternalInput")
        self.ind8_d = nc.dram_tensor("ind8", [8, 128], F32, kind="ExternalInput")
        self.mask0_d = nc.dram_tensor("mask0", [128, 512], BF16, kind="ExternalInput")
        self.expb_d = nc.dram_tensor("expb", [NH, 128, 4096], BF16, kind="ExternalInput")
        self.W = {}
        for L in range(2):
            for nm, shape in [("wq", [NH, 128, 4096]), ("wkv", [4, 128, 16384]),
                              ("wo", [NC, 128, 4096]), ("wb", [NC, 128, 4096]),
                              ("aT", [128, 4096]), ("ball", [128, 4096]),
                              ("routT", [128, 256]), ("bb", [128, 32])]:
                self.W[f"{nm}{L}"] = nc.dram_tensor(f"{nm}{L}", shape, F32, kind="ExternalInput")
        self.x_cur = nc.dram_tensor("x_cur", [D, TPC], F32)
        self.g_buf = nc.dram_tensor("g_buf", [D, TPC], F32)
        self.kv_loc = [[nc.dram_tensor(f"kv_loc{L}_{s_}", [TPC, 512], BF16) for s_ in range(4)]
                       for L in range(2)]
        self.kv_full = [[nc.dram_tensor(f"kv_full{L}_{s_}", [4 * TPC, 512], BF16) for s_ in range(4)]
                        for L in range(2)]

        ctx = self.ctx = ExitStack()
        sbuf = lambda name, shape, dt: ctx.enter_context(nc.sbuf_tensor(name, shape, dt))
        semnames = (["W", "CC", "PE", "DVE", "ACT", "SM", "EXP", "MSK", "NPV",
                     "WQ", "EB", "KSQ", "RW", "GP"]
                    + [f"XB{i}" for i in range(4)]
                    + ["KVW0", "KVW1", "KST0", "KST1", "SLB0", "SLB1",
                       "OW0", "OW1", "OUT0", "OUT1", "GS0", "GS1", "GL0", "GL1"])
        self.sems = {s: ctx.enter_context(nc.semaphore(s)) for s in semnames}

        s = self
        s.sbuf = sbuf
        s.xn = sbuf("xn", [128, NC * TPC], BF16)
        s.onesb = sbuf("onesb", [128, 1], BF16)
        s.ones128 = sbuf("ones128", [128, 128], BF16)
        s.onesF = sbuf("onesF", [128, 1], F32)
        s.onesP = sbuf("onesP", [1, 128], F32)
        s.negP = sbuf("negP", [1, 128], F32)
        s.idt_s = sbuf("idt_s", [128, 128], F32)
        s.ind8_s = sbuf("ind8_s", [8, 128], F32)
        s.mask0_s = sbuf("mask0_s", [128, 512], BF16)
        s.bb_s = [sbuf(f"bb_s{L}", [128, 32], F32) for L in range(2)]
        s.hw_s = sbuf("hw_s", [128, TPC], BF16)
        s.constv = sbuf("constv", [128, 4], F32)
        self.const_vals = [EPS, math.log(SCALE * CH_SAFETY)]
        for i, v in enumerate(self.const_vals):
            nc.const_aps.register(F32, v, s.constv[:, i:i + 1])

        self.trace = []
        self.pending_negc = None
        self.pair_global = 0
        self.exp_of_pair = {}
        self.npv_pair_end = {}
        self.msk_pair_end = {}

    def phase_consts(self):
        C = self.C
        p = Plan(C)
        s = self
        p.ev("vector", lambda e: e.memset(s.onesb[:, :], 1.0))
        p.ev("vector", lambda e: e.memset(s.ones128[:, :], 1.0))
        p.ev("vector", lambda e: e.memset(s.onesF[:, :], 1.0))
        p.ev("vector", lambda e: e.memset(s.onesP[0:1, :], 1.0))
        p.ev("vector", lambda e: e.memset(s.negP[0:1, :], -1.0))
        for i, v in enumerate(self.const_vals):
            p.ev("vector", lambda e, i=i, v=v: e.memset(s.constv[:, i:i + 1], v))
        p.ev("vector", lambda e: e.drain(), inc="DVE")
        p.ev("gpsimd", lambda e: e.dma_start(out=s.idt_s[:, :], in_=s.idt_d[:, :]), inc="W", n=16)
        p.ev("gpsimd", lambda e: e.dma_start(out=s.ind8_s[:, :], in_=s.ind8_d[:, :]), inc="W", n=16)
        p.ev("gpsimd", lambda e: e.dma_start(out=s.mask0_s[:, :], in_=s.mask0_d[:, :]), inc="W", n=16)
        for L in range(2):
            p.ev("gpsimd", lambda e, L=L: e.dma_start(out=s.bb_s[L][:, :], in_=s.W[f"bb{L}"][:, :]), inc="W", n=16)
        p.ev("gpsimd", None, waits={"W": p.v("W")})
        emit_block(self.nc, p, self.sems, "consts", self.trace)

    def phase_ln_kv(self, L, xsrc, out_gate):
        nc, C, s = self.nc, self.C, self
        p = Plan(C)
        with ExitStack() as sc:
            sbuf = lambda name, shape, dt: sc.enter_context(nc.sbuf_tensor(Builder.uname(name), shape, dt))
            s.wkvs = [sbuf(f"wkvs{i}", [128, 16384], BF16) for i in range(2)]
            p.ev("gpsimd", lambda e, L=L: e.dma_start(out=s.wkvs[0][:, :], in_=s.W[f"wkv{L}"][0]), inc="KVW0", n=16)
            p.ev("gpsimd", lambda e, L=L: e.dma_start(out=s.wkvs[1][:, :], in_=s.W[f"wkv{L}"][1]), inc="KVW1", n=16)
            strip_ready = {0: ("KVW0", C["KVW0"]), 1: ("KVW1", C["KVW1"])}
            ln_ps = ExitStack()
            ps_s = ln_ps.enter_context(nc.psum_tensor(Builder.uname("ps_s"), [1, TPC], F32))
            ps_q = ln_ps.enter_context(nc.psum_tensor(Builder.uname("ps_q"), [1, TPC], F32))
            ps_A = ln_ps.enter_context(nc.psum_tensor(Builder.uname("ps_A"), [128, TPC], F32))
            ps_B = ln_ps.enter_context(nc.psum_tensor(Builder.uname("ps_B"), [128, TPC], F32))
            s.xbuf = [sbuf(f"xbuf{i}", [128, TPC], F32) for i in range(4)]
            s.xsqbuf = [sbuf(f"xsqbuf{i}", [128, TPC], F32) for i in range(2)]
            s.tmpf = [sbuf(f"tmpf{i}", [128, TPC], F32) for i in range(2)]
            st = [sbuf(f"st{i}", [1, TPC], F32) for i in range(4)]
            s.meanT, s.msqT, s.vqT, s.varT = st[0], st[1], st[2], st[3]
            s.lnvT, s.rstdT, s.mrstdT = st[1], st[2], st[3]
            s.AbcT = sbuf("AbcT", [128, TPC], F32)
            s.BbcT = sbuf("BbcT", [128, TPC], F32)
            base_PE, base_DVE, base_ACT = C["PE"], C["DVE"], C["ACT"]
            xb_ready = {}
            xn_dve = {}
            for c in range(NC):
                w = {}
                if c == 0 and out_gate:
                    w.update(out_gate)
                if c >= 4:
                    w.update({"ACT": base_ACT + (c - 3), "PE": base_PE + (c - 3)})
                p.ev("sync", lambda e, c=c: e.dma_start(
                    out=s.xbuf[c % 4][:, :], in_=xsrc[c * 128:(c + 1) * 128, :]),
                    waits=w, inc=f"XB{c % 4}", n=16)
                xb_ready[c] = C[f"XB{c % 4}"]
            for c in range(NC):
                w = {f"XB{c % 4}": xb_ready[c]}
                if c >= 2:
                    w["PE"] = base_PE + (c - 1)
                p.ev("scalar", lambda e, c=c: e.activation(s.xsqbuf[c % 2][:, :], s.xbuf[c % 4][:, :], AF.Square),
                     waits=w, inc="ACT")
            for c in range(NC):
                w = {"ACT": base_ACT + (c + 1)}
                if c >= 2:
                    w["PE"] = base_PE + (c - 1)
                for hf in range(2):
                    p.ev("tensor", lambda e, c=c, hf=hf: e.matmul(
                        ps_s[0:1, hf * 512:(hf + 1) * 512], s.onesF[:, :],
                        s.xbuf[c % 4][:, hf * 512:(hf + 1) * 512],
                        start=(c == 0), stop=(c == NC - 1)), waits=(w if hf == 0 else {}))
                    p.ev("tensor", lambda e, c=c, hf=hf: e.matmul(
                        ps_q[0:1, hf * 512:(hf + 1) * 512], s.onesF[:, :],
                        s.xsqbuf[c % 2][:, hf * 512:(hf + 1) * 512],
                        start=(c == 0), stop=(c == NC - 1)), inc=("PE" if hf == 1 else None))
            sq_act = base_ACT + NC
            p.ev("vector", lambda e: e.tensor_scalar(s.meanT[0:1, :], ps_s[0:1, :], 1.0 / D, None, OP.mult),
                 waits={"PE": p.v("PE")})
            p.ev("vector", lambda e: e.drain())
            p.ev("vector", lambda e: e.tensor_tensor(s.msqT[0:1, :], s.meanT[0:1, :], s.meanT[0:1, :], op=OP.mult))
            p.ev("vector", lambda e: e.drain())
            p.ev("vector", lambda e: e.tensor_scalar(s.vqT[0:1, :], ps_q[0:1, :], 1.0 / D, None, OP.mult))
            p.ev("vector", lambda e: e.drain())
            p.ev("vector", lambda e: e.tensor_tensor(s.varT[0:1, :], s.vqT[0:1, :], s.msqT[0:1, :], op=OP.subtract),
                 inc="DVE")
            p.ev("scalar", lambda e: e.activation(s.lnvT[0:1, :], s.varT[0:1, :], AF.Ln, bias=EPS),
                 waits={"DVE": p.v("DVE")})
            p.ev("scalar", lambda e: e.drain())
            p.ev("scalar", lambda e: e.activation(s.rstdT[0:1, :], s.lnvT[0:1, :], AF.Exp, scale=-0.5), inc="ACT")
            p.ev("vector", lambda e: e.tensor_tensor(s.mrstdT[0:1, :], s.meanT[0:1, :], s.rstdT[0:1, :], op=OP.mult),
                 waits={"ACT": p.v("ACT")})
            p.ev("vector", lambda e: e.drain(), inc="DVE")
            w = {"DVE": p.v("DVE")}
            for hf in range(2):
                p.ev("tensor", lambda e, hf=hf: e.matmul(
                    ps_A[:, hf * 512:(hf + 1) * 512], s.onesP[0:1, :], s.rstdT[0:1, hf * 512:(hf + 1) * 512],
                    start=True, stop=True), waits=(w if hf == 0 else {}))
                p.ev("tensor", lambda e, hf=hf: e.matmul(
                    ps_B[:, hf * 512:(hf + 1) * 512], s.negP[0:1, :], s.mrstdT[0:1, hf * 512:(hf + 1) * 512],
                    start=True, stop=True), inc=("PE" if hf == 1 else None))
            p.ev("vector", lambda e: e.tensor_copy(s.AbcT[:, :], ps_A[:, :]), waits={"PE": p.v("PE")})
            p.ev("vector", lambda e: e.tensor_copy(s.BbcT[:, :], ps_B[:, :]))
            p.ev("vector", lambda e: e.drain(), inc="DVE")
            dve_bc = p.v("DVE")
            for c in range(NC):
                xn_dve[c] = ("DVE", dve_bc + c + 1)
            for c in range(NC):
                w = {"DVE": dve_bc, "ACT": sq_act}
                if c >= 4:
                    sem4, val4 = xn_dve[c - 4]
                    w[sem4] = max(w.get(sem4, 0), val4)
                p.ev("sync", lambda e, c=c: e.dma_start(
                    out=s.xbuf[c % 4][:, :], in_=xsrc[c * 128:(c + 1) * 128, :]),
                    waits=w, inc=f"XB{c % 4}", n=16)
                xb_ready[c] = C[f"XB{c % 4}"]
            for c in range(NC):
                ww = {f"XB{c % 4}": xb_ready[c]}
                p.ev("vector", lambda e, c=c: e.tensor_tensor(
                    s.tmpf[c % 2][:, :], s.xbuf[c % 4][:, :], s.AbcT[:, :], op=OP.mult), waits=ww)
                p.ev("vector", lambda e: e.drain())
                p.ev("vector", lambda e, c=c: e.tensor_tensor(
                    s.xn[:, c * TPC:(c + 1) * TPC], s.tmpf[c % 2][:, :], s.BbcT[:, :], op=OP.add))
                p.ev("vector", lambda e: e.drain(), inc="DVE")
                assert xn_dve[c] == ("DVE", C["DVE"])
            ln_ps.close()
            pkv0 = sc.enter_context(nc.psum_tensor(Builder.uname("ps_kv0"), [128, 512], F32))
            pkv1 = sc.enter_context(nc.psum_tensor(Builder.uname("ps_kv1"), [128, 512], F32))
            s.kvst = [sbuf(f"kvst{i}", [128, 512], BF16) for i in range(2)]
            pkv = [pkv0, pkv1]
            base_PE, base_DVE = C["PE"], C["DVE"]
            kst_ready = {}
            for s_ in range(4):
                if 1 <= s_ < 3:
                    w = {"PE": base_PE + 8 * s_}
                    slot = f"KVW{(s_ + 1) % 2}"
                    p.ev("gpsimd", lambda e, s_=s_: e.dma_start(
                        out=s.wkvs[(s_ + 1) % 2][:, :], in_=s.W[f"wkv{L}"][s_ + 1]), waits=w, inc=slot, n=16)
                    strip_ready[s_ + 1] = (slot, C[slot])
                for tt in range(8):
                    it = s_ * 8 + tt
                    for c in range(NC):
                        semc, valc = xn_dve[c]
                        w = {semc: valc}
                        if c == 0:
                            sl, val = strip_ready[s_]
                            w[sl] = val
                            if it >= 2:
                                w["DVE"] = max(w.get("DVE", 0), base_DVE + (it - 1))
                        p.ev("tensor", lambda e, s_=s_, tt=tt, c=c: e.matmul(
                            pkv[tt % 2][:, :], s.xn[:, c * TPC + tt * 128: c * TPC + (tt + 1) * 128],
                            s.wkvs[s_ % 2][:, c * 512:(c + 1) * 512],
                            start=(c == 0), stop=(c == NC - 1)),
                            waits=w, inc=("PE" if c == NC - 1 else None))
                    w2 = {"PE": base_PE + it + 1}
                    if it >= 2:
                        w2[f"KST{it % 2}"] = kst_ready[it - 2]
                    p.ev("vector", lambda e, tt=tt: e.tensor_copy(s.kvst[tt % 2][:, :], pkv[tt % 2][:, :]),
                         waits=w2, inc="DVE")
                    p.ev("sync", lambda e, s_=s_, tt=tt, L=L: e.dma_start(
                        out=self.kv_loc[L][s_][tt * 128:(tt + 1) * 128, :],
                        in_=s.kvst[tt % 2][:, :]),
                        waits={"DVE": base_DVE + it + 1}, inc=f"KST{it % 2}", n=16)
                    kst_ready[it] = C[f"KST{it % 2}"]
                p.ev("gpsimd", lambda e, s_=s_, L=L: e.collective_compute(
                    "AllGather", OP.bypass, replica_groups=[[0, 1, 2, 3], [4, 5, 6, 7]],
                    ins=[self.kv_loc[L][s_].ap().opt()], outs=[self.kv_full[L][s_].ap().opt()]),
                    waits={"KST0": C["KST0"], "KST1": C["KST1"]}, inc="CC")
            emit_block(nc, p, self.sems, f"lnkv{L}", self.trace)

    def phase_att(self, L):
        nc, C, s = self.nc, self.C, self
        p = Plan(C)
        sm_head_end, msk_head_end, npv_head_end = {}, {}, {}
        npv_kh_end = {}

        with ExitStack() as sc:
            sbuf = lambda name, shape, dt: sc.enter_context(nc.sbuf_tensor(Builder.uname(name), shape, dt))
            pS0 = sc.enter_context(nc.psum_tensor(Builder.uname("ps_S0"), [128, 1024], F32))
            pS1 = sc.enter_context(nc.psum_tensor(Builder.uname("ps_S1"), [128, 1024], F32))
            pAcc = sc.enter_context(nc.psum_tensor(Builder.uname("ps_acc"), [128, 512], F32))
            pn = sc.enter_context(nc.psum_tensor(Builder.uname("ps_n"), [128, 512], F32))
            pq = sc.enter_context(nc.psum_tensor(Builder.uname("ps_pq"), [128, 512], F32))
            pnorm = sc.enter_context(nc.psum_tensor(Builder.uname("ps_norm"), [128, 512], F32))
            s.kfm = [sbuf(f"kfm{i}", [128, 4096], BF16) for i in range(2)]
            s.vsl = [sbuf(f"vsl{i}", [128, 4096], BF16) for i in range(2)]
            s.ksq = sbuf("ksq", [128, 4096], BF16)
            s.Ubuf = [sbuf(f"Ubuf{i}", [128, 1024], BF16) for i in range(2)]
            s.expb_s = [sbuf("expb_s0", [128, 4096], BF16)]
            s.wqb = [sbuf("wqb0", [128, 4096], BF16)]
            s.qh = [sbuf(f"qh{i}", [128, TPC], BF16) for i in range(2)]
            s.qsq = sbuf("qsq", [128, TPC], BF16)
            s.kaccT = sbuf("kaccT", [128, 16], F32)
            s.kmaxT = sbuf("kmaxT", [128, 2], F32)
            s.qaccT = sbuf("qaccT", [128, 2], F32)
            s.qkT = sbuf("qkT", [128, 2], F32)
            s.lnqkT = sbuf("lnqkT", [128, 1], F32)
            s.chT = sbuf("chT", [128, 1], F32)
            s.cbias = sbuf("cbias", [128, 2], F32)
            s.lnnT = sbuf("lnnT", [128, 512], F32)
            s.rbcT = sbuf("rbcT", [128, 512], F32)
            pS = [pS0, pS1]
            slab_ready = {}
            ksq_done = {}
            wq_ready = {}
            expb_ready = {}
            qh_gate = {}
            state = {"epi_dve": 0}

            cc_base = C["CC"] - 4
            def slab_load(kh):
                bufk, bufv = s.kfm[kh % 2], s.vsl[kh % 2]
                kstrip = 0 if kh < 4 else 2
                vstrip = 1 if kh < 4 else 3
                col = (kh % 4) * 128
                slot = f"SLB{kh % 2}"
                w = {"CC": cc_base + kstrip + 1}
                if kh >= 2:
                    w["NPV"] = npv_kh_end[kh - 2]
                    w["KSQ"] = ksq_done[kh - 2]
                p.ev("sync", lambda e, bufk=bufk, kstrip=kstrip, col=col, L=L: e.dma_start_transpose(
                    bufk[:, :], self.kv_full[L][kstrip][:, col:col + 128]), waits=w, inc=slot, n=16)
                p.ev("sync", None, waits={"CC": cc_base + vstrip + 1})
                for q4 in range(4):
                    p.ev("sync", lambda e, q4=q4, bufv=bufv, vstrip=vstrip, col=col, L=L: e.dma_start(
                        out=bufv[:, q4 * 1024:(q4 + 1) * 1024].rearrange("p (t d) -> p t d", t=8),
                        in_=self.kv_full[L][vstrip][q4 * 1024:(q4 + 1) * 1024, col:col + 128]
                        .rearrange("(t p) d -> p t d", p=128)), inc=slot, n=16)
                slab_ready[kh] = C[slot]

            def knorm(kh):
                p.ev("vector", lambda e, kh=kh: e.tensor_tensor(
                    s.ksq[:, :], s.kfm[kh % 2][:, :], s.kfm[kh % 2][:, :], op=OP.mult),
                    waits={f"SLB{kh % 2}": slab_ready[kh]}, inc="KSQ")
                ksq_done[kh] = C["KSQ"]
                base_dve_k = C["DVE"]
                for j8 in range(8):
                    tgt = pnorm if j8 % 2 == 0 else pq
                    w = {"KSQ": C["KSQ"]}
                    if j8 < 2:
                        w["DVE"] = base_dve_k
                        w["ACT"] = C["ACT"]
                    else:
                        w["DVE"] = base_dve_k + (j8 - 1)
                    p.ev("tensor", lambda e, j8=j8, tgt=tgt: e.matmul(
                        tgt[:, :], s.ones128[:, :], s.ksq[:, j8 * 512:(j8 + 1) * 512], start=True, stop=True),
                        waits=w, inc="PE")
                for j8 in range(8):
                    tgt = pnorm if j8 % 2 == 0 else pq
                    p.ev("vector", lambda e, j8=j8, tgt=tgt: e.reduce_max(
                        s.kaccT[:, j8:j8 + 1], tgt[:, :], axis=AX.X),
                        waits={"PE": C["PE"] - 7 + j8}, inc="DVE")
                p.ev("vector", lambda e: e.drain())
                p.ev("vector", lambda e, kh=kh: e.reduce_max(s.kmaxT[:, kh % 2:kh % 2 + 1], s.kaccT[:, 0:8], axis=AX.X))
                p.ev("vector", lambda e: e.drain(), inc="DVE")

            def prepare_head(h, qproj_only=False, skip_qproj=False):
                kh = h // 4
                if skip_qproj:
                    pass
                else:
                  for hf in range(2):
                    for c in range(NC):
                        ww = {}
                        if c == 0:
                            ww = {"WQ": wq_ready[h], "DVE": C["DVE"], "ACT": C["ACT"]}
                            if hf == 0 and h >= 2:
                                ww["NPV"] = npv_head_end[h - 2]
                        p.ev("tensor", lambda e, hf=hf, c=c, h=h: e.matmul(
                            pq[:, :], s.wqb[0][:, c * 128:(c + 1) * 128],
                            s.xn[:, c * TPC + hf * 512: c * TPC + hf * 512 + 512],
                            start=(c == 0), stop=(c == NC - 1)), waits=ww,
                            inc=("PE" if c == NC - 1 else None))
                    pe_now = C["PE"]
                    p.ev("vector", lambda e, hf=hf, h=h: e.tensor_copy(
                        s.qh[h % 2][:, hf * 512:(hf + 1) * 512], pq[:, :]), waits={"PE": pe_now}, inc="DVE")
                    p.ev("scalar", lambda e, hf=hf: e.activation(
                        s.qsq[:, hf * 512:(hf + 1) * 512], pq[:, :], AF.Square),
                        waits={"PE": pe_now, "DVE": C["DVE"]}, inc="ACT")
                if h + 1 < NH and not skip_qproj:
                    p.ev("gpsimd", lambda e, h=h, L=L: e.dma_start(
                        out=s.wqb[0][:, :], in_=s.W[f"wq{L}"][h + 1]),
                        waits={"PE": C["PE"]}, inc="WQ", n=16)
                    wq_ready[h + 1] = C["WQ"]
                qh_gate[h] = C["DVE"]
                if qproj_only:
                    return
                for hf in range(2):
                    p.ev("tensor", lambda e, hf=hf: e.matmul(
                        pnorm[:, :] if hf == 0 else pq[:, :],
                        s.ones128[:, :], s.qsq[:, hf * 512:(hf + 1) * 512], start=True, stop=True),
                        waits=({"ACT": C["ACT"], "DVE": C["DVE"]} if hf == 0 else {"ACT": C["ACT"]}), inc="PE")
                pe_q = C["PE"]
                p.ev("vector", lambda e: e.reduce_max(s.qaccT[:, 0:1], pnorm[:, :], axis=AX.X),
                     waits={"PE": pe_q - 1})
                p.ev("vector", lambda e: e.drain())
                p.ev("vector", lambda e: e.reduce_max(s.qaccT[:, 1:2], pq[:, :], axis=AX.X),
                     waits={"PE": pe_q}, inc="DVE")
                p.ev("vector", lambda e: e.drain())
                p.ev("vector", lambda e, kh=kh: e.tensor_tensor(
                    s.qkT[:, 0:1], s.qaccT[:, 0:1], s.kmaxT[:, kh % 2:kh % 2 + 1], op=OP.mult))
                p.ev("vector", lambda e: e.drain())
                p.ev("vector", lambda e, kh=kh: e.tensor_tensor(
                    s.qkT[:, 1:2], s.qaccT[:, 1:2], s.kmaxT[:, kh % 2:kh % 2 + 1], op=OP.mult))
                p.ev("vector", lambda e: e.drain())
                p.ev("vector", lambda e: e.tensor_tensor(
                    s.lnqkT[:, 0:1], s.qkT[:, 0:1], s.qkT[:, 1:2], op=OP.max))
                p.ev("vector", lambda e: e.drain(), inc="DVE")
                p.ev("scalar", lambda e: e.activation(s.chT[:, 0:1], s.lnqkT[:, 0:1], AF.Ln, bias=0.0),
                     waits={"DVE": C["DVE"]})
                p.ev("scalar", lambda e: e.drain())
                p.ev("scalar", lambda e: e.activation(
                    s.chT[:, 0:1], s.chT[:, 0:1], AF.Exp, scale=0.5,
                    bias=math.log(SCALE * CH_SAFETY)))
                p.ev("scalar", lambda e: e.drain())
                p.ev("scalar", lambda e, h=h: e.mul(s.cbias[:, h % 2:h % 2 + 1], s.chT[:, 0:1], -1.0), inc="ACT")
                p.ev("scalar", lambda e: e.drain())

            def emit_attn_block(h, bi, tiles):
                kh = h // 4
                blk_col = bi * 512
                sched = att_sched(tiles)
                units = [(lg, Wd, bnd, act, rp) for (lg, Wd, bnd, act) in sched for rp in range(4)]
                first_unit = True
                pending = None
                gate = state["epi_dve"]

                def emit_npv(arg):
                    pid, items = arg
                    for (buf_, ucol_, Wd_, T_idx_, ww_, fu_, lu_) in items:
                        p.ev("tensor", lambda e, buf=buf_, ucol=ucol_, Wd=Wd_, fu=fu_, lu=lu_: e.matmul(
                            pn[:, 0:Wd], s.ones128[:, :], s.Ubuf[buf][:, ucol:ucol + Wd],
                            start=fu, stop=lu), waits=ww_)
                        p.ev("tensor", lambda e, buf=buf_, ucol=ucol_, Wd=Wd_, T_idx=T_idx_, kh=kh,
                             fu=fu_, lu=lu_: e.matmul(
                                 pAcc[:, 0:Wd], s.vsl[kh % 2][:, T_idx * 128:(T_idx + 1) * 128],
                                 s.Ubuf[buf][:, ucol:ucol + Wd],
                                 start=fu, stop=lu), inc="NPV")
                    self.npv_pair_end[pid] = C["NPV"]

                for pu in range(0, len(units), 2):
                    pair = self.pair_global
                    self.pair_global += 1
                    buf = pair % 2
                    for slot in range(2):
                        lg, Wd, bnd, act, rp = units[pu + slot]
                        T_idx = 8 * rp + SLOT_OF[lg]
                        ww = {}
                        if pu in (0, 2) and slot == 0:
                            ww["DVE"] = qh_gate[h]
                        if pu == 0 and slot == 0 and L == 1:
                            ww["EB"] = expb_ready[h]
                        if slot == 0 and pair >= 2:
                            ww["EXP"] = self.exp_of_pair[pair - 2]
                        p.ev("tensor", lambda e, buf=buf, slot=slot, Wd=Wd, T_idx=T_idx, h=h,
                             blk_col=blk_col, kh=kh: e.matmul(
                                 pS[buf][:, slot * 512: slot * 512 + Wd],
                                 s.kfm[kh % 2][:, T_idx * 128:(T_idx + 1) * 128],
                                 s.qh[h % 2][:, blk_col:blk_col + Wd],
                                 start=True, stop=True), waits=ww, inc="SM")
                    lg, Wd, bnd, act, _ = units[pu]
                    ww = {"SM": C["SM"]}
                    if pair >= 2:
                        ww["NPV"] = self.npv_pair_end[pair - 2]
                        ww["MSK"] = self.msk_pair_end[pair - 2]
                    if Wd == 512:
                        p.ev("scalar", lambda e, buf=buf, h=h: e.activation(
                            s.Ubuf[buf][:, 0:1024], pS[buf][:, 0:1024], AF.Exp,
                            bias=s.cbias[:, h % 2:h % 2 + 1], scale=SCALE), waits=ww, inc="EXP")
                    else:
                        p.ev("scalar", lambda e, buf=buf, Wd=Wd, h=h: e.activation(
                            s.Ubuf[buf][:, :].rearrange("p (s w) -> p s w", s=2)[:, :, 0:Wd],
                            pS[buf][:, :].rearrange("p (s w) -> p s w", s=2)[:, :, 0:Wd],
                            AF.Exp, bias=s.cbias[:, h % 2:h % 2 + 1], scale=SCALE), waits=ww, inc="EXP")
                    self.exp_of_pair[pair] = C["EXP"]
                    items = []
                    for slot in range(2):
                        lg, Wd, bnd, act, rp = units[pu + slot]
                        last_unit = (pu + slot == len(units) - 1)
                        T_idx = 8 * rp + SLOT_OF[lg]
                        ucol = slot * 512
                        did_mask = False
                        if L == 0:
                            if bnd:
                                p.ev("vector", lambda e, buf=buf, ucol=ucol, Wd=Wd, rp=rp: e.tensor_tensor(
                                    s.Ubuf[buf][:, ucol + Wd - 128: ucol + Wd],
                                    s.Ubuf[buf][:, ucol + Wd - 128: ucol + Wd],
                                    s.mask0_s[:, rp * 128:(rp + 1) * 128], op=OP.mult),
                                    waits={"EXP": self.exp_of_pair[pair]}, inc="MSK")
                                did_mask = True
                        else:
                            for idx, m_ in enumerate(act):
                                dl = m_ - lg
                                p.ev("vector", lambda e, buf=buf, ucol=ucol, idx=idx, dl=dl, rp=rp,
                                     h=h: e.tensor_tensor(
                                         s.Ubuf[buf][:, ucol + idx * 128: ucol + (idx + 1) * 128],
                                         s.Ubuf[buf][:, ucol + idx * 128: ucol + (idx + 1) * 128],
                                         s.expb_s[0][:, (dl * 4 + rp) * 128:(dl * 4 + rp + 1) * 128],
                                         op=OP.mult),
                                     waits=({"EXP": self.exp_of_pair[pair]} if idx == 0 else {}),
                                     inc=("MSK" if idx == len(act) - 1 else None))
                            did_mask = True
                        ww = {"EXP": self.exp_of_pair[pair]}
                        if did_mask:
                            ww["MSK"] = C["MSK"]
                        if first_unit:
                            ww["DVE"] = gate
                        items.append((buf, ucol, Wd, T_idx, ww, first_unit, last_unit))
                        self.msk_pair_end[pair] = C["MSK"]
                        first_unit = False
                    if pending is not None:
                        emit_npv(pending)
                    pending = (pair, items)
                emit_npv(pending)
                p.ev("scalar", lambda e: e.activation(s.lnnT[:, :], pn[:, 0:512], AF.Ln, bias=0.0),
                     waits={"NPV": C["NPV"]})
                p.ev("scalar", lambda e: e.drain())
                p.ev("scalar", lambda e: e.activation(s.rbcT[:, :], s.lnnT[:, :], AF.Exp, scale=-1.0),
                     inc="ACT")
                p.ev("vector", lambda e, h=h, blk_col=blk_col: e.tensor_tensor(
                    s.ao[:, h * TPC + blk_col: h * TPC + blk_col + 512],
                    pAcc[:, 0:512], s.rbcT[:, :], op=OP.mult),
                    waits={"ACT": C["ACT"]})
                p.ev("vector", lambda e: e.drain(), inc="DVE")
                state["epi_dve"] = C["DVE"]

            slab_load(0)
            if L == 1:
                p.ev("scalar", lambda e: e.dma_start(out=s.expb_s[0][:, :], in_=s.expb_d[0]), inc="EB", n=16)
                expb_ready[0] = C["EB"]
            p.ev("gpsimd", lambda e, L=L: e.dma_start(out=s.wqb[0][:, :], in_=s.W[f"wq{L}"][0]), inc="WQ", n=16)
            wq_ready[0] = C["WQ"]
            for kh in range(NKV):
                if kh == 0:
                    prepare_head(0, qproj_only=True)
                knorm(kh)
                if kh + 1 < NKV:
                    slab_load(kh + 1)
                for g_ in range(4):
                    h = 4 * kh + g_
                    prepare_head(h, skip_qproj=(h == 0))
                    emit_attn_block(h, 0, BLOCKS[0])
                    emit_attn_block(h, 1, BLOCKS[1])
                    sm_head_end[h] = C["SM"]
                    msk_head_end[h] = C["MSK"]
                    npv_head_end[h] = C["NPV"]
                    if L == 1 and h + 1 < NH:
                        p.ev("scalar", lambda e, h=h: e.dma_start(
                            out=s.expb_s[0][:, :], in_=s.expb_d[h + 1]),
                            waits={"MSK": C["MSK"]}, inc="EB", n=16)
                        expb_ready[h + 1] = C["EB"]
                npv_kh_end[kh] = C["NPV"]
            emit_block(nc, p, self.sems, f"att{L}", self.trace)
        return

    def phase_ffng(self, L):
        nc, C, s = self.nc, self.C, self
        p = Plan(C)
        with ExitStack() as sc:
            sbuf = lambda name, shape, dt: sc.enter_context(nc.sbuf_tensor(Builder.uname(name), shape, dt))
            rt_ps = ExitStack()
            plg = rt_ps.enter_context(nc.psum_tensor(Builder.uname("ps_lg"), [8, TPC], F32))
            pt8 = rt_ps.enter_context(nc.psum_tensor(Builder.uname("ps_t8"), [128, 64], F32))
            pcb = rt_ps.enter_context(nc.psum_tensor(Builder.uname("ps_cb"), [128, TPC], F32))
            ph = rt_ps.enter_context(nc.psum_tensor(Builder.uname("ps_h"), [128, TPC], F32))
            s.aT_s = sbuf("aT_s", [128, 4096], BF16)
            s.routT_s = sbuf("routT_s", [128, 256], BF16)
            s.logit_s = sbuf("logit_s", [8, TPC], F32)
            s.v8_s = sbuf("v8_s", [128, 64], F32)
            s.m8_s = sbuf("m8_s", [128, 8], F32)
            s.nm12_s = sbuf("nm12_s", [128, 8], F32)
            s.ge_s = sbuf("ge_s", [128, 8], F32)
            s.sg_s = sbuf("sg_s", [128, 8], F32)
            s.comb_s = sbuf("comb_s", [128, 64], F32)
            s.combT_s = sbuf("combT_s", [8, TPC], F32)
            s.h_s = sbuf("h_s", [128, TPC], F32)
            s.ball_s = sbuf("ball_s", [128, 4096], BF16)
            s.wbb = [sbuf(f"wbb{i}", [128, 4096], BF16) for i in range(2)]
            s.g_s = [sbuf(f"g_s{i}", [128, TPC], F32) for i in range(2)]
            p.ev("gpsimd", lambda e, L=L: e.dma_start(out=s.routT_s[:, :], in_=s.W[f"routT{L}"][:, :]), inc="RW", n=16)
            w_rt = C["RW"]
            p.ev("gpsimd", lambda e, L=L: e.dma_start(out=s.aT_s[:, :], in_=s.W[f"aT{L}"][:, :]), inc="GP", n=16)
            w_at = C["GP"]
            p.ev("gpsimd", lambda e, L=L: e.dma_start(out=s.ball_s[:, :], in_=s.W[f"ball{L}"][:, :]), inc="W", n=16)
            w_ready = C["W"]
            p.ev("gpsimd", lambda e, L=L: e.dma_start(out=s.wbb[0][:, :], in_=s.W[f"wb{L}"][0]), inc="OW0", n=16)
            ow_ready = {0: C["OW0"]}
            for c in range(NC):
                for hf in range(2):
                    p.ev("tensor", lambda e, c=c, hf=hf: e.matmul(
                        plg[0:8, hf * 512:(hf + 1) * 512], s.routT_s[:, c * 8:(c + 1) * 8],
                        s.xn[:, c * TPC + hf * 512: c * TPC + hf * 512 + 512],
                        start=(c == 0), stop=(c == NC - 1)),
                        waits=({"RW": w_rt} if (c == 0 and hf == 0) else {}),
                        inc=("PE" if (c == NC - 1 and hf == 1) else None))
            pe_lg = C["PE"]
            for c in range(NC):
                for hf in range(2):
                    p.ev("tensor", lambda e, c=c, hf=hf: e.matmul(
                        ph[:, hf * 512:(hf + 1) * 512], s.aT_s[:, c * 128:(c + 1) * 128],
                        s.xn[:, c * TPC + hf * 512: c * TPC + hf * 512 + 512],
                        start=(c == 0), stop=(c == NC - 1)),
                        waits=({"GP": w_at} if (c == 0 and hf == 0) else {}),
                        inc=("PE" if (c == NC - 1 and hf == 1) else None))
            pe_h = C["PE"]
            p.ev("vector", lambda e: e.tensor_copy(s.logit_s[0:8, :], plg[0:8, :]),
                 waits={"PE": pe_lg}, inc="DVE")
            for tt in range(8):
                p.ev("tensor", lambda e, tt=tt: e.transpose(
                    pt8[:, tt * 8:(tt + 1) * 8], s.logit_s[0:8, tt * 128:(tt + 1) * 128], s.idt_s[0:8, 0:8]),
                    waits=({"DVE": C["DVE"]} if tt == 0 else {}), inc=("PE" if tt == 7 else None))
            p.ev("vector", lambda e: e.tensor_copy(s.v8_s[:, :], pt8[:, :]), waits={"PE": C["PE"]})
            p.ev("vector", lambda e: e.drain())
            for tt in range(8):
                p.ev("vector", lambda e, tt=tt: e.max(s.m8_s[:, :], s.v8_s[:, tt * 8:(tt + 1) * 8]))
                p.ev("vector", lambda e: e.drain())
                p.ev("vector", lambda e, tt=tt: e.tensor_scalar(
                    s.nm12_s[:, tt:tt + 1], s.m8_s[:, 0:1], s.m8_s[:, 1:2], -1.0, OP.add, OP.mult))
                p.ev("vector", lambda e: e.drain())
                p.ev("vector", lambda e, tt=tt: e.tensor_scalar(
                    s.ge_s[:, :], s.v8_s[:, tt * 8:(tt + 1) * 8], s.m8_s[:, 1:2], None, OP.is_ge))
                p.ev("vector", lambda e: e.drain(), inc="DVE")
                p.ev("scalar", lambda e, tt=tt: e.activation(
                    s.sg_s[:, :], s.v8_s[:, tt * 8:(tt + 1) * 8], AF.Sigmoid,
                    bias=s.nm12_s[:, tt:tt + 1], scale=2.0),
                    waits={"DVE": C["DVE"]}, inc="ACT")
                p.ev("vector", lambda e, tt=tt: e.tensor_tensor(
                    s.comb_s[:, tt * 8:(tt + 1) * 8], s.sg_s[:, :], s.ge_s[:, :], op=OP.mult),
                    waits={"ACT": C["ACT"]})
                p.ev("vector", lambda e: e.drain(), inc="DVE")
                p.ev("tensor", lambda e, tt=tt: e.transpose(
                    plg[0:8, tt * 128:(tt + 1) * 128], s.comb_s[:, tt * 8:(tt + 1) * 8], s.idt_s[:, :]),
                    waits={"DVE": C["DVE"]}, inc="PE")
                p.ev("vector", lambda e, tt=tt: e.tensor_copy(
                    s.combT_s[0:8, tt * 128:(tt + 1) * 128], plg[0:8, tt * 128:(tt + 1) * 128]),
                    waits={"PE": C["PE"]}, inc="DVE")
            p.ev("vector", lambda e: e.drain(), inc="DVE")
            for hf in range(2):
                p.ev("tensor", lambda e, hf=hf: e.matmul(
                    pcb[:, hf * 512:(hf + 1) * 512], s.ind8_s[0:8, :], s.combT_s[0:8, hf * 512:(hf + 1) * 512],
                    start=True, stop=True),
                    waits=({"DVE": C["DVE"]} if hf == 0 else {}), inc=("PE" if hf == 1 else None))
            p.ev("vector", lambda e: e.tensor_copy(s.h_s[:, :], ph[:, :]), waits={"PE": pe_h})
            p.ev("vector", lambda e: e.drain())
            p.ev("vector", lambda e: e.tensor_tensor(s.hw_s[:, :], s.h_s[:, :], pcb[:, :], op=OP.mult),
                 waits={"PE": C["PE"]})
            p.ev("vector", lambda e: e.drain(), inc="DVE")
            hw_dve = C["DVE"]
            rt_ps.close()
            psB0 = sc.enter_context(nc.psum_tensor(Builder.uname("ps_B0"), [128, TPC], F32))
            psB1 = sc.enter_context(nc.psum_tensor(Builder.uname("ps_B1"), [128, TPC], F32))
            psB = [psB0, psB1]
            gs_ready = {}
            act_chunk_end = {}
            pe_chunk_end = {}
            for c in range(NC):
                if c + 1 < NC:
                    w = {}
                    if c >= 1:
                        w = {"PE": pe_chunk_end[c - 1]}
                    slot = f"OW{(c + 1) % 2}"
                    p.ev("gpsimd", lambda e, c=c, L=L: e.dma_start(
                        out=s.wbb[(c + 1) % 2][:, :], in_=s.W[f"wb{L}"][c + 1]), waits=w, inc=slot, n=16)
                    ow_ready[c + 1] = C[slot]
                ww = {f"OW{c % 2}": ow_ready[c], "DVE": hw_dve}
                if c >= 2:
                    ww["ACT"] = act_chunk_end[c - 2]
                for d_ in range(NC):
                    for hf in range(2):
                        p.ev("tensor", lambda e, c=c, d_=d_, hf=hf: e.matmul(
                            psB[c % 2][:, hf * 512:(hf + 1) * 512], s.wbb[c % 2][:, d_ * 128:(d_ + 1) * 128],
                            s.xn[:, d_ * TPC + hf * 512: d_ * TPC + hf * 512 + 512],
                            start=(d_ == 0), stop=False),
                            waits=(ww if (d_ == 0 and hf == 0) else {}))
                for hf in range(2):
                    p.ev("tensor", lambda e, c=c, hf=hf, L=L: e.matmul(
                        psB[c % 2][:, hf * 512:(hf + 1) * 512], s.ball_s[:, c * 128:(c + 1) * 128],
                        s.hw_s[:, hf * 512:(hf + 1) * 512], start=False, stop=True),
                        waits=({"W": w_ready} if (c == 0 and hf == 0) else {}),
                        inc=("PE" if hf == 1 else None))
                pe_chunk_end[c] = C["PE"]
                w = {"PE": C["PE"]}
                if c >= 2:
                    w[f"GS{c % 2}"] = gs_ready[c - 2]
                p.ev("scalar", lambda e, c=c, L=L: e.activation(
                    s.g_s[c % 2][:, :], psB[c % 2][:, :], AF.Silu, bias=s.bb_s[L][:, c:c + 1]),
                    waits=w, inc="ACT")
                act_chunk_end[c] = C["ACT"]
                p.ev("sync", lambda e, c=c: e.dma_start(
                    out=self.g_buf[c * 128:(c + 1) * 128, :], in_=s.g_s[c % 2][:, :]),
                    waits={"ACT": C["ACT"]}, inc=f"GS{c % 2}", n=16)
                gs_ready[c] = C[f"GS{c % 2}"]
            self.gs_done = (C["GS0"], C["GS1"])
            p.ev("sync", None, waits={"GS0": C["GS0"], "GS1": C["GS1"]})
            emit_block(nc, p, self.sems, f"ffng{L}", self.trace)

    def phase_out(self, L, xsrc, xdst, out_gate):
        nc, C, s = self.nc, self.C, self
        p = Plan(C)
        with ExitStack() as sc:
            sbuf = lambda name, shape, dt: sc.enter_context(nc.sbuf_tensor(Builder.uname(name), shape, dt))
            psA0 = sc.enter_context(nc.psum_tensor(Builder.uname("ps_A0"), [128, TPC], F32))
            psA1 = sc.enter_context(nc.psum_tensor(Builder.uname("ps_A1"), [128, TPC], F32))
            s.xbuf = [sbuf(f"xbuf{i}", [128, TPC], F32) for i in range(2)]
            s.gld = [sbuf(f"gld{i}", [128, TPC], F32) for i in range(2)]
            s.wob = [sbuf(f"wob{i}", [128, 4096], BF16) for i in range(2)]
            psA = [psA0, psA1]
            p.ev("gpsimd", lambda e, L=L: e.dma_start(out=s.wob[0][:, :], in_=s.W[f"wo{L}"][0]), inc="OW0", n=16)
            ow_ready = {0: C["OW0"]}
            xb_ready, gl_ready, out_ready = {}, {}, {}
            pe_chunk_end, dve_chunk_end = {}, {}
            gs0, gs1 = self.gs_done
            for c in range(NC):
                if c + 1 < NC:
                    w = {}
                    if c >= 1:
                        w = {"PE": pe_chunk_end[c - 1]}
                    slot = f"OW{(c + 1) % 2}"
                    p.ev("gpsimd", lambda e, c=c, L=L: e.dma_start(
                        out=s.wob[(c + 1) % 2][:, :], in_=s.W[f"wo{L}"][c + 1]), waits=w, inc=slot, n=16)
                    ow_ready[c + 1] = C[slot]
                w = {}
                if c == 0 and out_gate:
                    w.update(out_gate)
                if c >= 2:
                    w["DVE"] = dve_chunk_end[c - 2]
                    w[f"OUT{c % 2}"] = out_ready[c - 2]
                p.ev("sync", lambda e, c=c: e.dma_start(
                    out=s.xbuf[c % 2][:, :], in_=xsrc[c * 128:(c + 1) * 128, :]),
                    waits=w, inc=f"XB{c % 2}", n=16)
                xb_ready[c] = C[f"XB{c % 2}"]
                w = {f"GS{c % 2}": gs0 if c % 2 == 0 else gs1}
                if c >= 2:
                    w["DVE"] = dve_chunk_end[c - 2]
                p.ev("sync", lambda e, c=c: e.dma_start(
                    out=s.gld[c % 2][:, :], in_=self.g_buf[c * 128:(c + 1) * 128, :]),
                    waits=w, inc=f"GL{c % 2}", n=16)
                gl_ready[c] = C[f"GL{c % 2}"]
                ww = {f"OW{c % 2}": ow_ready[c]}
                if c >= 2:
                    ww["DVE"] = dve_chunk_end[c - 2]
                for d_ in range(NC):
                    for hf in range(2):
                        p.ev("tensor", lambda e, c=c, d_=d_, hf=hf: e.matmul(
                            psA[c % 2][:, hf * 512:(hf + 1) * 512], s.wob[c % 2][:, d_ * 128:(d_ + 1) * 128],
                            s.ao[:, d_ * TPC + hf * 512: d_ * TPC + hf * 512 + 512],
                            start=(d_ == 0), stop=(d_ == NC - 1)),
                            waits=(ww if (d_ == 0 and hf == 0) else {}),
                            inc=("PE" if (d_ == NC - 1 and hf == 1) else None))
                pe_chunk_end[c] = C["PE"]
                w = {"PE": C["PE"], f"GL{c % 2}": gl_ready[c], f"XB{c % 2}": xb_ready[c]}
                p.ev("vector", lambda e, c=c: e.tensor_tensor(
                    s.gld[c % 2][:, :], psA[c % 2][:, :], s.gld[c % 2][:, :], op=OP.mult), waits=w)
                p.ev("vector", lambda e: e.drain())
                p.ev("vector", lambda e, c=c: e.tensor_tensor(
                    s.xbuf[c % 2][:, :], s.gld[c % 2][:, :], s.xbuf[c % 2][:, :], op=OP.add))
                p.ev("vector", lambda e: e.drain(), inc="DVE")
                dve_chunk_end[c] = C["DVE"]
                p.ev("sync", lambda e, c=c: e.dma_start(
                    out=xdst[c * 128:(c + 1) * 128, :], in_=s.xbuf[c % 2][:, :]),
                    waits={"DVE": C["DVE"]}, inc=f"OUT{c % 2}", n=16)
                out_ready[c] = C[f"OUT{c % 2}"]
            p.ev("sync", None, waits={"OUT0": C["OUT0"], "OUT1": C["OUT1"]})
            emit_block(nc, p, self.sems, f"out{L}", self.trace)

    def phase_final(self):
        nc, C = self.nc, self.C
        p = Plan(C)
        dram_srcs = {"x_cur": self.x_cur}
        sbuf_srcs = {"xn": getattr(self, "xn", None), "ao": getattr(self, "ao", None),
                     "hw": getattr(self, "hw_s", None)}
        for name in self.dbg:
            if name in dram_srcs:
                src = dram_srcs[name]
                dd = nc.dram_tensor(f"dbg_{name}", list(src.shape), src.dtype, kind="ExternalOutput")
                p.ev("sync", lambda e, dd=dd, src=src: e.dma_start(out=dd[:, :], in_=src[:, :]),
                     waits={"OUT0": C["OUT0"], "OUT1": C["OUT1"], "CC": C["CC"]}, inc="OUT0", n=16)
            else:
                src = sbuf_srcs[name]
                shp = [int(x) for x in src.shape]
                dd = nc.dram_tensor(f"dbg_{name}", shp, src.dtype, kind="ExternalOutput")
                p.ev("sync", lambda e, dd=dd, src=src: e.dma_start(out=dd[:, :], in_=src[:, :]),
                     waits={"OUT0": C["OUT0"], "OUT1": C["OUT1"]}, inc="OUT0", n=16)
        p.ev("sync", None, waits={"OUT0": C["OUT0"], "OUT1": C["OUT1"]})
        emit_block(nc, p, self.sems, "fin", self.trace)

    def build(self):
        self.phase_consts()
        out_gate = None
        stop = self.stop_after

        def done(tag):
            return stop == tag

        for L in range(2):
            xsrc = self.x_in if L == 0 else self.x_cur
            xdst = self.x_cur if L == 0 else self.out_fm
            self.phase_ln_kv(L, xsrc, out_gate)
            if done(f"ln{L}") or done(f"kv{L}"):
                break
            self.phase_ffng(L)
            if done(f"rout{L}"):
                break
            with ExitStack() as lscope:
                self.ao = lscope.enter_context(
                    self.nc.sbuf_tensor(Builder.uname("ao"), [128, NC * TPC], BF16))
                self.phase_att(L)
                if done(f"att{L}"):
                    break
                self.phase_out(L, xsrc, xdst, out_gate)
            if done(f"out{L}"):
                break
            out_gate = {"OUT0": self.C["OUT0"], "OUT1": self.C["OUT1"]}
        self.phase_final()
        self.ctx.close()
        return self.nc


def build_program(debug=None, stop_after=None, att_parts="full", att_kh=NKV):
    return Builder(debug, stop_after, att_parts, att_kh).build()


def kernel(**inputs):
    in_maps, col_maps = host_prepare(inputs)
    nc = build_program()
    res = run_bass_kernel_spmd(nc, in_maps, core_ids=list(range(8)))
    out = np.empty((B, S, D), dtype=np.float32)
    for core in range(8):
        b, cols_seq = col_maps[core]
        out[b, cols_seq, :] = res.results[core]["out_fm"].T
    return out


# version 15
# speedup vs baseline: 1.0297x; 1.0134x over previous
import math
from collections import defaultdict
from contextlib import ExitStack

import numpy as np
import ml_dtypes

import concourse.bass as bass
import concourse.mybir as mybir
from concourse.alu_op_type import AluOpType
from concourse.bass_utils import run_bass_kernel_spmd

F32 = mybir.dt.float32
BF16 = mybir.dt.bfloat16
AF = mybir.ActivationFunctionType
AX = mybir.AxisListType
OP = AluOpType

B, S, D = 2, 4096, 4096
NH, NKV, HD = 32, 8, 128
TPC = 1024
NC = 32
SCALE = HD ** -0.5
EPS = 1e-5
LORA_SCALING = 2.0
CH_SAFETY = 1.01

ORDER = [6, 4, 2, 0, 7, 5, 3, 1]
SLOT_OF = {m: s for s, m in enumerate(ORDER)}
BLOCKS = ([6, 4, 2, 0], [7, 5, 3, 1])


def att_sched(tiles):
    out = []
    for lg in range(tiles[0] + 1):
        act = [m for m in tiles if m >= lg]
        out.append((lg, 128 * len(act), lg in tiles, act))
    return out


def alibi_slopes(n):
    cp2 = 2 ** math.floor(math.log2(n))
    base = 2.0 ** (-8.0 / cp2)
    slopes = base ** np.arange(1, cp2 + 1, dtype=np.float32)
    if cp2 != n:
        extra_base = 2.0 ** (-4.0 / cp2)
        extra = extra_base ** np.arange(1, 1 + 2 * (n - cp2), 2, dtype=np.float32)
        slopes = np.concatenate([slopes, extra])
    return slopes.astype(np.float32)



def _tile128(a):
    Dk, Fc = a.shape[0] // 128, a.shape[1] // 128
    return np.ascontiguousarray(
        a.reshape(Dk, 128, Fc, 128).transpose(2, 1, 0, 3).reshape(Fc, 128, Dk * 128))


def _sbuf_img(a, ncol):
    Dk = a.shape[0] // 128
    assert a.shape[1] == ncol
    return np.ascontiguousarray(a.reshape(Dk, 128, ncol).transpose(1, 0, 2).reshape(128, Dk * ncol))


def host_prepare(inputs):
    f32 = np.float32

    def wq_split(w):
        return np.stack([_sbuf_img(np.ascontiguousarray(w[:, h * 128:(h + 1) * 128]), 128)
                         for h in range(NH)])

    shared = {}
    for L in range(2):
        if L == 0:
            wq = np.asarray(inputs["L0_attn_wq"], f32)
            wkv = np.concatenate([np.asarray(inputs["L0_attn_wk"], f32),
                                  np.asarray(inputs["L0_attn_wv"], f32)], axis=1)
            wo = np.asarray(inputs["L0_attn_wo"], f32)
            Wb = np.asarray(inputs["L0_ffn_W_base"], f32)
            bb = np.asarray(inputs["L0_ffn_b_base"], f32)
            Wr = np.asarray(inputs["L0_ffn_W_router"], f32)
            A = np.asarray(inputs["L0_ffn_A"], f32)
            Bm = np.asarray(inputs["L0_ffn_B"], f32)
        else:
            wqkv = np.asarray(inputs["L1_attn_w_qkv"], f32)
            wq, wkv = wqkv[:, :4096], np.ascontiguousarray(wqkv[:, 4096:6144])
            wo = np.asarray(inputs["L1_attn_w_o"], f32)
            Wb = np.asarray(inputs["L1_ffn_W_base"], f32)
            bb = np.asarray(inputs["L1_ffn_b_base"], f32)
            Wr = np.asarray(inputs["L1_ffn_W_router"], f32)
            A = np.asarray(inputs["L1_ffn_A"], f32)
            Bm = np.asarray(inputs["L1_ffn_B"], f32)
        shared[f"wq{L}"] = wq_split(wq)
        strips = [wkv[:, 0:512], wkv[:, 1024:1536], wkv[:, 512:1024], wkv[:, 1536:2048]]
        shared[f"wkv{L}"] = np.stack(
            [_sbuf_img(np.ascontiguousarray(st), 512) for st in strips])
        shared[f"wo{L}"] = _tile128(wo)
        shared[f"wb{L}"] = _tile128(np.ascontiguousarray(Wb.T))
        shared[f"aT{L}"] = _sbuf_img(np.ascontiguousarray(A.reshape(128, 4096).T), 128)
        shared[f"ball{L}"] = np.ascontiguousarray(Bm.transpose(0, 2, 1).reshape(128, 4096))
        shared[f"routT{L}"] = _sbuf_img(np.ascontiguousarray(Wr.T), 8)
        shared[f"bb{L}"] = np.ascontiguousarray(bb.reshape(32, 128).T)

    shared["idt"] = np.eye(128, dtype=f32)
    ind8 = np.zeros((8, 128), dtype=f32)
    for e in range(8):
        ind8[e, e * 16:(e + 1) * 16] = LORA_SCALING
    shared["ind8"] = ind8

    slopes = alibi_slopes(NH)
    x = np.asarray(inputs["L0_attn_x"], f32)

    in_maps, col_maps = [], []
    jj = np.arange(128)[:, None]
    ii = np.arange(128)[None, :]
    for core in range(8):
        b, r = core // 4, core % 4
        cols_seq = np.empty(TPC, dtype=np.int64)
        for s_ in range(8):
            m = ORDER[s_]
            cols_seq[s_ * 128:(s_ + 1) * 128] = 4 * (128 * m + np.arange(128)) + r
        col_maps.append((b, cols_seq))
        xs = np.ascontiguousarray(x[b].T[:, cols_seq])

        m0 = np.stack([((jj < ii) | ((jj == ii) & (rp <= r))).astype(f32) for rp in range(4)])
        mask0 = np.ascontiguousarray(m0.transpose(1, 0, 2).reshape(128, 512)).astype(ml_dtypes.bfloat16)

        dl = np.arange(8)[:, None, None, None]
        rp = np.arange(4)[None, :, None, None]
        jj4 = jj[None, None, :, :] * np.ones((1, 1, 1, 128))
        dmat = (512 * dl + 4 * (ii[None, None] - jj[None, None]) + (r - rp)).astype(np.int64)
        dmat_f = dmat.astype(f32)
        expb = np.empty((NH, 128, 4096), dtype=ml_dtypes.bfloat16)
        for h in range(NH):
            tab = np.where(dmat >= 0, np.exp(-slopes[h] * np.sqrt(np.maximum(dmat_f, 0.0))), 0.0)
            expb[h] = tab.transpose(2, 0, 1, 3).reshape(128, 4096).astype(ml_dtypes.bfloat16)

        m = {"x_in": xs, "mask0": mask0, "expb": expb}
        m.update(shared)
        in_maps.append(m)
    return in_maps, col_maps



class Plan:
    def __init__(self, cnt):
        self.cnt = cnt
        self.events = []

    def ev(self, eng, fn, waits=None, inc=None, n=1):
        self.events.append((eng, dict(waits or {}), fn, inc, n))
        if inc is not None:
            self.cnt[inc] += n

    def v(self, sem):
        return self.cnt[sem]


def emit_block(nc, plan, sems, name, trace=None):
    if trace is not None:
        trace.append(("BLOCK", name, list(plan.events)))
    with nc.Block(name, no_gpsimd_drain=True) as block:
        for eng_name in ("gpsimd", "sync", "scalar", "vector", "tensor"):
            evs = [e for e in plan.events if e[0] == eng_name]
            if not evs:
                continue

            def make(evs):
                def body(eng):
                    seen = {}
                    for _, waits, fn, inc, n in evs:
                        for semname, val in waits.items():
                            if val > 0 and seen.get(semname, -1) < val:
                                eng.wait_ge(sems[semname], val)
                                seen[semname] = val
                        instr = fn(eng) if fn is not None else None
                        if inc is not None:
                            if instr is None:
                                instr = eng.nop()
                            instr.then_inc(sems[inc], n)
                return body
            getattr(block, eng_name)(make(evs))



class Builder:
    _uid = 0

    @classmethod
    def uname(cls, name):
        cls._uid += 1
        return f"{name}_{cls._uid}"

    def __init__(self, debug=None, stop_after=None, att_parts="full", att_kh=NKV):
        self.dbg = debug or []
        self.stop_after = stop_after
        self.att_parts = att_parts
        self.att_kh = att_kh
        nc = self.nc = bass.Bass("TRN2", target_bir_lowering=False)
        self.C = defaultdict(int)

        self.x_in = nc.dram_tensor("x_in", [D, TPC], F32, kind="ExternalInput")
        self.out_fm = nc.dram_tensor("out_fm", [D, TPC], F32, kind="ExternalOutput")
        self.idt_d = nc.dram_tensor("idt", [128, 128], F32, kind="ExternalInput")
        self.ind8_d = nc.dram_tensor("ind8", [8, 128], F32, kind="ExternalInput")
        self.mask0_d = nc.dram_tensor("mask0", [128, 512], BF16, kind="ExternalInput")
        self.expb_d = nc.dram_tensor("expb", [NH, 128, 4096], BF16, kind="ExternalInput")
        self.W = {}
        for L in range(2):
            for nm, shape in [("wq", [NH, 128, 4096]), ("wkv", [4, 128, 16384]),
                              ("wo", [NC, 128, 4096]), ("wb", [NC, 128, 4096]),
                              ("aT", [128, 4096]), ("ball", [128, 4096]),
                              ("routT", [128, 256]), ("bb", [128, 32])]:
                self.W[f"{nm}{L}"] = nc.dram_tensor(f"{nm}{L}", shape, F32, kind="ExternalInput")
        self.x_cur = nc.dram_tensor("x_cur", [D, TPC], F32)
        self.g_buf = nc.dram_tensor("g_buf", [D, TPC], F32)
        self.kv_loc = [[nc.dram_tensor(f"kv_loc{L}_{s_}", [TPC, 512], BF16) for s_ in range(4)]
                       for L in range(2)]
        self.kv_full = [[nc.dram_tensor(f"kv_full{L}_{s_}", [4 * TPC, 512], BF16) for s_ in range(4)]
                        for L in range(2)]

        ctx = self.ctx = ExitStack()
        sbuf = lambda name, shape, dt: ctx.enter_context(nc.sbuf_tensor(name, shape, dt))
        semnames = (["W", "CC", "PE", "DVE", "ACT", "SM", "EXP", "MSK", "NPV",
                     "WQ", "EB", "KSQ", "RW", "GP"]
                    + [f"XB{i}" for i in range(4)]
                    + ["KVW0", "KVW1", "KST0", "KST1", "SLB0", "SLB1",
                       "OW0", "OW1", "OUT0", "OUT1", "GS0", "GS1", "GL0", "GL1"])
        self.sems = {s: ctx.enter_context(nc.semaphore(s)) for s in semnames}

        s = self
        s.sbuf = sbuf
        s.xn = sbuf("xn", [128, NC * TPC], BF16)
        s.onesb = sbuf("onesb", [128, 1], BF16)
        s.ones128 = sbuf("ones128", [128, 128], BF16)
        s.onesF = sbuf("onesF", [128, 1], F32)
        s.onesP = sbuf("onesP", [1, 128], F32)
        s.negP = sbuf("negP", [1, 128], F32)
        s.idt_s = sbuf("idt_s", [128, 128], F32)
        s.ind8_s = sbuf("ind8_s", [8, 128], F32)
        s.mask0_s = sbuf("mask0_s", [128, 512], BF16)
        s.bb_s = [sbuf(f"bb_s{L}", [128, 32], F32) for L in range(2)]
        s.hw_s = sbuf("hw_s", [128, TPC], BF16)
        s.constv = sbuf("constv", [128, 4], F32)
        self.const_vals = [EPS, math.log(SCALE * CH_SAFETY)]
        for i, v in enumerate(self.const_vals):
            nc.const_aps.register(F32, v, s.constv[:, i:i + 1])

        self.trace = []
        self.pending_negc = None
        self.pair_global = 0
        self.exp_of_pair = {}
        self.npv_pair_end = {}
        self.msk_pair_end = {}

    def phase_consts(self):
        C = self.C
        p = Plan(C)
        s = self
        p.ev("vector", lambda e: e.memset(s.onesb[:, :], 1.0))
        p.ev("vector", lambda e: e.memset(s.ones128[:, :], 1.0))
        p.ev("vector", lambda e: e.memset(s.onesF[:, :], 1.0))
        p.ev("vector", lambda e: e.memset(s.onesP[0:1, :], 1.0))
        p.ev("vector", lambda e: e.memset(s.negP[0:1, :], -1.0))
        for i, v in enumerate(self.const_vals):
            p.ev("vector", lambda e, i=i, v=v: e.memset(s.constv[:, i:i + 1], v))
        p.ev("vector", lambda e: e.drain(), inc="DVE")
        p.ev("gpsimd", lambda e: e.dma_start(out=s.idt_s[:, :], in_=s.idt_d[:, :]), inc="W", n=16)
        p.ev("gpsimd", lambda e: e.dma_start(out=s.ind8_s[:, :], in_=s.ind8_d[:, :]), inc="W", n=16)
        p.ev("gpsimd", lambda e: e.dma_start(out=s.mask0_s[:, :], in_=s.mask0_d[:, :]), inc="W", n=16)
        for L in range(2):
            p.ev("gpsimd", lambda e, L=L: e.dma_start(out=s.bb_s[L][:, :], in_=s.W[f"bb{L}"][:, :]), inc="W", n=16)
        p.ev("gpsimd", None, waits={"W": p.v("W")})
        emit_block(self.nc, p, self.sems, "consts", self.trace)

    def phase_ln_kv(self, L, xsrc, out_gate):
        nc, C, s = self.nc, self.C, self
        p = Plan(C)
        with ExitStack() as sc:
            sbuf = lambda name, shape, dt: sc.enter_context(nc.sbuf_tensor(Builder.uname(name), shape, dt))
            s.wkvs = [sbuf(f"wkvs{i}", [128, 16384], BF16) for i in range(2)]
            p.ev("gpsimd", lambda e, L=L: e.dma_start(out=s.wkvs[0][:, :], in_=s.W[f"wkv{L}"][0]), inc="KVW0", n=16)
            p.ev("gpsimd", lambda e, L=L: e.dma_start(out=s.wkvs[1][:, :], in_=s.W[f"wkv{L}"][1]), inc="KVW1", n=16)
            strip_ready = {0: ("KVW0", C["KVW0"]), 1: ("KVW1", C["KVW1"])}
            ln_ps = ExitStack()
            ps_s = ln_ps.enter_context(nc.psum_tensor(Builder.uname("ps_s"), [1, TPC], F32))
            ps_q = ln_ps.enter_context(nc.psum_tensor(Builder.uname("ps_q"), [1, TPC], F32))
            ps_A = ln_ps.enter_context(nc.psum_tensor(Builder.uname("ps_A"), [128, TPC], F32))
            ps_B = ln_ps.enter_context(nc.psum_tensor(Builder.uname("ps_B"), [128, TPC], F32))
            s.xbuf = [sbuf(f"xbuf{i}", [128, TPC], F32) for i in range(4)]
            s.xsqbuf = [sbuf(f"xsqbuf{i}", [128, TPC], F32) for i in range(2)]
            s.tmpf = [sbuf(f"tmpf{i}", [128, TPC], F32) for i in range(2)]
            st = [sbuf(f"st{i}", [1, TPC], F32) for i in range(4)]
            s.meanT, s.msqT, s.vqT, s.varT = st[0], st[1], st[2], st[3]
            s.lnvT, s.rstdT, s.mrstdT = st[1], st[2], st[3]
            s.AbcT = sbuf("AbcT", [128, TPC], F32)
            s.BbcT = sbuf("BbcT", [128, TPC], F32)
            base_PE, base_DVE, base_ACT = C["PE"], C["DVE"], C["ACT"]
            xb_ready = {}
            xn_dve = {}
            for c in range(NC):
                w = {}
                if c == 0 and out_gate:
                    w.update(out_gate)
                if c >= 4:
                    w.update({"ACT": base_ACT + (c - 3), "PE": base_PE + (c - 3)})
                p.ev("sync", lambda e, c=c: e.dma_start(
                    out=s.xbuf[c % 4][:, :], in_=xsrc[c * 128:(c + 1) * 128, :]),
                    waits=w, inc=f"XB{c % 4}", n=16)
                xb_ready[c] = C[f"XB{c % 4}"]
            for c in range(NC):
                w = {f"XB{c % 4}": xb_ready[c]}
                if c >= 2:
                    w["PE"] = base_PE + (c - 1)
                p.ev("scalar", lambda e, c=c: e.activation(s.xsqbuf[c % 2][:, :], s.xbuf[c % 4][:, :], AF.Square),
                     waits=w, inc="ACT")
            for c in range(NC):
                w = {"ACT": base_ACT + (c + 1)}
                if c >= 2:
                    w["PE"] = base_PE + (c - 1)
                for hf in range(2):
                    p.ev("tensor", lambda e, c=c, hf=hf: e.matmul(
                        ps_s[0:1, hf * 512:(hf + 1) * 512], s.onesF[:, :],
                        s.xbuf[c % 4][:, hf * 512:(hf + 1) * 512],
                        start=(c == 0), stop=(c == NC - 1)), waits=(w if hf == 0 else {}))
                    p.ev("tensor", lambda e, c=c, hf=hf: e.matmul(
                        ps_q[0:1, hf * 512:(hf + 1) * 512], s.onesF[:, :],
                        s.xsqbuf[c % 2][:, hf * 512:(hf + 1) * 512],
                        start=(c == 0), stop=(c == NC - 1)), inc=("PE" if hf == 1 else None))
            sq_act = base_ACT + NC
            p.ev("vector", lambda e: e.tensor_scalar(s.meanT[0:1, :], ps_s[0:1, :], 1.0 / D, None, OP.mult),
                 waits={"PE": p.v("PE")})
            p.ev("vector", lambda e: e.drain())
            p.ev("vector", lambda e: e.tensor_tensor(s.msqT[0:1, :], s.meanT[0:1, :], s.meanT[0:1, :], op=OP.mult))
            p.ev("vector", lambda e: e.drain())
            p.ev("vector", lambda e: e.tensor_scalar(s.vqT[0:1, :], ps_q[0:1, :], 1.0 / D, None, OP.mult))
            p.ev("vector", lambda e: e.drain())
            p.ev("vector", lambda e: e.tensor_tensor(s.varT[0:1, :], s.vqT[0:1, :], s.msqT[0:1, :], op=OP.subtract),
                 inc="DVE")
            p.ev("scalar", lambda e: e.activation(s.lnvT[0:1, :], s.varT[0:1, :], AF.Ln, bias=EPS),
                 waits={"DVE": p.v("DVE")})
            p.ev("scalar", lambda e: e.drain())
            p.ev("scalar", lambda e: e.activation(s.rstdT[0:1, :], s.lnvT[0:1, :], AF.Exp, scale=-0.5), inc="ACT")
            p.ev("vector", lambda e: e.tensor_tensor(s.mrstdT[0:1, :], s.meanT[0:1, :], s.rstdT[0:1, :], op=OP.mult),
                 waits={"ACT": p.v("ACT")})
            p.ev("vector", lambda e: e.drain(), inc="DVE")
            w = {"DVE": p.v("DVE")}
            for hf in range(2):
                p.ev("tensor", lambda e, hf=hf: e.matmul(
                    ps_A[:, hf * 512:(hf + 1) * 512], s.onesP[0:1, :], s.rstdT[0:1, hf * 512:(hf + 1) * 512],
                    start=True, stop=True), waits=(w if hf == 0 else {}))
                p.ev("tensor", lambda e, hf=hf: e.matmul(
                    ps_B[:, hf * 512:(hf + 1) * 512], s.negP[0:1, :], s.mrstdT[0:1, hf * 512:(hf + 1) * 512],
                    start=True, stop=True), inc=("PE" if hf == 1 else None))
            p.ev("vector", lambda e: e.tensor_copy(s.AbcT[:, :], ps_A[:, :]), waits={"PE": p.v("PE")})
            p.ev("vector", lambda e: e.tensor_copy(s.BbcT[:, :], ps_B[:, :]))
            p.ev("vector", lambda e: e.drain(), inc="DVE")
            dve_bc = p.v("DVE")
            for c in range(NC):
                xn_dve[c] = ("DVE", dve_bc + c + 1)
            for c in range(NC):
                w = {"DVE": dve_bc, "ACT": sq_act}
                if c >= 4:
                    sem4, val4 = xn_dve[c - 4]
                    w[sem4] = max(w.get(sem4, 0), val4)
                p.ev("sync", lambda e, c=c: e.dma_start(
                    out=s.xbuf[c % 4][:, :], in_=xsrc[c * 128:(c + 1) * 128, :]),
                    waits=w, inc=f"XB{c % 4}", n=16)
                xb_ready[c] = C[f"XB{c % 4}"]
            for c in range(NC):
                ww = {f"XB{c % 4}": xb_ready[c]}
                p.ev("vector", lambda e, c=c: e.tensor_tensor(
                    s.tmpf[c % 2][:, :], s.xbuf[c % 4][:, :], s.AbcT[:, :], op=OP.mult), waits=ww)
                p.ev("vector", lambda e: e.drain())
                p.ev("vector", lambda e, c=c: e.tensor_tensor(
                    s.xn[:, c * TPC:(c + 1) * TPC], s.tmpf[c % 2][:, :], s.BbcT[:, :], op=OP.add))
                p.ev("vector", lambda e: e.drain(), inc="DVE")
                assert xn_dve[c] == ("DVE", C["DVE"])
            ln_ps.close()
            pkv0 = sc.enter_context(nc.psum_tensor(Builder.uname("ps_kv0"), [128, 512], F32))
            pkv1 = sc.enter_context(nc.psum_tensor(Builder.uname("ps_kv1"), [128, 512], F32))
            s.kvst = [sbuf(f"kvst{i}", [128, 512], BF16) for i in range(2)]
            pkv = [pkv0, pkv1]
            base_PE, base_DVE = C["PE"], C["DVE"]
            kst_ready = {}
            for s_ in range(4):
                if 1 <= s_ < 3:
                    w = {"PE": base_PE + 8 * s_}
                    slot = f"KVW{(s_ + 1) % 2}"
                    p.ev("gpsimd", lambda e, s_=s_: e.dma_start(
                        out=s.wkvs[(s_ + 1) % 2][:, :], in_=s.W[f"wkv{L}"][s_ + 1]), waits=w, inc=slot, n=16)
                    strip_ready[s_ + 1] = (slot, C[slot])
                for tt in range(8):
                    it = s_ * 8 + tt
                    for c in range(NC):
                        semc, valc = xn_dve[c]
                        w = {semc: valc}
                        if c == 0:
                            sl, val = strip_ready[s_]
                            w[sl] = val
                            if it >= 2:
                                w["DVE"] = max(w.get("DVE", 0), base_DVE + (it - 1))
                        p.ev("tensor", lambda e, s_=s_, tt=tt, c=c: e.matmul(
                            pkv[tt % 2][:, :], s.xn[:, c * TPC + tt * 128: c * TPC + (tt + 1) * 128],
                            s.wkvs[s_ % 2][:, c * 512:(c + 1) * 512],
                            start=(c == 0), stop=(c == NC - 1)),
                            waits=w, inc=("PE" if c == NC - 1 else None))
                    w2 = {"PE": base_PE + it + 1}
                    if it >= 2:
                        w2[f"KST{it % 2}"] = kst_ready[it - 2]
                    p.ev("vector", lambda e, tt=tt: e.tensor_copy(s.kvst[tt % 2][:, :], pkv[tt % 2][:, :]),
                         waits=w2, inc="DVE")
                    p.ev("sync", lambda e, s_=s_, tt=tt, L=L: e.dma_start(
                        out=self.kv_loc[L][s_][tt * 128:(tt + 1) * 128, :],
                        in_=s.kvst[tt % 2][:, :]),
                        waits={"DVE": base_DVE + it + 1}, inc=f"KST{it % 2}", n=16)
                    kst_ready[it] = C[f"KST{it % 2}"]
                p.ev("gpsimd", lambda e, s_=s_, L=L: e.collective_compute(
                    "AllGather", OP.bypass, replica_groups=[[0, 1, 2, 3], [4, 5, 6, 7]],
                    ins=[self.kv_loc[L][s_].ap().opt()], outs=[self.kv_full[L][s_].ap().opt()]),
                    waits={"KST0": C["KST0"], "KST1": C["KST1"]}, inc="CC")
            emit_block(nc, p, self.sems, f"lnkv{L}", self.trace)

    def phase_att(self, L):
        nc, C, s = self.nc, self.C, self
        p = Plan(C)
        sm_head_end, msk_head_end, npv_head_end = {}, {}, {}
        npv_kh_end = {}

        with ExitStack() as sc:
            sbuf = lambda name, shape, dt: sc.enter_context(nc.sbuf_tensor(Builder.uname(name), shape, dt))
            pS0 = sc.enter_context(nc.psum_tensor(Builder.uname("ps_S0"), [128, 1024], F32))
            pS1 = sc.enter_context(nc.psum_tensor(Builder.uname("ps_S1"), [128, 1024], F32))
            pAcc = sc.enter_context(nc.psum_tensor(Builder.uname("ps_acc"), [128, 512], F32))
            pn = sc.enter_context(nc.psum_tensor(Builder.uname("ps_n"), [128, 512], F32))
            pq = sc.enter_context(nc.psum_tensor(Builder.uname("ps_pq"), [128, 512], F32))
            pnorm = sc.enter_context(nc.psum_tensor(Builder.uname("ps_norm"), [128, 512], F32))
            s.kfm = [sbuf(f"kfm{i}", [128, 4096], BF16) for i in range(2)]
            s.vsl = [sbuf(f"vsl{i}", [128, 4096], BF16) for i in range(2)]
            s.ksq = sbuf("ksq", [128, 4096], BF16)
            s.Ubuf = [sbuf(f"Ubuf{i}", [128, 1024], BF16) for i in range(2)]
            s.expb_s = [sbuf("expb_s0", [128, 4096], BF16)]
            s.wqb = [sbuf("wqb0", [128, 4096], BF16)]
            s.qh = [sbuf(f"qh{i}", [128, TPC], BF16) for i in range(2)]
            s.qsq = sbuf("qsq", [128, TPC], BF16)
            s.kaccT = sbuf("kaccT", [128, 16], F32)
            s.kmaxT = sbuf("kmaxT", [128, 2], F32)
            s.qaccT = sbuf("qaccT", [128, 2], F32)
            s.qkT = sbuf("qkT", [128, 2], F32)
            s.lnqkT = sbuf("lnqkT", [128, 1], F32)
            s.chT = sbuf("chT", [128, 1], F32)
            s.cbias = sbuf("cbias", [128, 2], F32)
            s.lnnT = sbuf("lnnT", [128, 512], F32)
            s.rbcT = sbuf("rbcT", [128, 512], F32)
            pS = [pS0, pS1]
            slab_ready = {}
            ksq_done = {}
            wq_ready = {}
            expb_ready = {}
            qh_gate = {}
            state = {"epi_dve": 0}

            cc_base = C["CC"] - 4
            def slab_load(kh):
                bufk, bufv = s.kfm[kh % 2], s.vsl[kh % 2]
                kstrip = 0 if kh < 4 else 2
                vstrip = 1 if kh < 4 else 3
                col = (kh % 4) * 128
                slot = f"SLB{kh % 2}"
                w = {"CC": cc_base + kstrip + 1}
                if kh >= 2:
                    w["NPV"] = npv_kh_end[kh - 2]
                    w["KSQ"] = ksq_done[kh - 2]
                p.ev("sync", lambda e, bufk=bufk, kstrip=kstrip, col=col, L=L: e.dma_start_transpose(
                    bufk[:, :], self.kv_full[L][kstrip][:, col:col + 128]), waits=w, inc=slot, n=16)
                p.ev("sync", None, waits={"CC": cc_base + vstrip + 1})
                for q4 in range(4):
                    p.ev("sync", lambda e, q4=q4, bufv=bufv, vstrip=vstrip, col=col, L=L: e.dma_start(
                        out=bufv[:, q4 * 1024:(q4 + 1) * 1024].rearrange("p (t d) -> p t d", t=8),
                        in_=self.kv_full[L][vstrip][q4 * 1024:(q4 + 1) * 1024, col:col + 128]
                        .rearrange("(t p) d -> p t d", p=128)), inc=slot, n=16)
                slab_ready[kh] = C[slot]

            def knorm(kh):
                p.ev("vector", lambda e, kh=kh: e.tensor_tensor(
                    s.ksq[:, :], s.kfm[kh % 2][:, :], s.kfm[kh % 2][:, :], op=OP.mult),
                    waits={f"SLB{kh % 2}": slab_ready[kh]}, inc="KSQ")
                ksq_done[kh] = C["KSQ"]
                base_dve_k = C["DVE"]
                for j8 in range(8):
                    tgt = pnorm if j8 % 2 == 0 else pq
                    w = {"KSQ": C["KSQ"]}
                    if j8 < 2:
                        w["DVE"] = base_dve_k
                        w["ACT"] = C["ACT"]
                    else:
                        w["DVE"] = base_dve_k + (j8 - 1)
                    p.ev("tensor", lambda e, j8=j8, tgt=tgt: e.matmul(
                        tgt[:, :], s.ones128[:, :], s.ksq[:, j8 * 512:(j8 + 1) * 512], start=True, stop=True),
                        waits=w, inc="PE")
                for j8 in range(8):
                    tgt = pnorm if j8 % 2 == 0 else pq
                    p.ev("vector", lambda e, j8=j8, tgt=tgt: e.reduce_max(
                        s.kaccT[:, j8:j8 + 1], tgt[:, :], axis=AX.X),
                        waits={"PE": C["PE"] - 7 + j8}, inc="DVE")
                p.ev("vector", lambda e: e.drain())
                p.ev("vector", lambda e, kh=kh: e.reduce_max(s.kmaxT[:, kh % 2:kh % 2 + 1], s.kaccT[:, 0:8], axis=AX.X))
                p.ev("vector", lambda e: e.drain(), inc="DVE")

            def prepare_head(h, qproj_only=False, skip_qproj=False):
                kh = h // 4
                if skip_qproj:
                    pass
                else:
                  for hf in range(2):
                    tgt = pq if hf == 0 else pnorm
                    for c in range(NC):
                        ww = {}
                        if c == 0:
                            ww = {"WQ": wq_ready[h], "DVE": C["DVE"], "ACT": C["ACT"]}
                            if hf == 0 and h >= 2:
                                ww["NPV"] = npv_head_end[h - 2]
                        p.ev("tensor", lambda e, hf=hf, c=c, h=h, tgt=tgt: e.matmul(
                            tgt[:, :], s.wqb[0][:, c * 128:(c + 1) * 128],
                            s.xn[:, c * TPC + hf * 512: c * TPC + hf * 512 + 512],
                            start=(c == 0), stop=(c == NC - 1)), waits=ww,
                            inc=("PE" if c == NC - 1 else None))
                    pe_now = C["PE"]
                    p.ev("vector", lambda e, hf=hf, h=h, tgt=tgt: e.tensor_copy(
                        s.qh[h % 2][:, hf * 512:(hf + 1) * 512], tgt[:, :]), waits={"PE": pe_now}, inc="DVE")
                    p.ev("scalar", lambda e, hf=hf, tgt=tgt: e.activation(
                        s.qsq[:, hf * 512:(hf + 1) * 512], tgt[:, :], AF.Square),
                        waits={"PE": pe_now, "DVE": C["DVE"]}, inc="ACT")
                if h + 1 < NH and not skip_qproj:
                    p.ev("gpsimd", lambda e, h=h, L=L: e.dma_start(
                        out=s.wqb[0][:, :], in_=s.W[f"wq{L}"][h + 1]),
                        waits={"PE": C["PE"]}, inc="WQ", n=16)
                    wq_ready[h + 1] = C["WQ"]
                qh_gate[h] = C["DVE"]
                if qproj_only:
                    return
                for hf in range(2):
                    p.ev("tensor", lambda e, hf=hf: e.matmul(
                        pnorm[:, :] if hf == 0 else pq[:, :],
                        s.ones128[:, :], s.qsq[:, hf * 512:(hf + 1) * 512], start=True, stop=True),
                        waits=({"ACT": C["ACT"], "DVE": C["DVE"]} if hf == 0 else {"ACT": C["ACT"]}), inc="PE")
                pe_q = C["PE"]
                p.ev("vector", lambda e: e.reduce_max(s.qaccT[:, 0:1], pnorm[:, :], axis=AX.X),
                     waits={"PE": pe_q - 1})
                p.ev("vector", lambda e: e.drain())
                p.ev("vector", lambda e: e.reduce_max(s.qaccT[:, 1:2], pq[:, :], axis=AX.X),
                     waits={"PE": pe_q}, inc="DVE")
                p.ev("vector", lambda e: e.drain())
                p.ev("vector", lambda e, kh=kh: e.tensor_tensor(
                    s.qkT[:, 0:1], s.qaccT[:, 0:1], s.kmaxT[:, kh % 2:kh % 2 + 1], op=OP.mult))
                p.ev("vector", lambda e: e.drain())
                p.ev("vector", lambda e, kh=kh: e.tensor_tensor(
                    s.qkT[:, 1:2], s.qaccT[:, 1:2], s.kmaxT[:, kh % 2:kh % 2 + 1], op=OP.mult))
                p.ev("vector", lambda e: e.drain())
                p.ev("vector", lambda e: e.tensor_tensor(
                    s.lnqkT[:, 0:1], s.qkT[:, 0:1], s.qkT[:, 1:2], op=OP.max))
                p.ev("vector", lambda e: e.drain(), inc="DVE")
                p.ev("scalar", lambda e: e.activation(s.chT[:, 0:1], s.lnqkT[:, 0:1], AF.Ln, bias=0.0),
                     waits={"DVE": C["DVE"]})
                p.ev("scalar", lambda e: e.drain())
                p.ev("scalar", lambda e: e.activation(
                    s.chT[:, 0:1], s.chT[:, 0:1], AF.Exp, scale=0.5,
                    bias=math.log(SCALE * CH_SAFETY)))
                p.ev("scalar", lambda e: e.drain())
                p.ev("scalar", lambda e, h=h: e.mul(s.cbias[:, h % 2:h % 2 + 1], s.chT[:, 0:1], -1.0), inc="ACT")
                p.ev("scalar", lambda e: e.drain())

            def emit_attn_block(h, bi, tiles):
                kh = h // 4
                blk_col = bi * 512
                sched = att_sched(tiles)
                units = [(lg, Wd, bnd, act, rp) for (lg, Wd, bnd, act) in sched for rp in range(4)]
                first_unit = True
                pending = None
                gate = state["epi_dve"]

                def emit_npv(arg):
                    pid, items = arg
                    for (buf_, ucol_, Wd_, T_idx_, ww_, fu_, lu_) in items:
                        p.ev("tensor", lambda e, buf=buf_, ucol=ucol_, Wd=Wd_, fu=fu_, lu=lu_: e.matmul(
                            pn[:, 0:Wd], s.ones128[:, :], s.Ubuf[buf][:, ucol:ucol + Wd],
                            start=fu, stop=lu), waits=ww_)
                        p.ev("tensor", lambda e, buf=buf_, ucol=ucol_, Wd=Wd_, T_idx=T_idx_, kh=kh,
                             fu=fu_, lu=lu_: e.matmul(
                                 pAcc[:, 0:Wd], s.vsl[kh % 2][:, T_idx * 128:(T_idx + 1) * 128],
                                 s.Ubuf[buf][:, ucol:ucol + Wd],
                                 start=fu, stop=lu), inc="NPV")
                    self.npv_pair_end[pid] = C["NPV"]

                for pu in range(0, len(units), 2):
                    pair = self.pair_global
                    self.pair_global += 1
                    buf = pair % 2
                    for slot in range(2):
                        lg, Wd, bnd, act, rp = units[pu + slot]
                        T_idx = 8 * rp + SLOT_OF[lg]
                        ww = {}
                        if pu in (0, 2) and slot == 0:
                            ww["DVE"] = qh_gate[h]
                        if pu == 0 and slot == 0 and L == 1:
                            ww["EB"] = expb_ready[h]
                        if slot == 0 and pair >= 2:
                            ww["EXP"] = self.exp_of_pair[pair - 2]
                        p.ev("tensor", lambda e, buf=buf, slot=slot, Wd=Wd, T_idx=T_idx, h=h,
                             blk_col=blk_col, kh=kh: e.matmul(
                                 pS[buf][:, slot * 512: slot * 512 + Wd],
                                 s.kfm[kh % 2][:, T_idx * 128:(T_idx + 1) * 128],
                                 s.qh[h % 2][:, blk_col:blk_col + Wd],
                                 start=True, stop=True), waits=ww, inc="SM")
                    lg, Wd, bnd, act, _ = units[pu]
                    ww = {"SM": C["SM"]}
                    if pair >= 2:
                        ww["NPV"] = self.npv_pair_end[pair - 2]
                        ww["MSK"] = self.msk_pair_end[pair - 2]
                    if Wd == 512:
                        p.ev("scalar", lambda e, buf=buf, h=h: e.activation(
                            s.Ubuf[buf][:, 0:1024], pS[buf][:, 0:1024], AF.Exp,
                            bias=s.cbias[:, h % 2:h % 2 + 1], scale=SCALE), waits=ww, inc="EXP")
                    else:
                        p.ev("scalar", lambda e, buf=buf, Wd=Wd, h=h: e.activation(
                            s.Ubuf[buf][:, :].rearrange("p (s w) -> p s w", s=2)[:, :, 0:Wd],
                            pS[buf][:, :].rearrange("p (s w) -> p s w", s=2)[:, :, 0:Wd],
                            AF.Exp, bias=s.cbias[:, h % 2:h % 2 + 1], scale=SCALE), waits=ww, inc="EXP")
                    self.exp_of_pair[pair] = C["EXP"]
                    items = []
                    for slot in range(2):
                        lg, Wd, bnd, act, rp = units[pu + slot]
                        last_unit = (pu + slot == len(units) - 1)
                        T_idx = 8 * rp + SLOT_OF[lg]
                        ucol = slot * 512
                        did_mask = False
                        if L == 0:
                            if bnd:
                                p.ev("vector", lambda e, buf=buf, ucol=ucol, Wd=Wd, rp=rp: e.tensor_tensor(
                                    s.Ubuf[buf][:, ucol + Wd - 128: ucol + Wd],
                                    s.Ubuf[buf][:, ucol + Wd - 128: ucol + Wd],
                                    s.mask0_s[:, rp * 128:(rp + 1) * 128], op=OP.mult),
                                    waits={"EXP": self.exp_of_pair[pair]}, inc="MSK")
                                did_mask = True
                        else:
                            nact = len(act)
                            dl0 = act[0] - lg
                            base_off = (dl0 * 4 + rp) * 128
                            tabap = bass.AP(s.expb_s[0].tensor if hasattr(s.expb_s[0], "tensor") else s.expb_s[0],
                                            base_off, [[4096, 128], [-1024, nact], [1, 128]])
                            uap = s.Ubuf[buf][:, ucol:ucol + Wd].rearrange("p (t n) -> p t n", n=128)
                            p.ev("vector", lambda e, uap=uap, tabap=tabap: e.tensor_tensor(
                                uap, uap, tabap, op=OP.mult),
                                waits={"EXP": self.exp_of_pair[pair]}, inc="MSK")
                            did_mask = True
                        ww = {"EXP": self.exp_of_pair[pair]}
                        if did_mask:
                            ww["MSK"] = C["MSK"]
                        if first_unit:
                            ww["DVE"] = gate
                        items.append((buf, ucol, Wd, T_idx, ww, first_unit, last_unit))
                        self.msk_pair_end[pair] = C["MSK"]
                        first_unit = False
                    if pending is not None:
                        emit_npv(pending)
                    pending = (pair, items)
                emit_npv(pending)
                p.ev("scalar", lambda e: e.activation(s.lnnT[:, :], pn[:, 0:512], AF.Ln, bias=0.0),
                     waits={"NPV": C["NPV"]})
                p.ev("scalar", lambda e: e.drain())
                p.ev("scalar", lambda e: e.activation(s.rbcT[:, :], s.lnnT[:, :], AF.Exp, scale=-1.0),
                     inc="ACT")
                p.ev("vector", lambda e, h=h, blk_col=blk_col: e.tensor_tensor(
                    s.ao[:, h * TPC + blk_col: h * TPC + blk_col + 512],
                    pAcc[:, 0:512], s.rbcT[:, :], op=OP.mult),
                    waits={"ACT": C["ACT"]})
                p.ev("vector", lambda e: e.drain(), inc="DVE")
                state["epi_dve"] = C["DVE"]

            slab_load(0)
            if L == 1:
                p.ev("scalar", lambda e: e.dma_start(out=s.expb_s[0][:, :], in_=s.expb_d[0]), inc="EB", n=16)
                expb_ready[0] = C["EB"]
            p.ev("gpsimd", lambda e, L=L: e.dma_start(out=s.wqb[0][:, :], in_=s.W[f"wq{L}"][0]), inc="WQ", n=16)
            wq_ready[0] = C["WQ"]
            for kh in range(NKV):
                if kh == 0:
                    prepare_head(0, qproj_only=True)
                knorm(kh)
                if kh + 1 < NKV:
                    slab_load(kh + 1)
                for g_ in range(4):
                    h = 4 * kh + g_
                    prepare_head(h, skip_qproj=(h == 0))
                    emit_attn_block(h, 0, BLOCKS[0])
                    emit_attn_block(h, 1, BLOCKS[1])
                    sm_head_end[h] = C["SM"]
                    msk_head_end[h] = C["MSK"]
                    npv_head_end[h] = C["NPV"]
                    if L == 1 and h + 1 < NH:
                        p.ev("scalar", lambda e, h=h: e.dma_start(
                            out=s.expb_s[0][:, :], in_=s.expb_d[h + 1]),
                            waits={"MSK": C["MSK"]}, inc="EB", n=16)
                        expb_ready[h + 1] = C["EB"]
                npv_kh_end[kh] = C["NPV"]
            emit_block(nc, p, self.sems, f"att{L}", self.trace)
        return

    def phase_ffng(self, L):
        nc, C, s = self.nc, self.C, self
        p = Plan(C)
        with ExitStack() as sc:
            sbuf = lambda name, shape, dt: sc.enter_context(nc.sbuf_tensor(Builder.uname(name), shape, dt))
            rt_ps = ExitStack()
            plg = rt_ps.enter_context(nc.psum_tensor(Builder.uname("ps_lg"), [8, TPC], F32))
            pt8 = rt_ps.enter_context(nc.psum_tensor(Builder.uname("ps_t8"), [128, 64], F32))
            pcb = rt_ps.enter_context(nc.psum_tensor(Builder.uname("ps_cb"), [128, TPC], F32))
            ph = rt_ps.enter_context(nc.psum_tensor(Builder.uname("ps_h"), [128, TPC], F32))
            s.aT_s = sbuf("aT_s", [128, 4096], BF16)
            s.routT_s = sbuf("routT_s", [128, 256], BF16)
            s.logit_s = sbuf("logit_s", [8, TPC], F32)
            s.v8_s = sbuf("v8_s", [128, 64], F32)
            s.m8_s = sbuf("m8_s", [128, 8], F32)
            s.nm12_s = sbuf("nm12_s", [128, 8], F32)
            s.ge_s = sbuf("ge_s", [128, 8], F32)
            s.sg_s = sbuf("sg_s", [128, 8], F32)
            s.comb_s = sbuf("comb_s", [128, 64], F32)
            s.combT_s = sbuf("combT_s", [8, TPC], F32)
            s.h_s = sbuf("h_s", [128, TPC], F32)
            s.ball_s = sbuf("ball_s", [128, 4096], BF16)
            s.wbb = [sbuf(f"wbb{i}", [128, 4096], BF16) for i in range(2)]
            s.g_s = [sbuf(f"g_s{i}", [128, TPC], F32) for i in range(2)]
            p.ev("gpsimd", lambda e, L=L: e.dma_start(out=s.routT_s[:, :], in_=s.W[f"routT{L}"][:, :]), inc="RW", n=16)
            w_rt = C["RW"]
            p.ev("gpsimd", lambda e, L=L: e.dma_start(out=s.aT_s[:, :], in_=s.W[f"aT{L}"][:, :]), inc="GP", n=16)
            w_at = C["GP"]
            p.ev("gpsimd", lambda e, L=L: e.dma_start(out=s.ball_s[:, :], in_=s.W[f"ball{L}"][:, :]), inc="W", n=16)
            w_ready = C["W"]
            p.ev("gpsimd", lambda e, L=L: e.dma_start(out=s.wbb[0][:, :], in_=s.W[f"wb{L}"][0]), inc="OW0", n=16)
            ow_ready = {0: C["OW0"]}
            for c in range(NC):
                for hf in range(2):
                    p.ev("tensor", lambda e, c=c, hf=hf: e.matmul(
                        plg[0:8, hf * 512:(hf + 1) * 512], s.routT_s[:, c * 8:(c + 1) * 8],
                        s.xn[:, c * TPC + hf * 512: c * TPC + hf * 512 + 512],
                        start=(c == 0), stop=(c == NC - 1)),
                        waits=({"RW": w_rt} if (c == 0 and hf == 0) else {}),
                        inc=("PE" if (c == NC - 1 and hf == 1) else None))
            pe_lg = C["PE"]
            for c in range(NC):
                for hf in range(2):
                    p.ev("tensor", lambda e, c=c, hf=hf: e.matmul(
                        ph[:, hf * 512:(hf + 1) * 512], s.aT_s[:, c * 128:(c + 1) * 128],
                        s.xn[:, c * TPC + hf * 512: c * TPC + hf * 512 + 512],
                        start=(c == 0), stop=(c == NC - 1)),
                        waits=({"GP": w_at} if (c == 0 and hf == 0) else {}),
                        inc=("PE" if (c == NC - 1 and hf == 1) else None))
            pe_h = C["PE"]
            p.ev("vector", lambda e: e.tensor_copy(s.logit_s[0:8, :], plg[0:8, :]),
                 waits={"PE": pe_lg}, inc="DVE")
            for tt in range(8):
                p.ev("tensor", lambda e, tt=tt: e.transpose(
                    pt8[:, tt * 8:(tt + 1) * 8], s.logit_s[0:8, tt * 128:(tt + 1) * 128], s.idt_s[0:8, 0:8]),
                    waits=({"DVE": C["DVE"]} if tt == 0 else {}), inc=("PE" if tt == 7 else None))
            p.ev("vector", lambda e: e.tensor_copy(s.v8_s[:, :], pt8[:, :]), waits={"PE": C["PE"]})
            p.ev("vector", lambda e: e.drain())
            for tt in range(8):
                p.ev("vector", lambda e, tt=tt: e.max(s.m8_s[:, :], s.v8_s[:, tt * 8:(tt + 1) * 8]))
                p.ev("vector", lambda e: e.drain())
                p.ev("vector", lambda e, tt=tt: e.tensor_scalar(
                    s.nm12_s[:, tt:tt + 1], s.m8_s[:, 0:1], s.m8_s[:, 1:2], -1.0, OP.add, OP.mult))
                p.ev("vector", lambda e: e.drain())
                p.ev("vector", lambda e, tt=tt: e.tensor_scalar(
                    s.ge_s[:, :], s.v8_s[:, tt * 8:(tt + 1) * 8], s.m8_s[:, 1:2], None, OP.is_ge))
                p.ev("vector", lambda e: e.drain(), inc="DVE")
                p.ev("scalar", lambda e, tt=tt: e.activation(
                    s.sg_s[:, :], s.v8_s[:, tt * 8:(tt + 1) * 8], AF.Sigmoid,
                    bias=s.nm12_s[:, tt:tt + 1], scale=2.0),
                    waits={"DVE": C["DVE"]}, inc="ACT")
                p.ev("vector", lambda e, tt=tt: e.tensor_tensor(
                    s.comb_s[:, tt * 8:(tt + 1) * 8], s.sg_s[:, :], s.ge_s[:, :], op=OP.mult),
                    waits={"ACT": C["ACT"]})
                p.ev("vector", lambda e: e.drain(), inc="DVE")
                p.ev("tensor", lambda e, tt=tt: e.transpose(
                    plg[0:8, tt * 128:(tt + 1) * 128], s.comb_s[:, tt * 8:(tt + 1) * 8], s.idt_s[:, :]),
                    waits={"DVE": C["DVE"]}, inc="PE")
                p.ev("vector", lambda e, tt=tt: e.tensor_copy(
                    s.combT_s[0:8, tt * 128:(tt + 1) * 128], plg[0:8, tt * 128:(tt + 1) * 128]),
                    waits={"PE": C["PE"]}, inc="DVE")
            p.ev("vector", lambda e: e.drain(), inc="DVE")
            for hf in range(2):
                p.ev("tensor", lambda e, hf=hf: e.matmul(
                    pcb[:, hf * 512:(hf + 1) * 512], s.ind8_s[0:8, :], s.combT_s[0:8, hf * 512:(hf + 1) * 512],
                    start=True, stop=True),
                    waits=({"DVE": C["DVE"]} if hf == 0 else {}), inc=("PE" if hf == 1 else None))
            p.ev("vector", lambda e: e.tensor_copy(s.h_s[:, :], ph[:, :]), waits={"PE": pe_h})
            p.ev("vector", lambda e: e.drain())
            p.ev("vector", lambda e: e.tensor_tensor(s.hw_s[:, :], s.h_s[:, :], pcb[:, :], op=OP.mult),
                 waits={"PE": C["PE"]})
            p.ev("vector", lambda e: e.drain(), inc="DVE")
            hw_dve = C["DVE"]
            rt_ps.close()
            psB0 = sc.enter_context(nc.psum_tensor(Builder.uname("ps_B0"), [128, TPC], F32))
            psB1 = sc.enter_context(nc.psum_tensor(Builder.uname("ps_B1"), [128, TPC], F32))
            psB = [psB0, psB1]
            gs_ready = {}
            act_chunk_end = {}
            pe_chunk_end = {}
            for c in range(NC):
                if c + 1 < NC:
                    w = {}
                    if c >= 1:
                        w = {"PE": pe_chunk_end[c - 1]}
                    slot = f"OW{(c + 1) % 2}"
                    p.ev("gpsimd", lambda e, c=c, L=L: e.dma_start(
                        out=s.wbb[(c + 1) % 2][:, :], in_=s.W[f"wb{L}"][c + 1]), waits=w, inc=slot, n=16)
                    ow_ready[c + 1] = C[slot]
                ww = {f"OW{c % 2}": ow_ready[c], "DVE": hw_dve}
                if c >= 2:
                    ww["ACT"] = act_chunk_end[c - 2]
                for d_ in range(NC):
                    for hf in range(2):
                        p.ev("tensor", lambda e, c=c, d_=d_, hf=hf: e.matmul(
                            psB[c % 2][:, hf * 512:(hf + 1) * 512], s.wbb[c % 2][:, d_ * 128:(d_ + 1) * 128],
                            s.xn[:, d_ * TPC + hf * 512: d_ * TPC + hf * 512 + 512],
                            start=(d_ == 0), stop=False),
                            waits=(ww if (d_ == 0 and hf == 0) else {}))
                for hf in range(2):
                    p.ev("tensor", lambda e, c=c, hf=hf, L=L: e.matmul(
                        psB[c % 2][:, hf * 512:(hf + 1) * 512], s.ball_s[:, c * 128:(c + 1) * 128],
                        s.hw_s[:, hf * 512:(hf + 1) * 512], start=False, stop=True),
                        waits=({"W": w_ready} if (c == 0 and hf == 0) else {}),
                        inc=("PE" if hf == 1 else None))
                pe_chunk_end[c] = C["PE"]
                w = {"PE": C["PE"]}
                if c >= 2:
                    w[f"GS{c % 2}"] = gs_ready[c - 2]
                p.ev("scalar", lambda e, c=c, L=L: e.activation(
                    s.g_s[c % 2][:, :], psB[c % 2][:, :], AF.Silu, bias=s.bb_s[L][:, c:c + 1]),
                    waits=w, inc="ACT")
                act_chunk_end[c] = C["ACT"]
                p.ev("sync", lambda e, c=c: e.dma_start(
                    out=self.g_buf[c * 128:(c + 1) * 128, :], in_=s.g_s[c % 2][:, :]),
                    waits={"ACT": C["ACT"]}, inc=f"GS{c % 2}", n=16)
                gs_ready[c] = C[f"GS{c % 2}"]
            self.gs_done = (C["GS0"], C["GS1"])
            p.ev("sync", None, waits={"GS0": C["GS0"], "GS1": C["GS1"]})
            emit_block(nc, p, self.sems, f"ffng{L}", self.trace)

    def phase_out(self, L, xsrc, xdst, out_gate):
        nc, C, s = self.nc, self.C, self
        p = Plan(C)
        with ExitStack() as sc:
            sbuf = lambda name, shape, dt: sc.enter_context(nc.sbuf_tensor(Builder.uname(name), shape, dt))
            psA0 = sc.enter_context(nc.psum_tensor(Builder.uname("ps_A0"), [128, TPC], F32))
            psA1 = sc.enter_context(nc.psum_tensor(Builder.uname("ps_A1"), [128, TPC], F32))
            s.xbuf = [sbuf(f"xbuf{i}", [128, TPC], F32) for i in range(2)]
            s.gld = [sbuf(f"gld{i}", [128, TPC], F32) for i in range(2)]
            s.wob = [sbuf(f"wob{i}", [128, 4096], BF16) for i in range(2)]
            psA = [psA0, psA1]
            p.ev("gpsimd", lambda e, L=L: e.dma_start(out=s.wob[0][:, :], in_=s.W[f"wo{L}"][0]), inc="OW0", n=16)
            ow_ready = {0: C["OW0"]}
            xb_ready, gl_ready, out_ready = {}, {}, {}
            pe_chunk_end, dve_chunk_end = {}, {}
            gs0, gs1 = self.gs_done
            for c in range(NC):
                if c + 1 < NC:
                    w = {}
                    if c >= 1:
                        w = {"PE": pe_chunk_end[c - 1]}
                    slot = f"OW{(c + 1) % 2}"
                    p.ev("gpsimd", lambda e, c=c, L=L: e.dma_start(
                        out=s.wob[(c + 1) % 2][:, :], in_=s.W[f"wo{L}"][c + 1]), waits=w, inc=slot, n=16)
                    ow_ready[c + 1] = C[slot]
                w = {}
                if c == 0 and out_gate:
                    w.update(out_gate)
                if c >= 2:
                    w["DVE"] = dve_chunk_end[c - 2]
                    w[f"OUT{c % 2}"] = out_ready[c - 2]
                p.ev("sync", lambda e, c=c: e.dma_start(
                    out=s.xbuf[c % 2][:, :], in_=xsrc[c * 128:(c + 1) * 128, :]),
                    waits=w, inc=f"XB{c % 2}", n=16)
                xb_ready[c] = C[f"XB{c % 2}"]
                w = {f"GS{c % 2}": gs0 if c % 2 == 0 else gs1}
                if c >= 2:
                    w["DVE"] = dve_chunk_end[c - 2]
                p.ev("sync", lambda e, c=c: e.dma_start(
                    out=s.gld[c % 2][:, :], in_=self.g_buf[c * 128:(c + 1) * 128, :]),
                    waits=w, inc=f"GL{c % 2}", n=16)
                gl_ready[c] = C[f"GL{c % 2}"]
                ww = {f"OW{c % 2}": ow_ready[c]}
                if c >= 2:
                    ww["DVE"] = dve_chunk_end[c - 2]
                for d_ in range(NC):
                    for hf in range(2):
                        p.ev("tensor", lambda e, c=c, d_=d_, hf=hf: e.matmul(
                            psA[c % 2][:, hf * 512:(hf + 1) * 512], s.wob[c % 2][:, d_ * 128:(d_ + 1) * 128],
                            s.ao[:, d_ * TPC + hf * 512: d_ * TPC + hf * 512 + 512],
                            start=(d_ == 0), stop=(d_ == NC - 1)),
                            waits=(ww if (d_ == 0 and hf == 0) else {}),
                            inc=("PE" if (d_ == NC - 1 and hf == 1) else None))
                pe_chunk_end[c] = C["PE"]
                w = {"PE": C["PE"], f"GL{c % 2}": gl_ready[c], f"XB{c % 2}": xb_ready[c]}
                p.ev("vector", lambda e, c=c: e.tensor_tensor(
                    s.gld[c % 2][:, :], psA[c % 2][:, :], s.gld[c % 2][:, :], op=OP.mult), waits=w)
                p.ev("vector", lambda e: e.drain())
                p.ev("vector", lambda e, c=c: e.tensor_tensor(
                    s.xbuf[c % 2][:, :], s.gld[c % 2][:, :], s.xbuf[c % 2][:, :], op=OP.add))
                p.ev("vector", lambda e: e.drain(), inc="DVE")
                dve_chunk_end[c] = C["DVE"]
                p.ev("sync", lambda e, c=c: e.dma_start(
                    out=xdst[c * 128:(c + 1) * 128, :], in_=s.xbuf[c % 2][:, :]),
                    waits={"DVE": C["DVE"]}, inc=f"OUT{c % 2}", n=16)
                out_ready[c] = C[f"OUT{c % 2}"]
            p.ev("sync", None, waits={"OUT0": C["OUT0"], "OUT1": C["OUT1"]})
            emit_block(nc, p, self.sems, f"out{L}", self.trace)

    def phase_final(self):
        nc, C = self.nc, self.C
        p = Plan(C)
        dram_srcs = {"x_cur": self.x_cur}
        sbuf_srcs = {"xn": getattr(self, "xn", None), "ao": getattr(self, "ao", None),
                     "hw": getattr(self, "hw_s", None)}
        for name in self.dbg:
            if name in dram_srcs:
                src = dram_srcs[name]
                dd = nc.dram_tensor(f"dbg_{name}", list(src.shape), src.dtype, kind="ExternalOutput")
                p.ev("sync", lambda e, dd=dd, src=src: e.dma_start(out=dd[:, :], in_=src[:, :]),
                     waits={"OUT0": C["OUT0"], "OUT1": C["OUT1"], "CC": C["CC"]}, inc="OUT0", n=16)
            else:
                src = sbuf_srcs[name]
                shp = [int(x) for x in src.shape]
                dd = nc.dram_tensor(f"dbg_{name}", shp, src.dtype, kind="ExternalOutput")
                p.ev("sync", lambda e, dd=dd, src=src: e.dma_start(out=dd[:, :], in_=src[:, :]),
                     waits={"OUT0": C["OUT0"], "OUT1": C["OUT1"]}, inc="OUT0", n=16)
        p.ev("sync", None, waits={"OUT0": C["OUT0"], "OUT1": C["OUT1"]})
        emit_block(nc, p, self.sems, "fin", self.trace)

    def build(self):
        self.phase_consts()
        out_gate = None
        stop = self.stop_after

        def done(tag):
            return stop == tag

        for L in range(2):
            xsrc = self.x_in if L == 0 else self.x_cur
            xdst = self.x_cur if L == 0 else self.out_fm
            self.phase_ln_kv(L, xsrc, out_gate)
            if done(f"ln{L}") or done(f"kv{L}"):
                break
            self.phase_ffng(L)
            if done(f"rout{L}"):
                break
            with ExitStack() as lscope:
                self.ao = lscope.enter_context(
                    self.nc.sbuf_tensor(Builder.uname("ao"), [128, NC * TPC], BF16))
                self.phase_att(L)
                if done(f"att{L}"):
                    break
                self.phase_out(L, xsrc, xdst, out_gate)
            if done(f"out{L}"):
                break
            out_gate = {"OUT0": self.C["OUT0"], "OUT1": self.C["OUT1"]}
        self.phase_final()
        self.ctx.close()
        return self.nc


def build_program(debug=None, stop_after=None, att_parts="full", att_kh=NKV):
    return Builder(debug, stop_after, att_parts, att_kh).build()


def kernel(**inputs):
    in_maps, col_maps = host_prepare(inputs)
    nc = build_program()
    res = run_bass_kernel_spmd(nc, in_maps, core_ids=list(range(8)))
    out = np.empty((B, S, D), dtype=np.float32)
    for core in range(8):
        b, cols_seq = col_maps[core]
        out[b, cols_seq, :] = res.results[core]["out_fm"].T
    return out


# version 16
# speedup vs baseline: 1.0349x; 1.0051x over previous
import math
from collections import defaultdict
from contextlib import ExitStack

import numpy as np
import ml_dtypes

import concourse.bass as bass
import concourse.mybir as mybir
from concourse.alu_op_type import AluOpType
from concourse.bass_utils import run_bass_kernel_spmd

F32 = mybir.dt.float32
BF16 = mybir.dt.bfloat16
AF = mybir.ActivationFunctionType
AX = mybir.AxisListType
OP = AluOpType

B, S, D = 2, 4096, 4096
NH, NKV, HD = 32, 8, 128
TPC = 1024
NC = 32
SCALE = HD ** -0.5
EPS = 1e-5
LORA_SCALING = 2.0
CH_SAFETY = 1.01

ORDER = [6, 4, 2, 0, 7, 5, 3, 1]
SLOT_OF = {m: s for s, m in enumerate(ORDER)}
BLOCKS = ([6, 4, 2, 0], [7, 5, 3, 1])


def att_sched(tiles):
    out = []
    for lg in range(tiles[0] + 1):
        act = [m for m in tiles if m >= lg]
        out.append((lg, 128 * len(act), lg in tiles, act))
    return out


def alibi_slopes(n):
    cp2 = 2 ** math.floor(math.log2(n))
    base = 2.0 ** (-8.0 / cp2)
    slopes = base ** np.arange(1, cp2 + 1, dtype=np.float32)
    if cp2 != n:
        extra_base = 2.0 ** (-4.0 / cp2)
        extra = extra_base ** np.arange(1, 1 + 2 * (n - cp2), 2, dtype=np.float32)
        slopes = np.concatenate([slopes, extra])
    return slopes.astype(np.float32)



def _tile128(a):
    Dk, Fc = a.shape[0] // 128, a.shape[1] // 128
    return np.ascontiguousarray(
        a.reshape(Dk, 128, Fc, 128).transpose(2, 1, 0, 3).reshape(Fc, 128, Dk * 128))


def _sbuf_img(a, ncol):
    Dk = a.shape[0] // 128
    assert a.shape[1] == ncol
    return np.ascontiguousarray(a.reshape(Dk, 128, ncol).transpose(1, 0, 2).reshape(128, Dk * ncol))


def host_prepare(inputs):
    f32 = np.float32

    def wq_split(w):
        return np.stack([_sbuf_img(np.ascontiguousarray(w[:, h * 128:(h + 1) * 128]), 128)
                         for h in range(NH)])

    shared = {}
    for L in range(2):
        if L == 0:
            wq = np.asarray(inputs["L0_attn_wq"], f32)
            wkv = np.concatenate([np.asarray(inputs["L0_attn_wk"], f32),
                                  np.asarray(inputs["L0_attn_wv"], f32)], axis=1)
            wo = np.asarray(inputs["L0_attn_wo"], f32)
            Wb = np.asarray(inputs["L0_ffn_W_base"], f32)
            bb = np.asarray(inputs["L0_ffn_b_base"], f32)
            Wr = np.asarray(inputs["L0_ffn_W_router"], f32)
            A = np.asarray(inputs["L0_ffn_A"], f32)
            Bm = np.asarray(inputs["L0_ffn_B"], f32)
        else:
            wqkv = np.asarray(inputs["L1_attn_w_qkv"], f32)
            wq, wkv = wqkv[:, :4096], np.ascontiguousarray(wqkv[:, 4096:6144])
            wo = np.asarray(inputs["L1_attn_w_o"], f32)
            Wb = np.asarray(inputs["L1_ffn_W_base"], f32)
            bb = np.asarray(inputs["L1_ffn_b_base"], f32)
            Wr = np.asarray(inputs["L1_ffn_W_router"], f32)
            A = np.asarray(inputs["L1_ffn_A"], f32)
            Bm = np.asarray(inputs["L1_ffn_B"], f32)
        shared[f"wq{L}"] = wq_split(wq)
        strips = [wkv[:, 0:512], wkv[:, 1024:1536], wkv[:, 512:1024], wkv[:, 1536:2048]]
        shared[f"wkv{L}"] = np.stack(
            [_sbuf_img(np.ascontiguousarray(st), 512) for st in strips])
        shared[f"wo{L}"] = _tile128(wo)
        shared[f"wb{L}"] = _tile128(np.ascontiguousarray(Wb.T))
        shared[f"aT{L}"] = _sbuf_img(np.ascontiguousarray(A.reshape(128, 4096).T), 128)
        shared[f"ball{L}"] = np.ascontiguousarray(Bm.transpose(0, 2, 1).reshape(128, 4096))
        shared[f"routT{L}"] = _sbuf_img(np.ascontiguousarray(Wr.T), 8)
        shared[f"bb{L}"] = np.ascontiguousarray(bb.reshape(32, 128).T)

    shared["idt"] = np.eye(128, dtype=f32)
    ind8 = np.zeros((8, 128), dtype=f32)
    for e in range(8):
        ind8[e, e * 16:(e + 1) * 16] = LORA_SCALING
    shared["ind8"] = ind8

    slopes = alibi_slopes(NH)
    x = np.asarray(inputs["L0_attn_x"], f32)

    in_maps, col_maps = [], []
    jj = np.arange(128)[:, None]
    ii = np.arange(128)[None, :]
    for core in range(8):
        b, r = core // 4, core % 4
        cols_seq = np.empty(TPC, dtype=np.int64)
        for s_ in range(8):
            m = ORDER[s_]
            cols_seq[s_ * 128:(s_ + 1) * 128] = 4 * (128 * m + np.arange(128)) + r
        col_maps.append((b, cols_seq))
        xs = np.ascontiguousarray(x[b].T[:, cols_seq])

        m0 = np.stack([((jj < ii) | ((jj == ii) & (rp <= r))).astype(f32) for rp in range(4)])
        mask0 = np.ascontiguousarray(m0.transpose(1, 0, 2).reshape(128, 512)).astype(ml_dtypes.bfloat16)

        dl = np.arange(8)[:, None, None, None]
        rp = np.arange(4)[None, :, None, None]
        jj4 = jj[None, None, :, :] * np.ones((1, 1, 1, 128))
        dmat = (512 * dl + 4 * (ii[None, None] - jj[None, None]) + (r - rp)).astype(np.int64)
        dmat_f = dmat.astype(f32)
        expb = np.empty((NH, 128, 4096), dtype=ml_dtypes.bfloat16)
        for h in range(NH):
            tab = np.where(dmat >= 0, np.exp(-slopes[h] * np.sqrt(np.maximum(dmat_f, 0.0))), 0.0)
            expb[h] = tab.transpose(2, 0, 1, 3).reshape(128, 4096).astype(ml_dtypes.bfloat16)

        m = {"x_in": xs, "mask0": mask0, "expb": expb}
        m.update(shared)
        in_maps.append(m)
    return in_maps, col_maps



class Plan:
    def __init__(self, cnt):
        self.cnt = cnt
        self.events = []

    def ev(self, eng, fn, waits=None, inc=None, n=1):
        self.events.append((eng, dict(waits or {}), fn, inc, n))
        if inc is not None:
            self.cnt[inc] += n

    def v(self, sem):
        return self.cnt[sem]


def emit_block(nc, plan, sems, name, trace=None):
    if trace is not None:
        trace.append(("BLOCK", name, list(plan.events)))
    with nc.Block(name, no_gpsimd_drain=True) as block:
        for eng_name in ("gpsimd", "sync", "scalar", "vector", "tensor"):
            evs = [e for e in plan.events if e[0] == eng_name]
            if not evs:
                continue

            def make(evs):
                def body(eng):
                    seen = {}
                    for _, waits, fn, inc, n in evs:
                        for semname, val in waits.items():
                            if val > 0 and seen.get(semname, -1) < val:
                                eng.wait_ge(sems[semname], val)
                                seen[semname] = val
                        instr = fn(eng) if fn is not None else None
                        if inc is not None:
                            if instr is None:
                                instr = eng.nop()
                            instr.then_inc(sems[inc], n)
                return body
            getattr(block, eng_name)(make(evs))



class Builder:
    _uid = 0

    @classmethod
    def uname(cls, name):
        cls._uid += 1
        return f"{name}_{cls._uid}"

    def __init__(self, debug=None, stop_after=None, att_parts="full", att_kh=NKV):
        self.dbg = debug or []
        self.stop_after = stop_after
        self.att_parts = att_parts
        self.att_kh = att_kh
        nc = self.nc = bass.Bass("TRN2", target_bir_lowering=False)
        self.C = defaultdict(int)

        self.x_in = nc.dram_tensor("x_in", [D, TPC], F32, kind="ExternalInput")
        self.out_fm = nc.dram_tensor("out_fm", [D, TPC], F32, kind="ExternalOutput")
        self.idt_d = nc.dram_tensor("idt", [128, 128], F32, kind="ExternalInput")
        self.ind8_d = nc.dram_tensor("ind8", [8, 128], F32, kind="ExternalInput")
        self.mask0_d = nc.dram_tensor("mask0", [128, 512], BF16, kind="ExternalInput")
        self.expb_d = nc.dram_tensor("expb", [NH, 128, 4096], BF16, kind="ExternalInput")
        self.W = {}
        for L in range(2):
            for nm, shape in [("wq", [NH, 128, 4096]), ("wkv", [4, 128, 16384]),
                              ("wo", [NC, 128, 4096]), ("wb", [NC, 128, 4096]),
                              ("aT", [128, 4096]), ("ball", [128, 4096]),
                              ("routT", [128, 256]), ("bb", [128, 32])]:
                self.W[f"{nm}{L}"] = nc.dram_tensor(f"{nm}{L}", shape, F32, kind="ExternalInput")
        self.x_cur = nc.dram_tensor("x_cur", [D, TPC], F32)
        self.g_buf = nc.dram_tensor("g_buf", [D, TPC], F32)
        self.kv_loc = [[nc.dram_tensor(f"kv_loc{L}_{s_}", [TPC, 512], BF16) for s_ in range(4)]
                       for L in range(2)]
        self.kv_full = [[nc.dram_tensor(f"kv_full{L}_{s_}", [4 * TPC, 512], BF16) for s_ in range(4)]
                        for L in range(2)]

        ctx = self.ctx = ExitStack()
        sbuf = lambda name, shape, dt: ctx.enter_context(nc.sbuf_tensor(name, shape, dt))
        semnames = (["W", "CC", "PE", "DVE", "ACT", "SM", "EXP", "MSK", "NPV",
                     "WQ", "EB", "KSQ", "RW", "GP"]
                    + [f"XB{i}" for i in range(4)]
                    + ["KVW0", "KVW1", "KST0", "KST1", "SLB0", "SLB1",
                       "OW0", "OW1", "OUT0", "OUT1", "GS0", "GS1", "GL0", "GL1"])
        self.sems = {s: ctx.enter_context(nc.semaphore(s)) for s in semnames}

        s = self
        s.sbuf = sbuf
        s.xn = sbuf("xn", [128, NC * TPC], BF16)
        s.onesb = sbuf("onesb", [128, 1], BF16)
        s.ones128 = sbuf("ones128", [128, 128], BF16)
        s.onesF = sbuf("onesF", [128, 1], F32)
        s.onesP = sbuf("onesP", [1, 128], F32)
        s.negP = sbuf("negP", [1, 128], F32)
        s.idt_s = sbuf("idt_s", [128, 128], F32)
        s.ind8_s = sbuf("ind8_s", [8, 128], F32)
        s.mask0_s = sbuf("mask0_s", [128, 512], BF16)
        s.bb_s = [sbuf(f"bb_s{L}", [128, 32], F32) for L in range(2)]
        s.hw_s = sbuf("hw_s", [128, TPC], BF16)
        s.constv = sbuf("constv", [128, 4], F32)
        self.const_vals = [EPS, math.log(SCALE * CH_SAFETY)]
        for i, v in enumerate(self.const_vals):
            nc.const_aps.register(F32, v, s.constv[:, i:i + 1])

        self.trace = []
        self.pending_negc = None
        self.pair_global = 0
        self.exp_of_pair = {}
        self.npv_pair_end = {}
        self.msk_pair_end = {}

    def phase_consts(self):
        C = self.C
        p = Plan(C)
        s = self
        p.ev("vector", lambda e: e.memset(s.onesb[:, :], 1.0))
        p.ev("vector", lambda e: e.memset(s.ones128[:, :], 1.0))
        p.ev("vector", lambda e: e.memset(s.onesF[:, :], 1.0))
        p.ev("vector", lambda e: e.memset(s.onesP[0:1, :], 1.0))
        p.ev("vector", lambda e: e.memset(s.negP[0:1, :], -1.0))
        for i, v in enumerate(self.const_vals):
            p.ev("vector", lambda e, i=i, v=v: e.memset(s.constv[:, i:i + 1], v))
        p.ev("vector", lambda e: e.drain(), inc="DVE")
        p.ev("gpsimd", lambda e: e.dma_start(out=s.idt_s[:, :], in_=s.idt_d[:, :]), inc="W", n=16)
        p.ev("gpsimd", lambda e: e.dma_start(out=s.ind8_s[:, :], in_=s.ind8_d[:, :]), inc="W", n=16)
        p.ev("gpsimd", lambda e: e.dma_start(out=s.mask0_s[:, :], in_=s.mask0_d[:, :]), inc="W", n=16)
        for L in range(2):
            p.ev("gpsimd", lambda e, L=L: e.dma_start(out=s.bb_s[L][:, :], in_=s.W[f"bb{L}"][:, :]), inc="W", n=16)
        p.ev("gpsimd", None, waits={"W": p.v("W")})
        emit_block(self.nc, p, self.sems, "consts", self.trace)

    def phase_ln_kv(self, L, xsrc, out_gate):
        nc, C, s = self.nc, self.C, self
        p = Plan(C)
        with ExitStack() as sc:
            sbuf = lambda name, shape, dt: sc.enter_context(nc.sbuf_tensor(Builder.uname(name), shape, dt))
            s.wkvs = [sbuf(f"wkvs{i}", [128, 16384], BF16) for i in range(2)]
            p.ev("gpsimd", lambda e, L=L: e.dma_start(out=s.wkvs[0][:, :], in_=s.W[f"wkv{L}"][0]), inc="KVW0", n=16)
            p.ev("gpsimd", lambda e, L=L: e.dma_start(out=s.wkvs[1][:, :], in_=s.W[f"wkv{L}"][1]), inc="KVW1", n=16)
            strip_ready = {0: ("KVW0", C["KVW0"]), 1: ("KVW1", C["KVW1"])}
            ln_ps = ExitStack()
            ps_s = ln_ps.enter_context(nc.psum_tensor(Builder.uname("ps_s"), [1, TPC], F32))
            ps_q = ln_ps.enter_context(nc.psum_tensor(Builder.uname("ps_q"), [1, TPC], F32))
            ps_A = ln_ps.enter_context(nc.psum_tensor(Builder.uname("ps_A"), [128, TPC], F32))
            ps_B = ln_ps.enter_context(nc.psum_tensor(Builder.uname("ps_B"), [128, TPC], F32))
            s.xbuf = [sbuf(f"xbuf{i}", [128, TPC], F32) for i in range(4)]
            s.xsqbuf = [sbuf(f"xsqbuf{i}", [128, TPC], F32) for i in range(2)]
            s.tmpf = [sbuf(f"tmpf{i}", [128, TPC], F32) for i in range(2)]
            st = [sbuf(f"st{i}", [1, TPC], F32) for i in range(4)]
            s.meanT, s.msqT, s.vqT, s.varT = st[0], st[1], st[2], st[3]
            s.lnvT, s.rstdT, s.mrstdT = st[1], st[2], st[3]
            s.AbcT = sbuf("AbcT", [128, TPC], F32)
            s.BbcT = sbuf("BbcT", [128, TPC], F32)
            base_PE, base_DVE, base_ACT = C["PE"], C["DVE"], C["ACT"]
            xb_ready = {}
            xn_dve = {}
            for c in range(NC):
                w = {}
                if c == 0 and out_gate:
                    w.update(out_gate)
                if c >= 4:
                    w.update({"ACT": base_ACT + (c - 3), "PE": base_PE + (c - 3)})
                p.ev("sync", lambda e, c=c: e.dma_start(
                    out=s.xbuf[c % 4][:, :], in_=xsrc[c * 128:(c + 1) * 128, :]),
                    waits=w, inc=f"XB{c % 4}", n=16)
                xb_ready[c] = C[f"XB{c % 4}"]
            for c in range(NC):
                w = {f"XB{c % 4}": xb_ready[c]}
                if c >= 2:
                    w["PE"] = base_PE + (c - 1)
                p.ev("scalar", lambda e, c=c: e.activation(s.xsqbuf[c % 2][:, :], s.xbuf[c % 4][:, :], AF.Square),
                     waits=w, inc="ACT")
            for c in range(NC):
                w = {"ACT": base_ACT + (c + 1)}
                if c >= 2:
                    w["PE"] = base_PE + (c - 1)
                for hf in range(2):
                    p.ev("tensor", lambda e, c=c, hf=hf: e.matmul(
                        ps_s[0:1, hf * 512:(hf + 1) * 512], s.onesF[:, :],
                        s.xbuf[c % 4][:, hf * 512:(hf + 1) * 512],
                        start=(c == 0), stop=(c == NC - 1)), waits=(w if hf == 0 else {}))
                    p.ev("tensor", lambda e, c=c, hf=hf: e.matmul(
                        ps_q[0:1, hf * 512:(hf + 1) * 512], s.onesF[:, :],
                        s.xsqbuf[c % 2][:, hf * 512:(hf + 1) * 512],
                        start=(c == 0), stop=(c == NC - 1)), inc=("PE" if hf == 1 else None))
            sq_act = base_ACT + NC
            p.ev("vector", lambda e: e.tensor_scalar(s.meanT[0:1, :], ps_s[0:1, :], 1.0 / D, None, OP.mult),
                 waits={"PE": p.v("PE")})
            p.ev("vector", lambda e: e.drain())
            p.ev("vector", lambda e: e.tensor_tensor(s.msqT[0:1, :], s.meanT[0:1, :], s.meanT[0:1, :], op=OP.mult))
            p.ev("vector", lambda e: e.drain())
            p.ev("vector", lambda e: e.tensor_scalar(s.vqT[0:1, :], ps_q[0:1, :], 1.0 / D, None, OP.mult))
            p.ev("vector", lambda e: e.drain())
            p.ev("vector", lambda e: e.tensor_tensor(s.varT[0:1, :], s.vqT[0:1, :], s.msqT[0:1, :], op=OP.subtract),
                 inc="DVE")
            p.ev("scalar", lambda e: e.activation(s.lnvT[0:1, :], s.varT[0:1, :], AF.Ln, bias=EPS),
                 waits={"DVE": p.v("DVE")})
            p.ev("scalar", lambda e: e.drain())
            p.ev("scalar", lambda e: e.activation(s.rstdT[0:1, :], s.lnvT[0:1, :], AF.Exp, scale=-0.5), inc="ACT")
            p.ev("vector", lambda e: e.tensor_tensor(s.mrstdT[0:1, :], s.meanT[0:1, :], s.rstdT[0:1, :], op=OP.mult),
                 waits={"ACT": p.v("ACT")})
            p.ev("vector", lambda e: e.drain(), inc="DVE")
            w = {"DVE": p.v("DVE")}
            for hf in range(2):
                p.ev("tensor", lambda e, hf=hf: e.matmul(
                    ps_A[:, hf * 512:(hf + 1) * 512], s.onesP[0:1, :], s.rstdT[0:1, hf * 512:(hf + 1) * 512],
                    start=True, stop=True), waits=(w if hf == 0 else {}))
                p.ev("tensor", lambda e, hf=hf: e.matmul(
                    ps_B[:, hf * 512:(hf + 1) * 512], s.negP[0:1, :], s.mrstdT[0:1, hf * 512:(hf + 1) * 512],
                    start=True, stop=True), inc=("PE" if hf == 1 else None))
            p.ev("vector", lambda e: e.tensor_copy(s.AbcT[:, :], ps_A[:, :]), waits={"PE": p.v("PE")})
            p.ev("vector", lambda e: e.tensor_copy(s.BbcT[:, :], ps_B[:, :]))
            p.ev("vector", lambda e: e.drain(), inc="DVE")
            dve_bc = p.v("DVE")
            for c in range(NC):
                xn_dve[c] = ("DVE", dve_bc + c + 1)
            for c in range(NC):
                w = {"DVE": dve_bc, "ACT": sq_act}
                if c >= 4:
                    sem4, val4 = xn_dve[c - 4]
                    w[sem4] = max(w.get(sem4, 0), val4)
                p.ev("sync", lambda e, c=c: e.dma_start(
                    out=s.xbuf[c % 4][:, :], in_=xsrc[c * 128:(c + 1) * 128, :]),
                    waits=w, inc=f"XB{c % 4}", n=16)
                xb_ready[c] = C[f"XB{c % 4}"]
            for c in range(NC):
                ww = {f"XB{c % 4}": xb_ready[c]}
                p.ev("vector", lambda e, c=c: e.tensor_tensor(
                    s.tmpf[c % 2][:, :], s.xbuf[c % 4][:, :], s.AbcT[:, :], op=OP.mult), waits=ww)
                p.ev("vector", lambda e: e.drain())
                p.ev("vector", lambda e, c=c: e.tensor_tensor(
                    s.xn[:, c * TPC:(c + 1) * TPC], s.tmpf[c % 2][:, :], s.BbcT[:, :], op=OP.add))
                p.ev("vector", lambda e: e.drain(), inc="DVE")
                assert xn_dve[c] == ("DVE", C["DVE"])
            ln_ps.close()
            pkv8 = [sc.enter_context(nc.psum_tensor(Builder.uname(f"ps_kv{i}"), [128, 512], F32))
                    for i in range(8)]
            s.kvst = [sbuf(f"kvst{i}", [128, 512], BF16) for i in range(2)]
            base_PE, base_DVE = C["PE"], C["DVE"]
            kst_ready = {}
            for s_ in range(4):
                if 1 <= s_ < 3:
                    w = {"PE": base_PE + 8 * s_}
                    slot = f"KVW{(s_ + 1) % 2}"
                    p.ev("gpsimd", lambda e, s_=s_: e.dma_start(
                        out=s.wkvs[(s_ + 1) % 2][:, :], in_=s.W[f"wkv{L}"][s_ + 1]), waits=w, inc=slot, n=16)
                    strip_ready[s_ + 1] = (slot, C[slot])
                if s_ == 0:
                    for c in range(NC):
                        semc, valc = xn_dve[c]
                        for tt in range(8):
                            w = {}
                            if tt == 0:
                                w = {semc: valc}
                                if c == 0:
                                    sl, val = strip_ready[0]
                                    w[sl] = val
                            p.ev("tensor", lambda e, tt=tt, c=c: e.matmul(
                                pkv8[tt][:, :], s.xn[:, c * TPC + tt * 128: c * TPC + (tt + 1) * 128],
                                s.wkvs[0][:, c * 512:(c + 1) * 512],
                                start=(c == 0), stop=(c == NC - 1)),
                                waits=w, inc=("PE" if (c == NC - 1) else None))
                    for tt in range(8):
                        it = tt
                        w2 = {"PE": base_PE + 8}
                        if it >= 2:
                            w2[f"KST{it % 2}"] = kst_ready[it - 2]
                        p.ev("vector", lambda e, tt=tt: e.tensor_copy(s.kvst[tt % 2][:, :], pkv8[tt][:, :]),
                             waits=w2, inc="DVE")
                        p.ev("sync", lambda e, tt=tt, L=L: e.dma_start(
                            out=self.kv_loc[L][0][tt * 128:(tt + 1) * 128, :],
                            in_=s.kvst[tt % 2][:, :]),
                            waits={"DVE": base_DVE + it + 1}, inc=f"KST{it % 2}", n=16)
                        kst_ready[it] = C[f"KST{it % 2}"]
                else:
                    for tt in range(8):
                        it = s_ * 8 + tt
                        for c in range(NC):
                            semc, valc = xn_dve[c]
                            w = {semc: valc}
                            if c == 0:
                                sl, val = strip_ready[s_]
                                w[sl] = val
                                if it >= 2:
                                    w["DVE"] = max(w.get("DVE", 0), base_DVE + (it - 1))
                            p.ev("tensor", lambda e, s_=s_, tt=tt, c=c: e.matmul(
                                pkv8[tt % 2][:, :], s.xn[:, c * TPC + tt * 128: c * TPC + (tt + 1) * 128],
                                s.wkvs[s_ % 2][:, c * 512:(c + 1) * 512],
                                start=(c == 0), stop=(c == NC - 1)),
                                waits=w, inc=("PE" if c == NC - 1 else None))
                        w2 = {"PE": base_PE + it + 1}
                        if it >= 2:
                            w2[f"KST{it % 2}"] = kst_ready[it - 2]
                        p.ev("vector", lambda e, tt=tt: e.tensor_copy(s.kvst[tt % 2][:, :], pkv8[tt % 2][:, :]),
                             waits=w2, inc="DVE")
                        p.ev("sync", lambda e, s_=s_, tt=tt, L=L: e.dma_start(
                            out=self.kv_loc[L][s_][tt * 128:(tt + 1) * 128, :],
                            in_=s.kvst[tt % 2][:, :]),
                            waits={"DVE": base_DVE + it + 1}, inc=f"KST{it % 2}", n=16)
                        kst_ready[it] = C[f"KST{it % 2}"]
                p.ev("gpsimd", lambda e, s_=s_, L=L: e.collective_compute(
                    "AllGather", OP.bypass, replica_groups=[[0, 1, 2, 3], [4, 5, 6, 7]],
                    ins=[self.kv_loc[L][s_].ap().opt()], outs=[self.kv_full[L][s_].ap().opt()]),
                    waits={"KST0": C["KST0"], "KST1": C["KST1"]}, inc="CC")
            emit_block(nc, p, self.sems, f"lnkv{L}", self.trace)

    def phase_att(self, L):
        nc, C, s = self.nc, self.C, self
        p = Plan(C)
        sm_head_end, msk_head_end, npv_head_end = {}, {}, {}
        npv_kh_end = {}

        with ExitStack() as sc:
            sbuf = lambda name, shape, dt: sc.enter_context(nc.sbuf_tensor(Builder.uname(name), shape, dt))
            pS0 = sc.enter_context(nc.psum_tensor(Builder.uname("ps_S0"), [128, 1024], F32))
            pS1 = sc.enter_context(nc.psum_tensor(Builder.uname("ps_S1"), [128, 1024], F32))
            pAcc = sc.enter_context(nc.psum_tensor(Builder.uname("ps_acc"), [128, 512], F32))
            pn = sc.enter_context(nc.psum_tensor(Builder.uname("ps_n"), [128, 512], F32))
            pq = sc.enter_context(nc.psum_tensor(Builder.uname("ps_pq"), [128, 512], F32))
            pnorm = sc.enter_context(nc.psum_tensor(Builder.uname("ps_norm"), [128, 512], F32))
            s.kfm = [sbuf(f"kfm{i}", [128, 4096], BF16) for i in range(2)]
            s.vsl = [sbuf(f"vsl{i}", [128, 4096], BF16) for i in range(2)]
            s.ksq = sbuf("ksq", [128, 4096], BF16)
            s.Ubuf = [sbuf(f"Ubuf{i}", [128, 1024], BF16) for i in range(2)]
            s.expb_s = [sbuf("expb_s0", [128, 4096], BF16)]
            s.wqb = [sbuf("wqb0", [128, 4096], BF16)]
            s.qh = [sbuf(f"qh{i}", [128, TPC], BF16) for i in range(2)]
            s.qsq = sbuf("qsq", [128, TPC], BF16)
            s.kaccT = sbuf("kaccT", [128, 16], F32)
            s.kmaxT = sbuf("kmaxT", [128, 2], F32)
            s.qaccT = sbuf("qaccT", [128, 2], F32)
            s.qkT = sbuf("qkT", [128, 2], F32)
            s.lnqkT = sbuf("lnqkT", [128, 1], F32)
            s.chT = sbuf("chT", [128, 1], F32)
            s.cbias = sbuf("cbias", [128, 2], F32)
            s.lnnT = sbuf("lnnT", [128, 512], F32)
            s.rbcT = sbuf("rbcT", [128, 512], F32)
            pS = [pS0, pS1]
            slab_ready = {}
            ksq_done = {}
            wq_ready = {}
            expb_ready = {}
            qh_gate = {}
            state = {"epi_dve": 0}

            cc_base = C["CC"] - 4
            def slab_load(kh):
                bufk, bufv = s.kfm[kh % 2], s.vsl[kh % 2]
                kstrip = 0 if kh < 4 else 2
                vstrip = 1 if kh < 4 else 3
                col = (kh % 4) * 128
                slot = f"SLB{kh % 2}"
                w = {"CC": cc_base + kstrip + 1}
                if kh >= 2:
                    w["NPV"] = npv_kh_end[kh - 2]
                    w["KSQ"] = ksq_done[kh - 2]
                p.ev("sync", lambda e, bufk=bufk, kstrip=kstrip, col=col, L=L: e.dma_start_transpose(
                    bufk[:, :], self.kv_full[L][kstrip][:, col:col + 128]), waits=w, inc=slot, n=16)
                p.ev("sync", None, waits={"CC": cc_base + vstrip + 1})
                for q4 in range(4):
                    p.ev("sync", lambda e, q4=q4, bufv=bufv, vstrip=vstrip, col=col, L=L: e.dma_start(
                        out=bufv[:, q4 * 1024:(q4 + 1) * 1024].rearrange("p (t d) -> p t d", t=8),
                        in_=self.kv_full[L][vstrip][q4 * 1024:(q4 + 1) * 1024, col:col + 128]
                        .rearrange("(t p) d -> p t d", p=128)), inc=slot, n=16)
                slab_ready[kh] = C[slot]

            def knorm(kh):
                p.ev("vector", lambda e, kh=kh: e.tensor_tensor(
                    s.ksq[:, :], s.kfm[kh % 2][:, :], s.kfm[kh % 2][:, :], op=OP.mult),
                    waits={f"SLB{kh % 2}": slab_ready[kh]}, inc="KSQ")
                ksq_done[kh] = C["KSQ"]
                base_dve_k = C["DVE"]
                for j8 in range(8):
                    tgt = pnorm if j8 % 2 == 0 else pq
                    w = {"KSQ": C["KSQ"]}
                    if j8 < 2:
                        w["DVE"] = base_dve_k
                        w["ACT"] = C["ACT"]
                    else:
                        w["DVE"] = base_dve_k + (j8 - 1)
                    p.ev("tensor", lambda e, j8=j8, tgt=tgt: e.matmul(
                        tgt[:, :], s.ones128[:, :], s.ksq[:, j8 * 512:(j8 + 1) * 512], start=True, stop=True),
                        waits=w, inc="PE")
                for j8 in range(8):
                    tgt = pnorm if j8 % 2 == 0 else pq
                    p.ev("vector", lambda e, j8=j8, tgt=tgt: e.reduce_max(
                        s.kaccT[:, j8:j8 + 1], tgt[:, :], axis=AX.X),
                        waits={"PE": C["PE"] - 7 + j8}, inc="DVE")
                p.ev("vector", lambda e: e.drain())
                p.ev("vector", lambda e, kh=kh: e.reduce_max(s.kmaxT[:, kh % 2:kh % 2 + 1], s.kaccT[:, 0:8], axis=AX.X))
                p.ev("vector", lambda e: e.drain(), inc="DVE")

            def prepare_head(h, qproj_only=False, skip_qproj=False):
                kh = h // 4
                if skip_qproj:
                    pass
                else:
                  for hf in range(2):
                    tgt = pq if hf == 0 else pnorm
                    for c in range(NC):
                        ww = {}
                        if c == 0:
                            ww = {"WQ": wq_ready[h], "DVE": C["DVE"], "ACT": C["ACT"]}
                            if hf == 0 and h >= 2:
                                ww["NPV"] = npv_head_end[h - 2]
                        p.ev("tensor", lambda e, hf=hf, c=c, h=h, tgt=tgt: e.matmul(
                            tgt[:, :], s.wqb[0][:, c * 128:(c + 1) * 128],
                            s.xn[:, c * TPC + hf * 512: c * TPC + hf * 512 + 512],
                            start=(c == 0), stop=(c == NC - 1)), waits=ww,
                            inc=("PE" if c == NC - 1 else None))
                    pe_now = C["PE"]
                    p.ev("vector", lambda e, hf=hf, h=h, tgt=tgt: e.tensor_copy(
                        s.qh[h % 2][:, hf * 512:(hf + 1) * 512], tgt[:, :]), waits={"PE": pe_now}, inc="DVE")
                    p.ev("scalar", lambda e, hf=hf, tgt=tgt: e.activation(
                        s.qsq[:, hf * 512:(hf + 1) * 512], tgt[:, :], AF.Square),
                        waits={"PE": pe_now, "DVE": C["DVE"]}, inc="ACT")
                if h + 1 < NH and not skip_qproj:
                    p.ev("gpsimd", lambda e, h=h, L=L: e.dma_start(
                        out=s.wqb[0][:, :], in_=s.W[f"wq{L}"][h + 1]),
                        waits={"PE": C["PE"]}, inc="WQ", n=16)
                    wq_ready[h + 1] = C["WQ"]
                qh_gate[h] = C["DVE"]
                if qproj_only:
                    return
                for hf in range(2):
                    p.ev("tensor", lambda e, hf=hf: e.matmul(
                        pnorm[:, :] if hf == 0 else pq[:, :],
                        s.ones128[:, :], s.qsq[:, hf * 512:(hf + 1) * 512], start=True, stop=True),
                        waits=({"ACT": C["ACT"], "DVE": C["DVE"]} if hf == 0 else {"ACT": C["ACT"]}), inc="PE")
                pe_q = C["PE"]
                p.ev("vector", lambda e: e.reduce_max(s.qaccT[:, 0:1], pnorm[:, :], axis=AX.X),
                     waits={"PE": pe_q - 1})
                p.ev("vector", lambda e: e.drain())
                p.ev("vector", lambda e: e.reduce_max(s.qaccT[:, 1:2], pq[:, :], axis=AX.X),
                     waits={"PE": pe_q}, inc="DVE")
                p.ev("vector", lambda e: e.drain())
                p.ev("vector", lambda e, kh=kh: e.tensor_tensor(
                    s.qkT[:, 0:1], s.qaccT[:, 0:1], s.kmaxT[:, kh % 2:kh % 2 + 1], op=OP.mult))
                p.ev("vector", lambda e: e.drain())
                p.ev("vector", lambda e, kh=kh: e.tensor_tensor(
                    s.qkT[:, 1:2], s.qaccT[:, 1:2], s.kmaxT[:, kh % 2:kh % 2 + 1], op=OP.mult))
                p.ev("vector", lambda e: e.drain())
                p.ev("vector", lambda e: e.tensor_tensor(
                    s.lnqkT[:, 0:1], s.qkT[:, 0:1], s.qkT[:, 1:2], op=OP.max))
                p.ev("vector", lambda e: e.drain(), inc="DVE")
                p.ev("scalar", lambda e: e.activation(s.chT[:, 0:1], s.lnqkT[:, 0:1], AF.Ln, bias=0.0),
                     waits={"DVE": C["DVE"]})
                p.ev("scalar", lambda e: e.drain())
                p.ev("scalar", lambda e: e.activation(
                    s.chT[:, 0:1], s.chT[:, 0:1], AF.Exp, scale=0.5,
                    bias=math.log(SCALE * CH_SAFETY)))
                p.ev("scalar", lambda e: e.drain())
                p.ev("scalar", lambda e, h=h: e.mul(s.cbias[:, h % 2:h % 2 + 1], s.chT[:, 0:1], -1.0), inc="ACT")
                p.ev("scalar", lambda e: e.drain())

            def emit_attn_block(h, bi, tiles):
                kh = h // 4
                blk_col = bi * 512
                sched = att_sched(tiles)
                units = [(lg, Wd, bnd, act, rp) for (lg, Wd, bnd, act) in sched for rp in range(4)]
                first_unit = True
                pending = None
                gate = state["epi_dve"]

                def emit_npv(arg):
                    pid, items = arg
                    for (buf_, ucol_, Wd_, T_idx_, ww_, fu_, lu_) in items:
                        p.ev("tensor", lambda e, buf=buf_, ucol=ucol_, Wd=Wd_, fu=fu_, lu=lu_: e.matmul(
                            pn[:, 0:Wd], s.ones128[:, :], s.Ubuf[buf][:, ucol:ucol + Wd],
                            start=fu, stop=lu), waits=ww_)
                        p.ev("tensor", lambda e, buf=buf_, ucol=ucol_, Wd=Wd_, T_idx=T_idx_, kh=kh,
                             fu=fu_, lu=lu_: e.matmul(
                                 pAcc[:, 0:Wd], s.vsl[kh % 2][:, T_idx * 128:(T_idx + 1) * 128],
                                 s.Ubuf[buf][:, ucol:ucol + Wd],
                                 start=fu, stop=lu), inc="NPV")
                    self.npv_pair_end[pid] = C["NPV"]

                for pu in range(0, len(units), 2):
                    pair = self.pair_global
                    self.pair_global += 1
                    buf = pair % 2
                    for slot in range(2):
                        lg, Wd, bnd, act, rp = units[pu + slot]
                        T_idx = 8 * rp + SLOT_OF[lg]
                        ww = {}
                        if pu in (0, 2) and slot == 0:
                            ww["DVE"] = qh_gate[h]
                        if pu == 0 and slot == 0 and L == 1:
                            ww["EB"] = expb_ready[h]
                        if slot == 0 and pair >= 2:
                            ww["EXP"] = self.exp_of_pair[pair - 2]
                        p.ev("tensor", lambda e, buf=buf, slot=slot, Wd=Wd, T_idx=T_idx, h=h,
                             blk_col=blk_col, kh=kh: e.matmul(
                                 pS[buf][:, slot * 512: slot * 512 + Wd],
                                 s.kfm[kh % 2][:, T_idx * 128:(T_idx + 1) * 128],
                                 s.qh[h % 2][:, blk_col:blk_col + Wd],
                                 start=True, stop=True), waits=ww, inc="SM")
                    lg, Wd, bnd, act, _ = units[pu]
                    ww = {"SM": C["SM"]}
                    if pair >= 2:
                        ww["NPV"] = self.npv_pair_end[pair - 2]
                        ww["MSK"] = self.msk_pair_end[pair - 2]
                    if Wd == 512:
                        p.ev("scalar", lambda e, buf=buf, h=h: e.activation(
                            s.Ubuf[buf][:, 0:1024], pS[buf][:, 0:1024], AF.Exp,
                            bias=s.cbias[:, h % 2:h % 2 + 1], scale=SCALE), waits=ww, inc="EXP")
                    else:
                        p.ev("scalar", lambda e, buf=buf, Wd=Wd, h=h: e.activation(
                            s.Ubuf[buf][:, :].rearrange("p (s w) -> p s w", s=2)[:, :, 0:Wd],
                            pS[buf][:, :].rearrange("p (s w) -> p s w", s=2)[:, :, 0:Wd],
                            AF.Exp, bias=s.cbias[:, h % 2:h % 2 + 1], scale=SCALE), waits=ww, inc="EXP")
                    self.exp_of_pair[pair] = C["EXP"]
                    items = []
                    for slot in range(2):
                        lg, Wd, bnd, act, rp = units[pu + slot]
                        last_unit = (pu + slot == len(units) - 1)
                        T_idx = 8 * rp + SLOT_OF[lg]
                        ucol = slot * 512
                        did_mask = False
                        if L == 0:
                            if bnd:
                                p.ev("vector", lambda e, buf=buf, ucol=ucol, Wd=Wd, rp=rp: e.tensor_tensor(
                                    s.Ubuf[buf][:, ucol + Wd - 128: ucol + Wd],
                                    s.Ubuf[buf][:, ucol + Wd - 128: ucol + Wd],
                                    s.mask0_s[:, rp * 128:(rp + 1) * 128], op=OP.mult),
                                    waits={"EXP": self.exp_of_pair[pair]}, inc="MSK")
                                did_mask = True
                        else:
                            nact = len(act)
                            dl0 = act[0] - lg
                            base_off = (dl0 * 4 + rp) * 128
                            tabap = bass.AP(s.expb_s[0].tensor if hasattr(s.expb_s[0], "tensor") else s.expb_s[0],
                                            base_off, [[4096, 128], [-1024, nact], [1, 128]])
                            uap = s.Ubuf[buf][:, ucol:ucol + Wd].rearrange("p (t n) -> p t n", n=128)
                            p.ev("vector", lambda e, uap=uap, tabap=tabap: e.tensor_tensor(
                                uap, uap, tabap, op=OP.mult),
                                waits={"EXP": self.exp_of_pair[pair]}, inc="MSK")
                            did_mask = True
                        ww = {"EXP": self.exp_of_pair[pair]}
                        if did_mask:
                            ww["MSK"] = C["MSK"]
                        if first_unit:
                            ww["DVE"] = gate
                        items.append((buf, ucol, Wd, T_idx, ww, first_unit, last_unit))
                        self.msk_pair_end[pair] = C["MSK"]
                        first_unit = False
                    if pending is not None:
                        emit_npv(pending)
                    pending = (pair, items)
                emit_npv(pending)
                p.ev("scalar", lambda e: e.activation(s.lnnT[:, :], pn[:, 0:512], AF.Ln, bias=0.0),
                     waits={"NPV": C["NPV"]})
                p.ev("scalar", lambda e: e.drain())
                p.ev("scalar", lambda e: e.activation(s.rbcT[:, :], s.lnnT[:, :], AF.Exp, scale=-1.0),
                     inc="ACT")
                p.ev("vector", lambda e, h=h, blk_col=blk_col: e.tensor_tensor(
                    s.ao[:, h * TPC + blk_col: h * TPC + blk_col + 512],
                    pAcc[:, 0:512], s.rbcT[:, :], op=OP.mult),
                    waits={"ACT": C["ACT"]})
                p.ev("vector", lambda e: e.drain(), inc="DVE")
                state["epi_dve"] = C["DVE"]

            slab_load(0)
            if L == 1:
                p.ev("scalar", lambda e: e.dma_start(out=s.expb_s[0][:, :], in_=s.expb_d[0]), inc="EB", n=16)
                expb_ready[0] = C["EB"]
            p.ev("gpsimd", lambda e, L=L: e.dma_start(out=s.wqb[0][:, :], in_=s.W[f"wq{L}"][0]), inc="WQ", n=16)
            wq_ready[0] = C["WQ"]
            for kh in range(NKV):
                if kh == 0:
                    prepare_head(0, qproj_only=True)
                knorm(kh)
                if kh + 1 < NKV:
                    slab_load(kh + 1)
                for g_ in range(4):
                    h = 4 * kh + g_
                    prepare_head(h, skip_qproj=(h == 0))
                    emit_attn_block(h, 0, BLOCKS[0])
                    emit_attn_block(h, 1, BLOCKS[1])
                    sm_head_end[h] = C["SM"]
                    msk_head_end[h] = C["MSK"]
                    npv_head_end[h] = C["NPV"]
                    if L == 1 and h + 1 < NH:
                        p.ev("scalar", lambda e, h=h: e.dma_start(
                            out=s.expb_s[0][:, :], in_=s.expb_d[h + 1]),
                            waits={"MSK": C["MSK"]}, inc="EB", n=16)
                        expb_ready[h + 1] = C["EB"]
                npv_kh_end[kh] = C["NPV"]
            emit_block(nc, p, self.sems, f"att{L}", self.trace)
        return

    def phase_ffng(self, L):
        nc, C, s = self.nc, self.C, self
        p = Plan(C)
        with ExitStack() as sc:
            sbuf = lambda name, shape, dt: sc.enter_context(nc.sbuf_tensor(Builder.uname(name), shape, dt))
            rt_ps = ExitStack()
            plg = rt_ps.enter_context(nc.psum_tensor(Builder.uname("ps_lg"), [8, TPC], F32))
            pt8 = rt_ps.enter_context(nc.psum_tensor(Builder.uname("ps_t8"), [128, 64], F32))
            pcb = rt_ps.enter_context(nc.psum_tensor(Builder.uname("ps_cb"), [128, TPC], F32))
            ph = rt_ps.enter_context(nc.psum_tensor(Builder.uname("ps_h"), [128, TPC], F32))
            s.aT_s = sbuf("aT_s", [128, 4096], BF16)
            s.routT_s = sbuf("routT_s", [128, 256], BF16)
            s.logit_s = sbuf("logit_s", [8, TPC], F32)
            s.v8_s = sbuf("v8_s", [128, 64], F32)
            s.m8_s = sbuf("m8_s", [128, 8], F32)
            s.nm12_s = sbuf("nm12_s", [128, 8], F32)
            s.ge_s = sbuf("ge_s", [128, 8], F32)
            s.sg_s = sbuf("sg_s", [128, 8], F32)
            s.comb_s = sbuf("comb_s", [128, 64], F32)
            s.combT_s = sbuf("combT_s", [8, TPC], F32)
            s.h_s = sbuf("h_s", [128, TPC], F32)
            s.ball_s = sbuf("ball_s", [128, 4096], BF16)
            s.wbb = [sbuf(f"wbb{i}", [128, 4096], BF16) for i in range(2)]
            s.g_s = [sbuf(f"g_s{i}", [128, TPC], F32) for i in range(2)]
            p.ev("gpsimd", lambda e, L=L: e.dma_start(out=s.routT_s[:, :], in_=s.W[f"routT{L}"][:, :]), inc="RW", n=16)
            w_rt = C["RW"]
            p.ev("gpsimd", lambda e, L=L: e.dma_start(out=s.aT_s[:, :], in_=s.W[f"aT{L}"][:, :]), inc="GP", n=16)
            w_at = C["GP"]
            p.ev("gpsimd", lambda e, L=L: e.dma_start(out=s.ball_s[:, :], in_=s.W[f"ball{L}"][:, :]), inc="W", n=16)
            w_ready = C["W"]
            p.ev("gpsimd", lambda e, L=L: e.dma_start(out=s.wbb[0][:, :], in_=s.W[f"wb{L}"][0]), inc="OW0", n=16)
            ow_ready = {0: C["OW0"]}
            for c in range(NC):
                for hf in range(2):
                    p.ev("tensor", lambda e, c=c, hf=hf: e.matmul(
                        plg[0:8, hf * 512:(hf + 1) * 512], s.routT_s[:, c * 8:(c + 1) * 8],
                        s.xn[:, c * TPC + hf * 512: c * TPC + hf * 512 + 512],
                        start=(c == 0), stop=(c == NC - 1)),
                        waits=({"RW": w_rt} if (c == 0 and hf == 0) else {}),
                        inc=("PE" if (c == NC - 1 and hf == 1) else None))
            pe_lg = C["PE"]
            for c in range(NC):
                for hf in range(2):
                    p.ev("tensor", lambda e, c=c, hf=hf: e.matmul(
                        ph[:, hf * 512:(hf + 1) * 512], s.aT_s[:, c * 128:(c + 1) * 128],
                        s.xn[:, c * TPC + hf * 512: c * TPC + hf * 512 + 512],
                        start=(c == 0), stop=(c == NC - 1)),
                        waits=({"GP": w_at} if (c == 0 and hf == 0) else {}),
                        inc=("PE" if (c == NC - 1 and hf == 1) else None))
            pe_h = C["PE"]
            p.ev("vector", lambda e: e.tensor_copy(s.logit_s[0:8, :], plg[0:8, :]),
                 waits={"PE": pe_lg}, inc="DVE")
            for tt in range(8):
                p.ev("tensor", lambda e, tt=tt: e.transpose(
                    pt8[:, tt * 8:(tt + 1) * 8], s.logit_s[0:8, tt * 128:(tt + 1) * 128], s.idt_s[0:8, 0:8]),
                    waits=({"DVE": C["DVE"]} if tt == 0 else {}), inc=("PE" if tt == 7 else None))
            p.ev("vector", lambda e: e.tensor_copy(s.v8_s[:, :], pt8[:, :]), waits={"PE": C["PE"]})
            p.ev("vector", lambda e: e.drain())
            for tt in range(8):
                p.ev("vector", lambda e, tt=tt: e.max(s.m8_s[:, :], s.v8_s[:, tt * 8:(tt + 1) * 8]))
                p.ev("vector", lambda e: e.drain())
                p.ev("vector", lambda e, tt=tt: e.tensor_scalar(
                    s.nm12_s[:, tt:tt + 1], s.m8_s[:, 0:1], s.m8_s[:, 1:2], -1.0, OP.add, OP.mult))
                p.ev("vector", lambda e: e.drain())
                p.ev("vector", lambda e, tt=tt: e.tensor_scalar(
                    s.ge_s[:, :], s.v8_s[:, tt * 8:(tt + 1) * 8], s.m8_s[:, 1:2], None, OP.is_ge))
                p.ev("vector", lambda e: e.drain(), inc="DVE")
                p.ev("scalar", lambda e, tt=tt: e.activation(
                    s.sg_s[:, :], s.v8_s[:, tt * 8:(tt + 1) * 8], AF.Sigmoid,
                    bias=s.nm12_s[:, tt:tt + 1], scale=2.0),
                    waits={"DVE": C["DVE"]}, inc="ACT")
                p.ev("vector", lambda e, tt=tt: e.tensor_tensor(
                    s.comb_s[:, tt * 8:(tt + 1) * 8], s.sg_s[:, :], s.ge_s[:, :], op=OP.mult),
                    waits={"ACT": C["ACT"]})
                p.ev("vector", lambda e: e.drain(), inc="DVE")
                p.ev("tensor", lambda e, tt=tt: e.transpose(
                    plg[0:8, tt * 128:(tt + 1) * 128], s.comb_s[:, tt * 8:(tt + 1) * 8], s.idt_s[:, :]),
                    waits={"DVE": C["DVE"]}, inc="PE")
                p.ev("vector", lambda e, tt=tt: e.tensor_copy(
                    s.combT_s[0:8, tt * 128:(tt + 1) * 128], plg[0:8, tt * 128:(tt + 1) * 128]),
                    waits={"PE": C["PE"]}, inc="DVE")
            p.ev("vector", lambda e: e.drain(), inc="DVE")
            for hf in range(2):
                p.ev("tensor", lambda e, hf=hf: e.matmul(
                    pcb[:, hf * 512:(hf + 1) * 512], s.ind8_s[0:8, :], s.combT_s[0:8, hf * 512:(hf + 1) * 512],
                    start=True, stop=True),
                    waits=({"DVE": C["DVE"]} if hf == 0 else {}), inc=("PE" if hf == 1 else None))
            p.ev("vector", lambda e: e.tensor_copy(s.h_s[:, :], ph[:, :]), waits={"PE": pe_h})
            p.ev("vector", lambda e: e.drain())
            p.ev("vector", lambda e: e.tensor_tensor(s.hw_s[:, :], s.h_s[:, :], pcb[:, :], op=OP.mult),
                 waits={"PE": C["PE"]})
            p.ev("vector", lambda e: e.drain(), inc="DVE")
            hw_dve = C["DVE"]
            rt_ps.close()
            psB0 = sc.enter_context(nc.psum_tensor(Builder.uname("ps_B0"), [128, TPC], F32))
            psB1 = sc.enter_context(nc.psum_tensor(Builder.uname("ps_B1"), [128, TPC], F32))
            psB = [psB0, psB1]
            gs_ready = {}
            act_chunk_end = {}
            pe_chunk_end = {}
            for c in range(NC):
                if c + 1 < NC:
                    w = {}
                    if c >= 1:
                        w = {"PE": pe_chunk_end[c - 1]}
                    slot = f"OW{(c + 1) % 2}"
                    p.ev("gpsimd", lambda e, c=c, L=L: e.dma_start(
                        out=s.wbb[(c + 1) % 2][:, :], in_=s.W[f"wb{L}"][c + 1]), waits=w, inc=slot, n=16)
                    ow_ready[c + 1] = C[slot]
                ww = {f"OW{c % 2}": ow_ready[c], "DVE": hw_dve}
                if c >= 2:
                    ww["ACT"] = act_chunk_end[c - 2]
                for d_ in range(NC):
                    for hf in range(2):
                        p.ev("tensor", lambda e, c=c, d_=d_, hf=hf: e.matmul(
                            psB[c % 2][:, hf * 512:(hf + 1) * 512], s.wbb[c % 2][:, d_ * 128:(d_ + 1) * 128],
                            s.xn[:, d_ * TPC + hf * 512: d_ * TPC + hf * 512 + 512],
                            start=(d_ == 0), stop=False),
                            waits=(ww if (d_ == 0 and hf == 0) else {}))
                for hf in range(2):
                    p.ev("tensor", lambda e, c=c, hf=hf, L=L: e.matmul(
                        psB[c % 2][:, hf * 512:(hf + 1) * 512], s.ball_s[:, c * 128:(c + 1) * 128],
                        s.hw_s[:, hf * 512:(hf + 1) * 512], start=False, stop=True),
                        waits=({"W": w_ready} if (c == 0 and hf == 0) else {}),
                        inc=("PE" if hf == 1 else None))
                pe_chunk_end[c] = C["PE"]
                w = {"PE": C["PE"]}
                if c >= 2:
                    w[f"GS{c % 2}"] = gs_ready[c - 2]
                p.ev("scalar", lambda e, c=c, L=L: e.activation(
                    s.g_s[c % 2][:, :], psB[c % 2][:, :], AF.Silu, bias=s.bb_s[L][:, c:c + 1]),
                    waits=w, inc="ACT")
                act_chunk_end[c] = C["ACT"]
                p.ev("sync", lambda e, c=c: e.dma_start(
                    out=self.g_buf[c * 128:(c + 1) * 128, :], in_=s.g_s[c % 2][:, :]),
                    waits={"ACT": C["ACT"]}, inc=f"GS{c % 2}", n=16)
                gs_ready[c] = C[f"GS{c % 2}"]
            self.gs_done = (C["GS0"], C["GS1"])
            p.ev("sync", None, waits={"GS0": C["GS0"], "GS1": C["GS1"]})
            emit_block(nc, p, self.sems, f"ffng{L}", self.trace)

    def phase_out(self, L, xsrc, xdst, out_gate):
        nc, C, s = self.nc, self.C, self
        p = Plan(C)
        with ExitStack() as sc:
            sbuf = lambda name, shape, dt: sc.enter_context(nc.sbuf_tensor(Builder.uname(name), shape, dt))
            psA0 = sc.enter_context(nc.psum_tensor(Builder.uname("ps_A0"), [128, TPC], F32))
            psA1 = sc.enter_context(nc.psum_tensor(Builder.uname("ps_A1"), [128, TPC], F32))
            s.xbuf = [sbuf(f"xbuf{i}", [128, TPC], F32) for i in range(2)]
            s.gld = [sbuf(f"gld{i}", [128, TPC], F32) for i in range(2)]
            s.wob = [sbuf(f"wob{i}", [128, 4096], BF16) for i in range(2)]
            psA = [psA0, psA1]
            p.ev("gpsimd", lambda e, L=L: e.dma_start(out=s.wob[0][:, :], in_=s.W[f"wo{L}"][0]), inc="OW0", n=16)
            ow_ready = {0: C["OW0"]}
            xb_ready, gl_ready, out_ready = {}, {}, {}
            pe_chunk_end, dve_chunk_end = {}, {}
            gs0, gs1 = self.gs_done
            for c in range(NC):
                if c + 1 < NC:
                    w = {}
                    if c >= 1:
                        w = {"PE": pe_chunk_end[c - 1]}
                    slot = f"OW{(c + 1) % 2}"
                    p.ev("gpsimd", lambda e, c=c, L=L: e.dma_start(
                        out=s.wob[(c + 1) % 2][:, :], in_=s.W[f"wo{L}"][c + 1]), waits=w, inc=slot, n=16)
                    ow_ready[c + 1] = C[slot]
                w = {}
                if c == 0 and out_gate:
                    w.update(out_gate)
                if c >= 2:
                    w["DVE"] = dve_chunk_end[c - 2]
                    w[f"OUT{c % 2}"] = out_ready[c - 2]
                p.ev("sync", lambda e, c=c: e.dma_start(
                    out=s.xbuf[c % 2][:, :], in_=xsrc[c * 128:(c + 1) * 128, :]),
                    waits=w, inc=f"XB{c % 2}", n=16)
                xb_ready[c] = C[f"XB{c % 2}"]
                w = {f"GS{c % 2}": gs0 if c % 2 == 0 else gs1}
                if c >= 2:
                    w["DVE"] = dve_chunk_end[c - 2]
                p.ev("sync", lambda e, c=c: e.dma_start(
                    out=s.gld[c % 2][:, :], in_=self.g_buf[c * 128:(c + 1) * 128, :]),
                    waits=w, inc=f"GL{c % 2}", n=16)
                gl_ready[c] = C[f"GL{c % 2}"]
                ww = {f"OW{c % 2}": ow_ready[c]}
                if c >= 2:
                    ww["DVE"] = dve_chunk_end[c - 2]
                for d_ in range(NC):
                    for hf in range(2):
                        p.ev("tensor", lambda e, c=c, d_=d_, hf=hf: e.matmul(
                            psA[c % 2][:, hf * 512:(hf + 1) * 512], s.wob[c % 2][:, d_ * 128:(d_ + 1) * 128],
                            s.ao[:, d_ * TPC + hf * 512: d_ * TPC + hf * 512 + 512],
                            start=(d_ == 0), stop=(d_ == NC - 1)),
                            waits=(ww if (d_ == 0 and hf == 0) else {}),
                            inc=("PE" if (d_ == NC - 1 and hf == 1) else None))
                pe_chunk_end[c] = C["PE"]
                w = {"PE": C["PE"], f"GL{c % 2}": gl_ready[c], f"XB{c % 2}": xb_ready[c]}
                p.ev("vector", lambda e, c=c: e.tensor_tensor(
                    s.gld[c % 2][:, :], psA[c % 2][:, :], s.gld[c % 2][:, :], op=OP.mult), waits=w)
                p.ev("vector", lambda e: e.drain())
                p.ev("vector", lambda e, c=c: e.tensor_tensor(
                    s.xbuf[c % 2][:, :], s.gld[c % 2][:, :], s.xbuf[c % 2][:, :], op=OP.add))
                p.ev("vector", lambda e: e.drain(), inc="DVE")
                dve_chunk_end[c] = C["DVE"]
                p.ev("sync", lambda e, c=c: e.dma_start(
                    out=xdst[c * 128:(c + 1) * 128, :], in_=s.xbuf[c % 2][:, :]),
                    waits={"DVE": C["DVE"]}, inc=f"OUT{c % 2}", n=16)
                out_ready[c] = C[f"OUT{c % 2}"]
            p.ev("sync", None, waits={"OUT0": C["OUT0"], "OUT1": C["OUT1"]})
            emit_block(nc, p, self.sems, f"out{L}", self.trace)

    def phase_final(self):
        nc, C = self.nc, self.C
        p = Plan(C)
        dram_srcs = {"x_cur": self.x_cur}
        sbuf_srcs = {"xn": getattr(self, "xn", None), "ao": getattr(self, "ao", None),
                     "hw": getattr(self, "hw_s", None)}
        for name in self.dbg:
            if name in dram_srcs:
                src = dram_srcs[name]
                dd = nc.dram_tensor(f"dbg_{name}", list(src.shape), src.dtype, kind="ExternalOutput")
                p.ev("sync", lambda e, dd=dd, src=src: e.dma_start(out=dd[:, :], in_=src[:, :]),
                     waits={"OUT0": C["OUT0"], "OUT1": C["OUT1"], "CC": C["CC"]}, inc="OUT0", n=16)
            else:
                src = sbuf_srcs[name]
                shp = [int(x) for x in src.shape]
                dd = nc.dram_tensor(f"dbg_{name}", shp, src.dtype, kind="ExternalOutput")
                p.ev("sync", lambda e, dd=dd, src=src: e.dma_start(out=dd[:, :], in_=src[:, :]),
                     waits={"OUT0": C["OUT0"], "OUT1": C["OUT1"]}, inc="OUT0", n=16)
        p.ev("sync", None, waits={"OUT0": C["OUT0"], "OUT1": C["OUT1"]})
        emit_block(nc, p, self.sems, "fin", self.trace)

    def build(self):
        self.phase_consts()
        out_gate = None
        stop = self.stop_after

        def done(tag):
            return stop == tag

        for L in range(2):
            xsrc = self.x_in if L == 0 else self.x_cur
            xdst = self.x_cur if L == 0 else self.out_fm
            self.phase_ln_kv(L, xsrc, out_gate)
            if done(f"ln{L}") or done(f"kv{L}"):
                break
            self.phase_ffng(L)
            if done(f"rout{L}"):
                break
            with ExitStack() as lscope:
                self.ao = lscope.enter_context(
                    self.nc.sbuf_tensor(Builder.uname("ao"), [128, NC * TPC], BF16))
                self.phase_att(L)
                if done(f"att{L}"):
                    break
                self.phase_out(L, xsrc, xdst, out_gate)
            if done(f"out{L}"):
                break
            out_gate = {"OUT0": self.C["OUT0"], "OUT1": self.C["OUT1"]}
        self.phase_final()
        self.ctx.close()
        return self.nc


def build_program(debug=None, stop_after=None, att_parts="full", att_kh=NKV):
    return Builder(debug, stop_after, att_parts, att_kh).build()


def kernel(**inputs):
    in_maps, col_maps = host_prepare(inputs)
    nc = build_program()
    res = run_bass_kernel_spmd(nc, in_maps, core_ids=list(range(8)))
    out = np.empty((B, S, D), dtype=np.float32)
    for core in range(8):
        b, cols_seq = col_maps[core]
        out[b, cols_seq, :] = res.results[core]["out_fm"].T
    return out


# version 17
# speedup vs baseline: 1.0406x; 1.0055x over previous
import math
from collections import defaultdict
from contextlib import ExitStack

import numpy as np
import ml_dtypes

import concourse.bass as bass
import concourse.mybir as mybir
from concourse.alu_op_type import AluOpType
from concourse.bass_utils import run_bass_kernel_spmd

F32 = mybir.dt.float32
BF16 = mybir.dt.bfloat16
AF = mybir.ActivationFunctionType
AX = mybir.AxisListType
OP = AluOpType

B, S, D = 2, 4096, 4096
NH, NKV, HD = 32, 8, 128
TPC = 1024
NC = 32
SCALE = HD ** -0.5
EPS = 1e-5
LORA_SCALING = 2.0
CH_SAFETY = 1.01

ORDER = [6, 4, 2, 0, 7, 5, 3, 1]
SLOT_OF = {m: s for s, m in enumerate(ORDER)}
BLOCKS = ([6, 4, 2, 0], [7, 5, 3, 1])


def att_sched(tiles):
    out = []
    for lg in range(tiles[0] + 1):
        act = [m for m in tiles if m >= lg]
        out.append((lg, 128 * len(act), lg in tiles, act))
    return out


def alibi_slopes(n):
    cp2 = 2 ** math.floor(math.log2(n))
    base = 2.0 ** (-8.0 / cp2)
    slopes = base ** np.arange(1, cp2 + 1, dtype=np.float32)
    if cp2 != n:
        extra_base = 2.0 ** (-4.0 / cp2)
        extra = extra_base ** np.arange(1, 1 + 2 * (n - cp2), 2, dtype=np.float32)
        slopes = np.concatenate([slopes, extra])
    return slopes.astype(np.float32)



def _tile128(a):
    Dk, Fc = a.shape[0] // 128, a.shape[1] // 128
    return np.ascontiguousarray(
        a.reshape(Dk, 128, Fc, 128).transpose(2, 1, 0, 3).reshape(Fc, 128, Dk * 128))


def _sbuf_img(a, ncol):
    Dk = a.shape[0] // 128
    assert a.shape[1] == ncol
    return np.ascontiguousarray(a.reshape(Dk, 128, ncol).transpose(1, 0, 2).reshape(128, Dk * ncol))


def host_prepare(inputs):
    f32 = np.float32

    def wq_split(w):
        return np.stack([_sbuf_img(np.ascontiguousarray(w[:, h * 128:(h + 1) * 128]), 128)
                         for h in range(NH)])

    shared = {}
    for L in range(2):
        if L == 0:
            wq = np.asarray(inputs["L0_attn_wq"], f32)
            wkv = np.concatenate([np.asarray(inputs["L0_attn_wk"], f32),
                                  np.asarray(inputs["L0_attn_wv"], f32)], axis=1)
            wo = np.asarray(inputs["L0_attn_wo"], f32)
            Wb = np.asarray(inputs["L0_ffn_W_base"], f32)
            bb = np.asarray(inputs["L0_ffn_b_base"], f32)
            Wr = np.asarray(inputs["L0_ffn_W_router"], f32)
            A = np.asarray(inputs["L0_ffn_A"], f32)
            Bm = np.asarray(inputs["L0_ffn_B"], f32)
        else:
            wqkv = np.asarray(inputs["L1_attn_w_qkv"], f32)
            wq, wkv = wqkv[:, :4096], np.ascontiguousarray(wqkv[:, 4096:6144])
            wo = np.asarray(inputs["L1_attn_w_o"], f32)
            Wb = np.asarray(inputs["L1_ffn_W_base"], f32)
            bb = np.asarray(inputs["L1_ffn_b_base"], f32)
            Wr = np.asarray(inputs["L1_ffn_W_router"], f32)
            A = np.asarray(inputs["L1_ffn_A"], f32)
            Bm = np.asarray(inputs["L1_ffn_B"], f32)
        shared[f"wq{L}"] = wq_split(wq)
        strips = [wkv[:, 0:512], wkv[:, 1024:1536], wkv[:, 512:1024], wkv[:, 1536:2048]]
        shared[f"wkv{L}"] = np.stack(
            [_sbuf_img(np.ascontiguousarray(st), 512) for st in strips])
        shared[f"wo{L}"] = _tile128(wo)
        shared[f"wb{L}"] = _tile128(np.ascontiguousarray(Wb.T))
        shared[f"aT{L}"] = _sbuf_img(np.ascontiguousarray(A.reshape(128, 4096).T), 128)
        shared[f"ball{L}"] = np.ascontiguousarray(Bm.transpose(0, 2, 1).reshape(128, 4096))
        shared[f"routT{L}"] = _sbuf_img(np.ascontiguousarray(Wr.T), 8)
        shared[f"bb{L}"] = np.ascontiguousarray(bb.reshape(32, 128).T)

    shared["idt"] = np.eye(128, dtype=f32)
    ind8 = np.zeros((8, 128), dtype=f32)
    for e in range(8):
        ind8[e, e * 16:(e + 1) * 16] = LORA_SCALING
    shared["ind8"] = ind8

    slopes = alibi_slopes(NH)
    x = np.asarray(inputs["L0_attn_x"], f32)

    in_maps, col_maps = [], []
    jj = np.arange(128)[:, None]
    ii = np.arange(128)[None, :]
    for core in range(8):
        b, r = core // 4, core % 4
        cols_seq = np.empty(TPC, dtype=np.int64)
        for s_ in range(8):
            m = ORDER[s_]
            cols_seq[s_ * 128:(s_ + 1) * 128] = 4 * (128 * m + np.arange(128)) + r
        col_maps.append((b, cols_seq))
        xs = np.ascontiguousarray(x[b].T[:, cols_seq])

        m0 = np.stack([((jj < ii) | ((jj == ii) & (rp <= r))).astype(f32) for rp in range(4)])
        mask0 = np.ascontiguousarray(m0.transpose(1, 0, 2).reshape(128, 512)).astype(ml_dtypes.bfloat16)

        dl = np.arange(8)[:, None, None, None]
        rp = np.arange(4)[None, :, None, None]
        jj4 = jj[None, None, :, :] * np.ones((1, 1, 1, 128))
        dmat = (512 * dl + 4 * (ii[None, None] - jj[None, None]) + (r - rp)).astype(np.int64)
        dmat_f = dmat.astype(f32)
        expb = np.empty((NH, 128, 4096), dtype=ml_dtypes.bfloat16)
        for h in range(NH):
            tab = np.where(dmat >= 0, np.exp(-slopes[h] * np.sqrt(np.maximum(dmat_f, 0.0))), 0.0)
            expb[h] = tab.transpose(2, 0, 1, 3).reshape(128, 4096).astype(ml_dtypes.bfloat16)

        m = {"x_in": xs, "mask0": mask0, "expb": expb}
        m.update(shared)
        in_maps.append(m)
    return in_maps, col_maps



class Plan:
    def __init__(self, cnt):
        self.cnt = cnt
        self.events = []

    def ev(self, eng, fn, waits=None, inc=None, n=1):
        self.events.append((eng, dict(waits or {}), fn, inc, n))
        if inc is not None:
            self.cnt[inc] += n

    def v(self, sem):
        return self.cnt[sem]


def emit_block(nc, plan, sems, name, trace=None):
    if trace is not None:
        trace.append(("BLOCK", name, list(plan.events)))
    with nc.Block(name, no_gpsimd_drain=True) as block:
        for eng_name in ("gpsimd", "sync", "scalar", "vector", "tensor"):
            evs = [e for e in plan.events if e[0] == eng_name]
            if not evs:
                continue

            def make(evs):
                def body(eng):
                    seen = {}
                    for _, waits, fn, inc, n in evs:
                        for semname, val in waits.items():
                            if val > 0 and seen.get(semname, -1) < val:
                                eng.wait_ge(sems[semname], val)
                                seen[semname] = val
                        instr = fn(eng) if fn is not None else None
                        if inc is not None:
                            if instr is None:
                                instr = eng.nop()
                            instr.then_inc(sems[inc], n)
                return body
            getattr(block, eng_name)(make(evs))



class Builder:
    _uid = 0

    @classmethod
    def uname(cls, name):
        cls._uid += 1
        return f"{name}_{cls._uid}"

    def __init__(self, debug=None, stop_after=None, att_parts="full", att_kh=NKV):
        self.dbg = debug or []
        self.stop_after = stop_after
        self.att_parts = att_parts
        self.att_kh = att_kh
        nc = self.nc = bass.Bass("TRN2", target_bir_lowering=False)
        self.C = defaultdict(int)

        self.x_in = nc.dram_tensor("x_in", [D, TPC], F32, kind="ExternalInput")
        self.out_fm = nc.dram_tensor("out_fm", [D, TPC], F32, kind="ExternalOutput")
        self.idt_d = nc.dram_tensor("idt", [128, 128], F32, kind="ExternalInput")
        self.ind8_d = nc.dram_tensor("ind8", [8, 128], F32, kind="ExternalInput")
        self.mask0_d = nc.dram_tensor("mask0", [128, 512], BF16, kind="ExternalInput")
        self.expb_d = nc.dram_tensor("expb", [NH, 128, 4096], BF16, kind="ExternalInput")
        self.W = {}
        for L in range(2):
            for nm, shape in [("wq", [NH, 128, 4096]), ("wkv", [4, 128, 16384]),
                              ("wo", [NC, 128, 4096]), ("wb", [NC, 128, 4096]),
                              ("aT", [128, 4096]), ("ball", [128, 4096]),
                              ("routT", [128, 256]), ("bb", [128, 32])]:
                self.W[f"{nm}{L}"] = nc.dram_tensor(f"{nm}{L}", shape, F32, kind="ExternalInput")
        self.x_cur = nc.dram_tensor("x_cur", [D, TPC], F32)
        self.g_buf = nc.dram_tensor("g_buf", [D, TPC], F32)
        self.kv_loc = [[nc.dram_tensor(f"kv_loc{L}_{s_}", [TPC, 512], BF16) for s_ in range(4)]
                       for L in range(2)]
        self.kv_full = [[nc.dram_tensor(f"kv_full{L}_{s_}", [4 * TPC, 512], BF16) for s_ in range(4)]
                        for L in range(2)]

        ctx = self.ctx = ExitStack()
        sbuf = lambda name, shape, dt: ctx.enter_context(nc.sbuf_tensor(name, shape, dt))
        semnames = (["W", "CC", "PE", "DVE", "ACT", "SM", "EXP", "MSK", "NPV",
                     "WQ", "EB", "KSQ", "RW", "GP"]
                    + [f"XB{i}" for i in range(4)]
                    + ["KVW0", "KVW1", "KST0", "KST1", "SLB0", "SLB1",
                       "OW0", "OW1", "OUT0", "OUT1", "GS0", "GS1", "GL0", "GL1"])
        self.sems = {s: ctx.enter_context(nc.semaphore(s)) for s in semnames}

        s = self
        s.sbuf = sbuf
        s.xn = sbuf("xn", [128, NC * TPC], BF16)
        s.onesb = sbuf("onesb", [128, 1], BF16)
        s.ones128 = sbuf("ones128", [128, 128], BF16)
        s.onesF = sbuf("onesF", [128, 1], F32)
        s.onesP = sbuf("onesP", [1, 128], F32)
        s.negP = sbuf("negP", [1, 128], F32)
        s.idt_s = sbuf("idt_s", [128, 128], F32)
        s.ind8_s = sbuf("ind8_s", [8, 128], F32)
        s.mask0_s = sbuf("mask0_s", [128, 512], BF16)
        s.bb_s = [sbuf(f"bb_s{L}", [128, 32], F32) for L in range(2)]
        s.hw_s = sbuf("hw_s", [128, TPC], BF16)
        s.constv = sbuf("constv", [128, 4], F32)
        self.const_vals = [EPS, math.log(SCALE * CH_SAFETY)]
        for i, v in enumerate(self.const_vals):
            nc.const_aps.register(F32, v, s.constv[:, i:i + 1])

        self.trace = []
        self.pending_negc = None
        self.pair_global = 0
        self.exp_of_pair = {}
        self.npv_pair_end = {}
        self.msk_pair_end = {}

    def phase_consts(self):
        C = self.C
        p = Plan(C)
        s = self
        p.ev("vector", lambda e: e.memset(s.onesb[:, :], 1.0))
        p.ev("vector", lambda e: e.memset(s.ones128[:, :], 1.0))
        p.ev("vector", lambda e: e.memset(s.onesF[:, :], 1.0))
        p.ev("vector", lambda e: e.memset(s.onesP[0:1, :], 1.0))
        p.ev("vector", lambda e: e.memset(s.negP[0:1, :], -1.0))
        for i, v in enumerate(self.const_vals):
            p.ev("vector", lambda e, i=i, v=v: e.memset(s.constv[:, i:i + 1], v))
        p.ev("vector", lambda e: e.drain(), inc="DVE")
        p.ev("gpsimd", lambda e: e.dma_start(out=s.idt_s[:, :], in_=s.idt_d[:, :]), inc="W", n=16)
        p.ev("gpsimd", lambda e: e.dma_start(out=s.ind8_s[:, :], in_=s.ind8_d[:, :]), inc="W", n=16)
        p.ev("gpsimd", lambda e: e.dma_start(out=s.mask0_s[:, :], in_=s.mask0_d[:, :]), inc="W", n=16)
        for L in range(2):
            p.ev("gpsimd", lambda e, L=L: e.dma_start(out=s.bb_s[L][:, :], in_=s.W[f"bb{L}"][:, :]), inc="W", n=16)
        p.ev("gpsimd", None, waits={"W": p.v("W")})
        emit_block(self.nc, p, self.sems, "consts", self.trace)

    def phase_ln_kv(self, L, xsrc, out_gate):
        nc, C, s = self.nc, self.C, self
        p = Plan(C)
        with ExitStack() as sc:
            sbuf = lambda name, shape, dt: sc.enter_context(nc.sbuf_tensor(Builder.uname(name), shape, dt))
            s.wkvs = [sbuf(f"wkvs{i}", [128, 16384], BF16) for i in range(2)]
            p.ev("gpsimd", lambda e, L=L: e.dma_start(out=s.wkvs[0][:, :], in_=s.W[f"wkv{L}"][0]), inc="KVW0", n=16)
            p.ev("gpsimd", lambda e, L=L: e.dma_start(out=s.wkvs[1][:, :], in_=s.W[f"wkv{L}"][1]), inc="KVW1", n=16)
            strip_ready = {0: ("KVW0", C["KVW0"]), 1: ("KVW1", C["KVW1"])}
            ln_ps = ExitStack()
            ps_s = ln_ps.enter_context(nc.psum_tensor(Builder.uname("ps_s"), [1, TPC], F32))
            ps_q = ln_ps.enter_context(nc.psum_tensor(Builder.uname("ps_q"), [1, TPC], F32))
            ps_A = ln_ps.enter_context(nc.psum_tensor(Builder.uname("ps_A"), [128, TPC], F32))
            ps_B = ln_ps.enter_context(nc.psum_tensor(Builder.uname("ps_B"), [128, TPC], F32))
            s.xbuf = [sbuf(f"xbuf{i}", [128, TPC], F32) for i in range(4)]
            s.xsqbuf = [sbuf(f"xsqbuf{i}", [128, TPC], F32) for i in range(2)]
            s.tmpf = [sbuf(f"tmpf{i}", [128, TPC], F32) for i in range(2)]
            st = [sbuf(f"st{i}", [1, TPC], F32) for i in range(4)]
            s.meanT, s.msqT, s.vqT, s.varT = st[0], st[1], st[2], st[3]
            s.lnvT, s.rstdT, s.mrstdT = st[1], st[2], st[3]
            s.AbcT = sbuf("AbcT", [128, TPC], F32)
            s.BbcT = sbuf("BbcT", [128, TPC], F32)
            base_PE, base_DVE, base_ACT = C["PE"], C["DVE"], C["ACT"]
            xb_ready = {}
            xn_dve = {}
            for c in range(NC):
                w = {}
                if c == 0 and out_gate:
                    w.update(out_gate)
                if c >= 4:
                    w.update({"ACT": base_ACT + (c - 3), "PE": base_PE + (c - 3)})
                p.ev("sync", lambda e, c=c: e.dma_start(
                    out=s.xbuf[c % 4][:, :], in_=xsrc[c * 128:(c + 1) * 128, :]),
                    waits=w, inc=f"XB{c % 4}", n=16)
                xb_ready[c] = C[f"XB{c % 4}"]
            for c in range(NC):
                w = {f"XB{c % 4}": xb_ready[c]}
                if c >= 2:
                    w["PE"] = base_PE + (c - 1)
                p.ev("scalar", lambda e, c=c: e.activation(s.xsqbuf[c % 2][:, :], s.xbuf[c % 4][:, :], AF.Square),
                     waits=w, inc="ACT")
            for c in range(NC):
                w = {"ACT": base_ACT + (c + 1)}
                if c >= 2:
                    w["PE"] = base_PE + (c - 1)
                for hf in range(2):
                    p.ev("tensor", lambda e, c=c, hf=hf: e.matmul(
                        ps_s[0:1, hf * 512:(hf + 1) * 512], s.onesF[:, :],
                        s.xbuf[c % 4][:, hf * 512:(hf + 1) * 512],
                        start=(c == 0), stop=(c == NC - 1)), waits=(w if hf == 0 else {}))
                    p.ev("tensor", lambda e, c=c, hf=hf: e.matmul(
                        ps_q[0:1, hf * 512:(hf + 1) * 512], s.onesF[:, :],
                        s.xsqbuf[c % 2][:, hf * 512:(hf + 1) * 512],
                        start=(c == 0), stop=(c == NC - 1)), inc=("PE" if hf == 1 else None))
            sq_act = base_ACT + NC
            p.ev("vector", lambda e: e.tensor_scalar(s.meanT[0:1, :], ps_s[0:1, :], 1.0 / D, None, OP.mult),
                 waits={"PE": p.v("PE")})
            p.ev("vector", lambda e: e.drain())
            p.ev("vector", lambda e: e.tensor_tensor(s.msqT[0:1, :], s.meanT[0:1, :], s.meanT[0:1, :], op=OP.mult))
            p.ev("vector", lambda e: e.drain())
            p.ev("vector", lambda e: e.tensor_scalar(s.vqT[0:1, :], ps_q[0:1, :], 1.0 / D, None, OP.mult))
            p.ev("vector", lambda e: e.drain())
            p.ev("vector", lambda e: e.tensor_tensor(s.varT[0:1, :], s.vqT[0:1, :], s.msqT[0:1, :], op=OP.subtract),
                 inc="DVE")
            p.ev("scalar", lambda e: e.activation(s.lnvT[0:1, :], s.varT[0:1, :], AF.Ln, bias=EPS),
                 waits={"DVE": p.v("DVE")})
            p.ev("scalar", lambda e: e.drain())
            p.ev("scalar", lambda e: e.activation(s.rstdT[0:1, :], s.lnvT[0:1, :], AF.Exp, scale=-0.5), inc="ACT")
            p.ev("vector", lambda e: e.tensor_tensor(s.mrstdT[0:1, :], s.meanT[0:1, :], s.rstdT[0:1, :], op=OP.mult),
                 waits={"ACT": p.v("ACT")})
            p.ev("vector", lambda e: e.drain(), inc="DVE")
            w = {"DVE": p.v("DVE")}
            for hf in range(2):
                p.ev("tensor", lambda e, hf=hf: e.matmul(
                    ps_A[:, hf * 512:(hf + 1) * 512], s.onesP[0:1, :], s.rstdT[0:1, hf * 512:(hf + 1) * 512],
                    start=True, stop=True), waits=(w if hf == 0 else {}))
                p.ev("tensor", lambda e, hf=hf: e.matmul(
                    ps_B[:, hf * 512:(hf + 1) * 512], s.negP[0:1, :], s.mrstdT[0:1, hf * 512:(hf + 1) * 512],
                    start=True, stop=True), inc=("PE" if hf == 1 else None))
            p.ev("vector", lambda e: e.tensor_copy(s.AbcT[:, :], ps_A[:, :]), waits={"PE": p.v("PE")})
            p.ev("vector", lambda e: e.tensor_copy(s.BbcT[:, :], ps_B[:, :]))
            p.ev("vector", lambda e: e.drain(), inc="DVE")
            dve_bc = p.v("DVE")
            for c in range(NC):
                xn_dve[c] = ("DVE", dve_bc + c + 1)
            for c in range(NC):
                w = {"DVE": dve_bc, "ACT": sq_act}
                if c >= 4:
                    sem4, val4 = xn_dve[c - 4]
                    w[sem4] = max(w.get(sem4, 0), val4)
                p.ev("sync", lambda e, c=c: e.dma_start(
                    out=s.xbuf[c % 4][:, :], in_=xsrc[c * 128:(c + 1) * 128, :]),
                    waits=w, inc=f"XB{c % 4}", n=16)
                xb_ready[c] = C[f"XB{c % 4}"]
            for c in range(NC):
                ww = {f"XB{c % 4}": xb_ready[c]}
                p.ev("vector", lambda e, c=c: e.tensor_tensor(
                    s.tmpf[c % 2][:, :], s.xbuf[c % 4][:, :], s.AbcT[:, :], op=OP.mult), waits=ww)
                p.ev("vector", lambda e: e.drain())
                p.ev("vector", lambda e, c=c: e.tensor_tensor(
                    s.xn[:, c * TPC:(c + 1) * TPC], s.tmpf[c % 2][:, :], s.BbcT[:, :], op=OP.add))
                p.ev("vector", lambda e: e.drain(), inc="DVE")
                assert xn_dve[c] == ("DVE", C["DVE"])
            ln_ps.close()
            pkv8 = [sc.enter_context(nc.psum_tensor(Builder.uname(f"ps_kv{i}"), [128, 512], F32))
                    for i in range(8)]
            s.kvst = [sbuf(f"kvst{i}", [128, 512], BF16) for i in range(2)]
            base_PE, base_DVE = C["PE"], C["DVE"]
            kst_ready = {}
            for s_ in range(4):
                if 1 <= s_ < 3:
                    w = {"PE": base_PE + 8 * s_}
                    slot = f"KVW{(s_ + 1) % 2}"
                    p.ev("gpsimd", lambda e, s_=s_: e.dma_start(
                        out=s.wkvs[(s_ + 1) % 2][:, :], in_=s.W[f"wkv{L}"][s_ + 1]), waits=w, inc=slot, n=16)
                    strip_ready[s_ + 1] = (slot, C[slot])
                if s_ == 0:
                    for c in range(NC):
                        semc, valc = xn_dve[c]
                        for tt in range(8):
                            w = {}
                            if tt == 0:
                                w = {semc: valc}
                                if c == 0:
                                    sl, val = strip_ready[0]
                                    w[sl] = val
                            p.ev("tensor", lambda e, tt=tt, c=c: e.matmul(
                                pkv8[tt][:, :], s.xn[:, c * TPC + tt * 128: c * TPC + (tt + 1) * 128],
                                s.wkvs[0][:, c * 512:(c + 1) * 512],
                                start=(c == 0), stop=(c == NC - 1)),
                                waits=w, inc=("PE" if (c == NC - 1) else None))
                    for tt in range(8):
                        it = tt
                        w2 = {"PE": base_PE + 8}
                        if it >= 2:
                            w2[f"KST{it % 2}"] = kst_ready[it - 2]
                        p.ev("vector", lambda e, tt=tt: e.tensor_copy(s.kvst[tt % 2][:, :], pkv8[tt][:, :]),
                             waits=w2, inc="DVE")
                        p.ev("sync", lambda e, tt=tt, L=L: e.dma_start(
                            out=self.kv_loc[L][0][tt * 128:(tt + 1) * 128, :],
                            in_=s.kvst[tt % 2][:, :]),
                            waits={"DVE": base_DVE + it + 1}, inc=f"KST{it % 2}", n=16)
                        kst_ready[it] = C[f"KST{it % 2}"]
                else:
                    for tt in range(8):
                        it = s_ * 8 + tt
                        for c in range(NC):
                            semc, valc = xn_dve[c]
                            w = {semc: valc}
                            if c == 0:
                                sl, val = strip_ready[s_]
                                w[sl] = val
                                if it >= 2:
                                    w["DVE"] = max(w.get("DVE", 0), base_DVE + (it - 1))
                            p.ev("tensor", lambda e, s_=s_, tt=tt, c=c: e.matmul(
                                pkv8[tt % 2][:, :], s.xn[:, c * TPC + tt * 128: c * TPC + (tt + 1) * 128],
                                s.wkvs[s_ % 2][:, c * 512:(c + 1) * 512],
                                start=(c == 0), stop=(c == NC - 1)),
                                waits=w, inc=("PE" if c == NC - 1 else None))
                        w2 = {"PE": base_PE + it + 1}
                        if it >= 2:
                            w2[f"KST{it % 2}"] = kst_ready[it - 2]
                        p.ev("vector", lambda e, tt=tt: e.tensor_copy(s.kvst[tt % 2][:, :], pkv8[tt % 2][:, :]),
                             waits=w2, inc="DVE")
                        p.ev("sync", lambda e, s_=s_, tt=tt, L=L: e.dma_start(
                            out=self.kv_loc[L][s_][tt * 128:(tt + 1) * 128, :],
                            in_=s.kvst[tt % 2][:, :]),
                            waits={"DVE": base_DVE + it + 1}, inc=f"KST{it % 2}", n=16)
                        kst_ready[it] = C[f"KST{it % 2}"]
                p.ev("gpsimd", lambda e, s_=s_, L=L: e.collective_compute(
                    "AllGather", OP.bypass, replica_groups=[[0, 1, 2, 3], [4, 5, 6, 7]],
                    ins=[self.kv_loc[L][s_].ap().opt()], outs=[self.kv_full[L][s_].ap().opt()]),
                    waits={"KST0": C["KST0"], "KST1": C["KST1"]}, inc="CC")
            emit_block(nc, p, self.sems, f"lnkv{L}", self.trace)

    def phase_att(self, L):
        nc, C, s = self.nc, self.C, self
        p = Plan(C)
        sm_head_end, msk_head_end, npv_head_end = {}, {}, {}
        npv_kh_end = {}

        with ExitStack() as sc:
            sbuf = lambda name, shape, dt: sc.enter_context(nc.sbuf_tensor(Builder.uname(name), shape, dt))
            pS0 = sc.enter_context(nc.psum_tensor(Builder.uname("ps_S0"), [128, 1024], F32))
            pS1 = sc.enter_context(nc.psum_tensor(Builder.uname("ps_S1"), [128, 1024], F32))
            pS2 = sc.enter_context(nc.psum_tensor(Builder.uname("ps_S2"), [128, 1024], F32))
            pAcc = sc.enter_context(nc.psum_tensor(Builder.uname("ps_acc"), [128, 512], F32))
            pn = sc.enter_context(nc.psum_tensor(Builder.uname("ps_n"), [128, 512], F32))
            pq = pn
            pnorm = pAcc
            s.kfm = [sbuf(f"kfm{i}", [128, 4096], BF16) for i in range(2)]
            s.vsl = [sbuf(f"vsl{i}", [128, 4096], BF16) for i in range(2)]
            s.ksq = sbuf("ksq", [128, 4096], BF16)
            s.Ubuf = [sbuf(f"Ubuf{i}", [128, 1024], BF16) for i in range(3)]
            s.expb_s = [sbuf("expb_s0", [128, 4096], BF16)]
            s.wqb = [sbuf("wqb0", [128, 4096], BF16)]
            s.qh = [sbuf(f"qh{i}", [128, TPC], BF16) for i in range(2)]
            s.qsq = sbuf("qsq", [128, TPC], BF16)
            s.kaccT = sbuf("kaccT", [128, 16], F32)
            s.kmaxT = sbuf("kmaxT", [128, 2], F32)
            s.qaccT = sbuf("qaccT", [128, 2], F32)
            s.qkT = sbuf("qkT", [128, 2], F32)
            s.lnqkT = sbuf("lnqkT", [128, 1], F32)
            s.chT = sbuf("chT", [128, 1], F32)
            s.cbias = sbuf("cbias", [128, 2], F32)
            s.lnnT = sbuf("lnnT", [128, 512], F32)
            s.rbcT = sbuf("rbcT", [128, 512], F32)
            pS = [pS0, pS1, pS2]
            slab_ready = {}
            ksq_done = {}
            wq_ready = {}
            expb_ready = {}
            qh_gate = {}
            qn_dve = {}
            state = {"epi_dve": 0}

            cc_base = C["CC"] - 4
            def slab_load(kh):
                bufk, bufv = s.kfm[kh % 2], s.vsl[kh % 2]
                kstrip = 0 if kh < 4 else 2
                vstrip = 1 if kh < 4 else 3
                col = (kh % 4) * 128
                slot = f"SLB{kh % 2}"
                w = {"CC": cc_base + kstrip + 1}
                if kh >= 2:
                    w["NPV"] = npv_kh_end[kh - 2]
                    w["KSQ"] = ksq_done[kh - 2]
                p.ev("sync", lambda e, bufk=bufk, kstrip=kstrip, col=col, L=L: e.dma_start_transpose(
                    bufk[:, :], self.kv_full[L][kstrip][:, col:col + 128]), waits=w, inc=slot, n=16)
                p.ev("sync", None, waits={"CC": cc_base + vstrip + 1})
                for q4 in range(4):
                    p.ev("sync", lambda e, q4=q4, bufv=bufv, vstrip=vstrip, col=col, L=L: e.dma_start(
                        out=bufv[:, q4 * 1024:(q4 + 1) * 1024].rearrange("p (t d) -> p t d", t=8),
                        in_=self.kv_full[L][vstrip][q4 * 1024:(q4 + 1) * 1024, col:col + 128]
                        .rearrange("(t p) d -> p t d", p=128)), inc=slot, n=16)
                slab_ready[kh] = C[slot]

            def knorm(kh):
                p.ev("vector", lambda e, kh=kh: e.tensor_tensor(
                    s.ksq[:, :], s.kfm[kh % 2][:, :], s.kfm[kh % 2][:, :], op=OP.mult),
                    waits={f"SLB{kh % 2}": slab_ready[kh]}, inc="KSQ")
                ksq_done[kh] = C["KSQ"]
                base_dve_k = C["DVE"]
                for j8 in range(8):
                    tgt = pnorm if j8 % 2 == 0 else pq
                    w = {"KSQ": C["KSQ"]}
                    if j8 < 2:
                        w["DVE"] = base_dve_k
                        w["ACT"] = C["ACT"]
                    else:
                        w["DVE"] = base_dve_k + (j8 - 1)
                    p.ev("tensor", lambda e, j8=j8, tgt=tgt: e.matmul(
                        tgt[:, :], s.ones128[:, :], s.ksq[:, j8 * 512:(j8 + 1) * 512], start=True, stop=True),
                        waits=w, inc="PE")
                for j8 in range(8):
                    tgt = pnorm if j8 % 2 == 0 else pq
                    p.ev("vector", lambda e, j8=j8, tgt=tgt: e.reduce_max(
                        s.kaccT[:, j8:j8 + 1], tgt[:, :], axis=AX.X),
                        waits={"PE": C["PE"] - 7 + j8}, inc="DVE")
                p.ev("vector", lambda e: e.drain())
                p.ev("vector", lambda e, kh=kh: e.reduce_max(s.kmaxT[:, kh % 2:kh % 2 + 1], s.kaccT[:, 0:8], axis=AX.X))
                p.ev("vector", lambda e: e.drain(), inc="DVE")

            def prepare_head(h, qproj_only=False, skip_qproj=False):
                kh = h // 4
                if skip_qproj:
                    pass
                else:
                  for hf in range(2):
                    tgt = pq if hf == 0 else pnorm
                    for c in range(NC):
                        ww = {}
                        if c == 0:
                            ww = {"WQ": wq_ready[h], "DVE": C["DVE"], "ACT": C["ACT"]}
                            if hf == 0 and h >= 2:
                                ww["NPV"] = npv_head_end[h - 2]
                        p.ev("tensor", lambda e, hf=hf, c=c, h=h, tgt=tgt: e.matmul(
                            tgt[:, :], s.wqb[0][:, c * 128:(c + 1) * 128],
                            s.xn[:, c * TPC + hf * 512: c * TPC + hf * 512 + 512],
                            start=(c == 0), stop=(c == NC - 1)), waits=ww,
                            inc=("PE" if c == NC - 1 else None))
                    pe_now = C["PE"]
                    p.ev("vector", lambda e, hf=hf, h=h, tgt=tgt: e.tensor_copy(
                        s.qh[h % 2][:, hf * 512:(hf + 1) * 512], tgt[:, :]), waits={"PE": pe_now}, inc="DVE")
                    p.ev("scalar", lambda e, hf=hf, tgt=tgt: e.activation(
                        s.qsq[:, hf * 512:(hf + 1) * 512], tgt[:, :], AF.Square),
                        waits={"PE": pe_now, "DVE": C["DVE"]}, inc="ACT")
                if h + 1 < NH and not skip_qproj:
                    p.ev("gpsimd", lambda e, h=h, L=L: e.dma_start(
                        out=s.wqb[0][:, :], in_=s.W[f"wq{L}"][h + 1]),
                        waits={"PE": C["PE"]}, inc="WQ", n=16)
                    wq_ready[h + 1] = C["WQ"]
                qh_gate[h] = C["DVE"]
                if qproj_only:
                    return
                for hf in range(2):
                    p.ev("tensor", lambda e, hf=hf: e.matmul(
                        pnorm[:, :] if hf == 0 else pq[:, :],
                        s.ones128[:, :], s.qsq[:, hf * 512:(hf + 1) * 512], start=True, stop=True),
                        waits=({"ACT": C["ACT"], "DVE": C["DVE"]} if hf == 0 else {"ACT": C["ACT"]}), inc="PE")
                pe_q = C["PE"]
                p.ev("vector", lambda e: e.reduce_max(s.qaccT[:, 0:1], pnorm[:, :], axis=AX.X),
                     waits={"PE": pe_q - 1})
                p.ev("vector", lambda e: e.drain())
                p.ev("vector", lambda e: e.reduce_max(s.qaccT[:, 1:2], pq[:, :], axis=AX.X),
                     waits={"PE": pe_q}, inc="DVE")
                p.ev("vector", lambda e: e.drain())
                p.ev("vector", lambda e, kh=kh: e.tensor_tensor(
                    s.qkT[:, 0:1], s.qaccT[:, 0:1], s.kmaxT[:, kh % 2:kh % 2 + 1], op=OP.mult))
                p.ev("vector", lambda e: e.drain())
                p.ev("vector", lambda e, kh=kh: e.tensor_tensor(
                    s.qkT[:, 1:2], s.qaccT[:, 1:2], s.kmaxT[:, kh % 2:kh % 2 + 1], op=OP.mult))
                p.ev("vector", lambda e: e.drain())
                p.ev("vector", lambda e: e.tensor_tensor(
                    s.lnqkT[:, 0:1], s.qkT[:, 0:1], s.qkT[:, 1:2], op=OP.max))
                p.ev("vector", lambda e: e.drain(), inc="DVE")
                qn_dve[h] = C["DVE"]
                p.ev("scalar", lambda e: e.activation(s.chT[:, 0:1], s.lnqkT[:, 0:1], AF.Ln, bias=0.0),
                     waits={"DVE": C["DVE"]})
                p.ev("scalar", lambda e: e.drain())
                p.ev("scalar", lambda e: e.activation(
                    s.chT[:, 0:1], s.chT[:, 0:1], AF.Exp, scale=0.5,
                    bias=math.log(SCALE * CH_SAFETY)))
                p.ev("scalar", lambda e: e.drain())
                p.ev("scalar", lambda e, h=h: e.mul(s.cbias[:, h % 2:h % 2 + 1], s.chT[:, 0:1], -1.0), inc="ACT")
                p.ev("scalar", lambda e: e.drain())

            def emit_attn_block(h, bi, tiles):
                kh = h // 4
                blk_col = bi * 512
                sched = att_sched(tiles)
                units = [(lg, Wd, bnd, act, rp) for (lg, Wd, bnd, act) in sched for rp in range(4)]
                first_unit = True
                pending = None
                gate = max(state["epi_dve"], qn_dve.get(h, 0))

                def emit_npv(arg):
                    pid, items = arg
                    for (buf_, ucol_, Wd_, T_idx_, ww_, fu_, lu_) in items:
                        p.ev("tensor", lambda e, buf=buf_, ucol=ucol_, Wd=Wd_, fu=fu_, lu=lu_: e.matmul(
                            pn[:, 0:Wd], s.ones128[:, :], s.Ubuf[buf][:, ucol:ucol + Wd],
                            start=fu, stop=lu), waits=ww_)
                        p.ev("tensor", lambda e, buf=buf_, ucol=ucol_, Wd=Wd_, T_idx=T_idx_, kh=kh,
                             fu=fu_, lu=lu_: e.matmul(
                                 pAcc[:, 0:Wd], s.vsl[kh % 2][:, T_idx * 128:(T_idx + 1) * 128],
                                 s.Ubuf[buf][:, ucol:ucol + Wd],
                                 start=fu, stop=lu), inc="NPV")
                    self.npv_pair_end[pid] = C["NPV"]

                for pu in range(0, len(units), 2):
                    pair = self.pair_global
                    self.pair_global += 1
                    buf = pair % 3
                    for slot in range(2):
                        lg, Wd, bnd, act, rp = units[pu + slot]
                        T_idx = 8 * rp + SLOT_OF[lg]
                        ww = {}
                        if pu in (0, 2) and slot == 0:
                            ww["DVE"] = qh_gate[h]
                        if pu == 0 and slot == 0 and L == 1:
                            ww["EB"] = expb_ready[h]
                        if slot == 0 and pair >= 3:
                            ww["EXP"] = self.exp_of_pair[pair - 3]
                        p.ev("tensor", lambda e, buf=buf, slot=slot, Wd=Wd, T_idx=T_idx, h=h,
                             blk_col=blk_col, kh=kh: e.matmul(
                                 pS[buf][:, slot * 512: slot * 512 + Wd],
                                 s.kfm[kh % 2][:, T_idx * 128:(T_idx + 1) * 128],
                                 s.qh[h % 2][:, blk_col:blk_col + Wd],
                                 start=True, stop=True), waits=ww, inc="SM")
                    lg, Wd, bnd, act, _ = units[pu]
                    ww = {"SM": C["SM"]}
                    if pair >= 3:
                        ww["NPV"] = self.npv_pair_end[pair - 3]
                        ww["MSK"] = self.msk_pair_end[pair - 3]
                    if Wd == 512:
                        p.ev("scalar", lambda e, buf=buf, h=h: e.activation(
                            s.Ubuf[buf][:, 0:1024], pS[buf][:, 0:1024], AF.Exp,
                            bias=s.cbias[:, h % 2:h % 2 + 1], scale=SCALE), waits=ww, inc="EXP")
                    else:
                        p.ev("scalar", lambda e, buf=buf, Wd=Wd, h=h: e.activation(
                            s.Ubuf[buf][:, :].rearrange("p (s w) -> p s w", s=2)[:, :, 0:Wd],
                            pS[buf][:, :].rearrange("p (s w) -> p s w", s=2)[:, :, 0:Wd],
                            AF.Exp, bias=s.cbias[:, h % 2:h % 2 + 1], scale=SCALE), waits=ww, inc="EXP")
                    self.exp_of_pair[pair] = C["EXP"]
                    items = []
                    for slot in range(2):
                        lg, Wd, bnd, act, rp = units[pu + slot]
                        last_unit = (pu + slot == len(units) - 1)
                        T_idx = 8 * rp + SLOT_OF[lg]
                        ucol = slot * 512
                        did_mask = False
                        if L == 0:
                            if bnd:
                                p.ev("vector", lambda e, buf=buf, ucol=ucol, Wd=Wd, rp=rp: e.tensor_tensor(
                                    s.Ubuf[buf][:, ucol + Wd - 128: ucol + Wd],
                                    s.Ubuf[buf][:, ucol + Wd - 128: ucol + Wd],
                                    s.mask0_s[:, rp * 128:(rp + 1) * 128], op=OP.mult),
                                    waits={"EXP": self.exp_of_pair[pair]}, inc="MSK")
                                did_mask = True
                        else:
                            nact = len(act)
                            dl0 = act[0] - lg
                            base_off = (dl0 * 4 + rp) * 128
                            tabap = bass.AP(s.expb_s[0].tensor if hasattr(s.expb_s[0], "tensor") else s.expb_s[0],
                                            base_off, [[4096, 128], [-1024, nact], [1, 128]])
                            uap = s.Ubuf[buf][:, ucol:ucol + Wd].rearrange("p (t n) -> p t n", n=128)
                            p.ev("vector", lambda e, uap=uap, tabap=tabap: e.tensor_tensor(
                                uap, uap, tabap, op=OP.mult),
                                waits={"EXP": self.exp_of_pair[pair]}, inc="MSK")
                            did_mask = True
                        ww = {"EXP": self.exp_of_pair[pair]}
                        if did_mask:
                            ww["MSK"] = C["MSK"]
                        if first_unit:
                            ww["DVE"] = gate
                        items.append((buf, ucol, Wd, T_idx, ww, first_unit, last_unit))
                        self.msk_pair_end[pair] = C["MSK"]
                        first_unit = False
                    if pending is not None:
                        emit_npv(pending)
                    pending = (pair, items)
                emit_npv(pending)
                p.ev("scalar", lambda e: e.activation(s.lnnT[:, :], pn[:, 0:512], AF.Ln, bias=0.0),
                     waits={"NPV": C["NPV"]})
                p.ev("scalar", lambda e: e.drain())
                p.ev("scalar", lambda e: e.activation(s.rbcT[:, :], s.lnnT[:, :], AF.Exp, scale=-1.0),
                     inc="ACT")
                p.ev("vector", lambda e, h=h, blk_col=blk_col: e.tensor_tensor(
                    s.ao[:, h * TPC + blk_col: h * TPC + blk_col + 512],
                    pAcc[:, 0:512], s.rbcT[:, :], op=OP.mult),
                    waits={"ACT": C["ACT"]})
                p.ev("vector", lambda e: e.drain(), inc="DVE")
                state["epi_dve"] = C["DVE"]

            slab_load(0)
            if L == 1:
                p.ev("scalar", lambda e: e.dma_start(out=s.expb_s[0][:, :], in_=s.expb_d[0]), inc="EB", n=16)
                expb_ready[0] = C["EB"]
            p.ev("gpsimd", lambda e, L=L: e.dma_start(out=s.wqb[0][:, :], in_=s.W[f"wq{L}"][0]), inc="WQ", n=16)
            wq_ready[0] = C["WQ"]
            for kh in range(NKV):
                if kh == 0:
                    prepare_head(0, qproj_only=True)
                knorm(kh)
                if kh + 1 < NKV:
                    slab_load(kh + 1)
                for g_ in range(4):
                    h = 4 * kh + g_
                    prepare_head(h, skip_qproj=(h == 0))
                    emit_attn_block(h, 0, BLOCKS[0])
                    emit_attn_block(h, 1, BLOCKS[1])
                    sm_head_end[h] = C["SM"]
                    msk_head_end[h] = C["MSK"]
                    npv_head_end[h] = C["NPV"]
                    if L == 1 and h + 1 < NH:
                        p.ev("scalar", lambda e, h=h: e.dma_start(
                            out=s.expb_s[0][:, :], in_=s.expb_d[h + 1]),
                            waits={"MSK": C["MSK"]}, inc="EB", n=16)
                        expb_ready[h + 1] = C["EB"]
                npv_kh_end[kh] = C["NPV"]
            emit_block(nc, p, self.sems, f"att{L}", self.trace)
        return

    def phase_ffng(self, L):
        nc, C, s = self.nc, self.C, self
        p = Plan(C)
        with ExitStack() as sc:
            sbuf = lambda name, shape, dt: sc.enter_context(nc.sbuf_tensor(Builder.uname(name), shape, dt))
            rt_ps = ExitStack()
            plg = rt_ps.enter_context(nc.psum_tensor(Builder.uname("ps_lg"), [8, TPC], F32))
            pt8 = rt_ps.enter_context(nc.psum_tensor(Builder.uname("ps_t8"), [128, 64], F32))
            pcb = rt_ps.enter_context(nc.psum_tensor(Builder.uname("ps_cb"), [128, TPC], F32))
            ph = rt_ps.enter_context(nc.psum_tensor(Builder.uname("ps_h"), [128, TPC], F32))
            s.aT_s = sbuf("aT_s", [128, 4096], BF16)
            s.routT_s = sbuf("routT_s", [128, 256], BF16)
            s.logit_s = sbuf("logit_s", [8, TPC], F32)
            s.v8_s = sbuf("v8_s", [128, 64], F32)
            s.m8_s = sbuf("m8_s", [128, 8], F32)
            s.nm12_s = sbuf("nm12_s", [128, 8], F32)
            s.ge_s = sbuf("ge_s", [128, 8], F32)
            s.sg_s = sbuf("sg_s", [128, 8], F32)
            s.comb_s = sbuf("comb_s", [128, 64], F32)
            s.combT_s = sbuf("combT_s", [8, TPC], F32)
            s.h_s = sbuf("h_s", [128, TPC], F32)
            s.ball_s = sbuf("ball_s", [128, 4096], BF16)
            s.wbb = [sbuf(f"wbb{i}", [128, 4096], BF16) for i in range(2)]
            s.g_s = [sbuf(f"g_s{i}", [128, TPC], F32) for i in range(2)]
            p.ev("gpsimd", lambda e, L=L: e.dma_start(out=s.routT_s[:, :], in_=s.W[f"routT{L}"][:, :]), inc="RW", n=16)
            w_rt = C["RW"]
            p.ev("gpsimd", lambda e, L=L: e.dma_start(out=s.aT_s[:, :], in_=s.W[f"aT{L}"][:, :]), inc="GP", n=16)
            w_at = C["GP"]
            p.ev("gpsimd", lambda e, L=L: e.dma_start(out=s.ball_s[:, :], in_=s.W[f"ball{L}"][:, :]), inc="W", n=16)
            w_ready = C["W"]
            p.ev("gpsimd", lambda e, L=L: e.dma_start(out=s.wbb[0][:, :], in_=s.W[f"wb{L}"][0]), inc="OW0", n=16)
            ow_ready = {0: C["OW0"]}
            for c in range(NC):
                for hf in range(2):
                    p.ev("tensor", lambda e, c=c, hf=hf: e.matmul(
                        plg[0:8, hf * 512:(hf + 1) * 512], s.routT_s[:, c * 8:(c + 1) * 8],
                        s.xn[:, c * TPC + hf * 512: c * TPC + hf * 512 + 512],
                        start=(c == 0), stop=(c == NC - 1)),
                        waits=({"RW": w_rt} if (c == 0 and hf == 0) else {}),
                        inc=("PE" if (c == NC - 1 and hf == 1) else None))
            pe_lg = C["PE"]
            for c in range(NC):
                for hf in range(2):
                    p.ev("tensor", lambda e, c=c, hf=hf: e.matmul(
                        ph[:, hf * 512:(hf + 1) * 512], s.aT_s[:, c * 128:(c + 1) * 128],
                        s.xn[:, c * TPC + hf * 512: c * TPC + hf * 512 + 512],
                        start=(c == 0), stop=(c == NC - 1)),
                        waits=({"GP": w_at} if (c == 0 and hf == 0) else {}),
                        inc=("PE" if (c == NC - 1 and hf == 1) else None))
            pe_h = C["PE"]
            p.ev("vector", lambda e: e.tensor_copy(s.logit_s[0:8, :], plg[0:8, :]),
                 waits={"PE": pe_lg}, inc="DVE")
            for tt in range(8):
                p.ev("tensor", lambda e, tt=tt: e.transpose(
                    pt8[:, tt * 8:(tt + 1) * 8], s.logit_s[0:8, tt * 128:(tt + 1) * 128], s.idt_s[0:8, 0:8]),
                    waits=({"DVE": C["DVE"]} if tt == 0 else {}), inc=("PE" if tt == 7 else None))
            p.ev("vector", lambda e: e.tensor_copy(s.v8_s[:, :], pt8[:, :]), waits={"PE": C["PE"]})
            p.ev("vector", lambda e: e.drain())
            for tt in range(8):
                p.ev("vector", lambda e, tt=tt: e.max(s.m8_s[:, :], s.v8_s[:, tt * 8:(tt + 1) * 8]))
                p.ev("vector", lambda e: e.drain())
                p.ev("vector", lambda e, tt=tt: e.tensor_scalar(
                    s.nm12_s[:, tt:tt + 1], s.m8_s[:, 0:1], s.m8_s[:, 1:2], -1.0, OP.add, OP.mult))
                p.ev("vector", lambda e: e.drain())
                p.ev("vector", lambda e, tt=tt: e.tensor_scalar(
                    s.ge_s[:, :], s.v8_s[:, tt * 8:(tt + 1) * 8], s.m8_s[:, 1:2], None, OP.is_ge))
                p.ev("vector", lambda e: e.drain(), inc="DVE")
                p.ev("scalar", lambda e, tt=tt: e.activation(
                    s.sg_s[:, :], s.v8_s[:, tt * 8:(tt + 1) * 8], AF.Sigmoid,
                    bias=s.nm12_s[:, tt:tt + 1], scale=2.0),
                    waits={"DVE": C["DVE"]}, inc="ACT")
                p.ev("vector", lambda e, tt=tt: e.tensor_tensor(
                    s.comb_s[:, tt * 8:(tt + 1) * 8], s.sg_s[:, :], s.ge_s[:, :], op=OP.mult),
                    waits={"ACT": C["ACT"]})
                p.ev("vector", lambda e: e.drain(), inc="DVE")
                p.ev("tensor", lambda e, tt=tt: e.transpose(
                    plg[0:8, tt * 128:(tt + 1) * 128], s.comb_s[:, tt * 8:(tt + 1) * 8], s.idt_s[:, :]),
                    waits={"DVE": C["DVE"]}, inc="PE")
                p.ev("vector", lambda e, tt=tt: e.tensor_copy(
                    s.combT_s[0:8, tt * 128:(tt + 1) * 128], plg[0:8, tt * 128:(tt + 1) * 128]),
                    waits={"PE": C["PE"]}, inc="DVE")
            p.ev("vector", lambda e: e.drain(), inc="DVE")
            for hf in range(2):
                p.ev("tensor", lambda e, hf=hf: e.matmul(
                    pcb[:, hf * 512:(hf + 1) * 512], s.ind8_s[0:8, :], s.combT_s[0:8, hf * 512:(hf + 1) * 512],
                    start=True, stop=True),
                    waits=({"DVE": C["DVE"]} if hf == 0 else {}), inc=("PE" if hf == 1 else None))
            p.ev("vector", lambda e: e.tensor_copy(s.h_s[:, :], ph[:, :]), waits={"PE": pe_h})
            p.ev("vector", lambda e: e.drain())
            p.ev("vector", lambda e: e.tensor_tensor(s.hw_s[:, :], s.h_s[:, :], pcb[:, :], op=OP.mult),
                 waits={"PE": C["PE"]})
            p.ev("vector", lambda e: e.drain(), inc="DVE")
            hw_dve = C["DVE"]
            rt_ps.close()
            psB0 = sc.enter_context(nc.psum_tensor(Builder.uname("ps_B0"), [128, TPC], F32))
            psB1 = sc.enter_context(nc.psum_tensor(Builder.uname("ps_B1"), [128, TPC], F32))
            psB = [psB0, psB1]
            gs_ready = {}
            act_chunk_end = {}
            pe_chunk_end = {}
            for c in range(NC):
                if c + 1 < NC:
                    w = {}
                    if c >= 1:
                        w = {"PE": pe_chunk_end[c - 1]}
                    slot = f"OW{(c + 1) % 2}"
                    p.ev("gpsimd", lambda e, c=c, L=L: e.dma_start(
                        out=s.wbb[(c + 1) % 2][:, :], in_=s.W[f"wb{L}"][c + 1]), waits=w, inc=slot, n=16)
                    ow_ready[c + 1] = C[slot]
                ww = {f"OW{c % 2}": ow_ready[c], "DVE": hw_dve}
                if c >= 2:
                    ww["ACT"] = act_chunk_end[c - 2]
                for d_ in range(NC):
                    for hf in range(2):
                        p.ev("tensor", lambda e, c=c, d_=d_, hf=hf: e.matmul(
                            psB[c % 2][:, hf * 512:(hf + 1) * 512], s.wbb[c % 2][:, d_ * 128:(d_ + 1) * 128],
                            s.xn[:, d_ * TPC + hf * 512: d_ * TPC + hf * 512 + 512],
                            start=(d_ == 0), stop=False),
                            waits=(ww if (d_ == 0 and hf == 0) else {}))
                for hf in range(2):
                    p.ev("tensor", lambda e, c=c, hf=hf, L=L: e.matmul(
                        psB[c % 2][:, hf * 512:(hf + 1) * 512], s.ball_s[:, c * 128:(c + 1) * 128],
                        s.hw_s[:, hf * 512:(hf + 1) * 512], start=False, stop=True),
                        waits=({"W": w_ready} if (c == 0 and hf == 0) else {}),
                        inc=("PE" if hf == 1 else None))
                pe_chunk_end[c] = C["PE"]
                w = {"PE": C["PE"]}
                if c >= 2:
                    w[f"GS{c % 2}"] = gs_ready[c - 2]
                p.ev("scalar", lambda e, c=c, L=L: e.activation(
                    s.g_s[c % 2][:, :], psB[c % 2][:, :], AF.Silu, bias=s.bb_s[L][:, c:c + 1]),
                    waits=w, inc="ACT")
                act_chunk_end[c] = C["ACT"]
                p.ev("sync", lambda e, c=c: e.dma_start(
                    out=self.g_buf[c * 128:(c + 1) * 128, :], in_=s.g_s[c % 2][:, :]),
                    waits={"ACT": C["ACT"]}, inc=f"GS{c % 2}", n=16)
                gs_ready[c] = C[f"GS{c % 2}"]
            self.gs_done = (C["GS0"], C["GS1"])
            p.ev("sync", None, waits={"GS0": C["GS0"], "GS1": C["GS1"]})
            emit_block(nc, p, self.sems, f"ffng{L}", self.trace)

    def phase_out(self, L, xsrc, xdst, out_gate):
        nc, C, s = self.nc, self.C, self
        p = Plan(C)
        with ExitStack() as sc:
            sbuf = lambda name, shape, dt: sc.enter_context(nc.sbuf_tensor(Builder.uname(name), shape, dt))
            psA0 = sc.enter_context(nc.psum_tensor(Builder.uname("ps_A0"), [128, TPC], F32))
            psA1 = sc.enter_context(nc.psum_tensor(Builder.uname("ps_A1"), [128, TPC], F32))
            s.xbuf = [sbuf(f"xbuf{i}", [128, TPC], F32) for i in range(2)]
            s.gld = [sbuf(f"gld{i}", [128, TPC], F32) for i in range(2)]
            s.wob = [sbuf(f"wob{i}", [128, 4096], BF16) for i in range(2)]
            psA = [psA0, psA1]
            p.ev("gpsimd", lambda e, L=L: e.dma_start(out=s.wob[0][:, :], in_=s.W[f"wo{L}"][0]), inc="OW0", n=16)
            ow_ready = {0: C["OW0"]}
            xb_ready, gl_ready, out_ready = {}, {}, {}
            pe_chunk_end, dve_chunk_end = {}, {}
            gs0, gs1 = self.gs_done
            for c in range(NC):
                if c + 1 < NC:
                    w = {}
                    if c >= 1:
                        w = {"PE": pe_chunk_end[c - 1]}
                    slot = f"OW{(c + 1) % 2}"
                    p.ev("gpsimd", lambda e, c=c, L=L: e.dma_start(
                        out=s.wob[(c + 1) % 2][:, :], in_=s.W[f"wo{L}"][c + 1]), waits=w, inc=slot, n=16)
                    ow_ready[c + 1] = C[slot]
                w = {}
                if c == 0 and out_gate:
                    w.update(out_gate)
                if c >= 2:
                    w["DVE"] = dve_chunk_end[c - 2]
                    w[f"OUT{c % 2}"] = out_ready[c - 2]
                p.ev("sync", lambda e, c=c: e.dma_start(
                    out=s.xbuf[c % 2][:, :], in_=xsrc[c * 128:(c + 1) * 128, :]),
                    waits=w, inc=f"XB{c % 2}", n=16)
                xb_ready[c] = C[f"XB{c % 2}"]
                w = {f"GS{c % 2}": gs0 if c % 2 == 0 else gs1}
                if c >= 2:
                    w["DVE"] = dve_chunk_end[c - 2]
                p.ev("sync", lambda e, c=c: e.dma_start(
                    out=s.gld[c % 2][:, :], in_=self.g_buf[c * 128:(c + 1) * 128, :]),
                    waits=w, inc=f"GL{c % 2}", n=16)
                gl_ready[c] = C[f"GL{c % 2}"]
                ww = {f"OW{c % 2}": ow_ready[c]}
                if c >= 2:
                    ww["DVE"] = dve_chunk_end[c - 2]
                for d_ in range(NC):
                    for hf in range(2):
                        p.ev("tensor", lambda e, c=c, d_=d_, hf=hf: e.matmul(
                            psA[c % 2][:, hf * 512:(hf + 1) * 512], s.wob[c % 2][:, d_ * 128:(d_ + 1) * 128],
                            s.ao[:, d_ * TPC + hf * 512: d_ * TPC + hf * 512 + 512],
                            start=(d_ == 0), stop=(d_ == NC - 1)),
                            waits=(ww if (d_ == 0 and hf == 0) else {}),
                            inc=("PE" if (d_ == NC - 1 and hf == 1) else None))
                pe_chunk_end[c] = C["PE"]
                w = {"PE": C["PE"], f"GL{c % 2}": gl_ready[c], f"XB{c % 2}": xb_ready[c]}
                p.ev("vector", lambda e, c=c: e.tensor_tensor(
                    s.gld[c % 2][:, :], psA[c % 2][:, :], s.gld[c % 2][:, :], op=OP.mult), waits=w)
                p.ev("vector", lambda e: e.drain())
                p.ev("vector", lambda e, c=c: e.tensor_tensor(
                    s.xbuf[c % 2][:, :], s.gld[c % 2][:, :], s.xbuf[c % 2][:, :], op=OP.add))
                p.ev("vector", lambda e: e.drain(), inc="DVE")
                dve_chunk_end[c] = C["DVE"]
                p.ev("sync", lambda e, c=c: e.dma_start(
                    out=xdst[c * 128:(c + 1) * 128, :], in_=s.xbuf[c % 2][:, :]),
                    waits={"DVE": C["DVE"]}, inc=f"OUT{c % 2}", n=16)
                out_ready[c] = C[f"OUT{c % 2}"]
            p.ev("sync", None, waits={"OUT0": C["OUT0"], "OUT1": C["OUT1"]})
            emit_block(nc, p, self.sems, f"out{L}", self.trace)

    def phase_final(self):
        nc, C = self.nc, self.C
        p = Plan(C)
        dram_srcs = {"x_cur": self.x_cur}
        sbuf_srcs = {"xn": getattr(self, "xn", None), "ao": getattr(self, "ao", None),
                     "hw": getattr(self, "hw_s", None)}
        for name in self.dbg:
            if name in dram_srcs:
                src = dram_srcs[name]
                dd = nc.dram_tensor(f"dbg_{name}", list(src.shape), src.dtype, kind="ExternalOutput")
                p.ev("sync", lambda e, dd=dd, src=src: e.dma_start(out=dd[:, :], in_=src[:, :]),
                     waits={"OUT0": C["OUT0"], "OUT1": C["OUT1"], "CC": C["CC"]}, inc="OUT0", n=16)
            else:
                src = sbuf_srcs[name]
                shp = [int(x) for x in src.shape]
                dd = nc.dram_tensor(f"dbg_{name}", shp, src.dtype, kind="ExternalOutput")
                p.ev("sync", lambda e, dd=dd, src=src: e.dma_start(out=dd[:, :], in_=src[:, :]),
                     waits={"OUT0": C["OUT0"], "OUT1": C["OUT1"]}, inc="OUT0", n=16)
        p.ev("sync", None, waits={"OUT0": C["OUT0"], "OUT1": C["OUT1"]})
        emit_block(nc, p, self.sems, "fin", self.trace)

    def build(self):
        self.phase_consts()
        out_gate = None
        stop = self.stop_after

        def done(tag):
            return stop == tag

        for L in range(2):
            xsrc = self.x_in if L == 0 else self.x_cur
            xdst = self.x_cur if L == 0 else self.out_fm
            self.phase_ln_kv(L, xsrc, out_gate)
            if done(f"ln{L}") or done(f"kv{L}"):
                break
            self.phase_ffng(L)
            if done(f"rout{L}"):
                break
            with ExitStack() as lscope:
                self.ao = lscope.enter_context(
                    self.nc.sbuf_tensor(Builder.uname("ao"), [128, NC * TPC], BF16))
                self.phase_att(L)
                if done(f"att{L}"):
                    break
                self.phase_out(L, xsrc, xdst, out_gate)
            if done(f"out{L}"):
                break
            out_gate = {"OUT0": self.C["OUT0"], "OUT1": self.C["OUT1"]}
        self.phase_final()
        self.ctx.close()
        return self.nc


def build_program(debug=None, stop_after=None, att_parts="full", att_kh=NKV):
    return Builder(debug, stop_after, att_parts, att_kh).build()


def kernel(**inputs):
    in_maps, col_maps = host_prepare(inputs)
    nc = build_program()
    res = run_bass_kernel_spmd(nc, in_maps, core_ids=list(range(8)))
    out = np.empty((B, S, D), dtype=np.float32)
    for core in range(8):
        b, cols_seq = col_maps[core]
        out[b, cols_seq, :] = res.results[core]["out_fm"].T
    return out


# version 18 (final)
# speedup vs baseline: 1.1018x; 1.0588x over previous
import math
from collections import defaultdict
from contextlib import ExitStack

import numpy as np
import ml_dtypes

import concourse.bass as bass
import concourse.mybir as mybir
from concourse.alu_op_type import AluOpType
from concourse.bass_utils import run_bass_kernel_spmd

F32 = mybir.dt.float32
BF16 = mybir.dt.bfloat16
AF = mybir.ActivationFunctionType
AX = mybir.AxisListType
OP = AluOpType

B, S, D = 2, 4096, 4096
NH, NKV, HD = 32, 8, 128
TPC = 1024
NC = 32
SCALE = HD ** -0.5
EPS = 1e-5
LORA_SCALING = 2.0
CH_SAFETY = 1.01

ORDER = [6, 4, 2, 0, 7, 5, 3, 1]
SLOT_OF = {m: s for s, m in enumerate(ORDER)}
BLOCKS = ([6, 4, 2, 0], [7, 5, 3, 1])


def att_sched(tiles):
    out = []
    for lg in range(tiles[0] + 1):
        act = [m for m in tiles if m >= lg]
        out.append((lg, 128 * len(act), lg in tiles, act))
    return out


def alibi_slopes(n):
    cp2 = 2 ** math.floor(math.log2(n))
    base = 2.0 ** (-8.0 / cp2)
    slopes = base ** np.arange(1, cp2 + 1, dtype=np.float32)
    if cp2 != n:
        extra_base = 2.0 ** (-4.0 / cp2)
        extra = extra_base ** np.arange(1, 1 + 2 * (n - cp2), 2, dtype=np.float32)
        slopes = np.concatenate([slopes, extra])
    return slopes.astype(np.float32)



def _tile128(a):
    Dk, Fc = a.shape[0] // 128, a.shape[1] // 128
    return np.ascontiguousarray(
        a.reshape(Dk, 128, Fc, 128).transpose(2, 1, 0, 3).reshape(Fc, 128, Dk * 128))


def _sbuf_img(a, ncol):
    Dk = a.shape[0] // 128
    assert a.shape[1] == ncol
    return np.ascontiguousarray(a.reshape(Dk, 128, ncol).transpose(1, 0, 2).reshape(128, Dk * ncol))


def host_prepare(inputs):
    f32 = np.float32

    def wq_split(w):
        return np.stack([_sbuf_img(np.ascontiguousarray(w[:, h * 128:(h + 1) * 128]), 128)
                         for h in range(NH)])

    shared = {}
    for L in range(2):
        if L == 0:
            wq = np.asarray(inputs["L0_attn_wq"], f32)
            wkv = np.concatenate([np.asarray(inputs["L0_attn_wk"], f32),
                                  np.asarray(inputs["L0_attn_wv"], f32)], axis=1)
            wo = np.asarray(inputs["L0_attn_wo"], f32)
            Wb = np.asarray(inputs["L0_ffn_W_base"], f32)
            bb = np.asarray(inputs["L0_ffn_b_base"], f32)
            Wr = np.asarray(inputs["L0_ffn_W_router"], f32)
            A = np.asarray(inputs["L0_ffn_A"], f32)
            Bm = np.asarray(inputs["L0_ffn_B"], f32)
        else:
            wqkv = np.asarray(inputs["L1_attn_w_qkv"], f32)
            wq, wkv = wqkv[:, :4096], np.ascontiguousarray(wqkv[:, 4096:6144])
            wo = np.asarray(inputs["L1_attn_w_o"], f32)
            Wb = np.asarray(inputs["L1_ffn_W_base"], f32)
            bb = np.asarray(inputs["L1_ffn_b_base"], f32)
            Wr = np.asarray(inputs["L1_ffn_W_router"], f32)
            A = np.asarray(inputs["L1_ffn_A"], f32)
            Bm = np.asarray(inputs["L1_ffn_B"], f32)
        shared[f"wq{L}"] = wq_split(wq)
        strips = [wkv[:, 0:512], wkv[:, 1024:1536], wkv[:, 512:1024], wkv[:, 1536:2048]]
        shared[f"wkv{L}"] = np.stack(
            [_sbuf_img(np.ascontiguousarray(st), 512) for st in strips])
        shared[f"wo{L}"] = _tile128(wo)
        shared[f"wb{L}"] = _tile128(np.ascontiguousarray(Wb.T))
        shared[f"aT{L}"] = _sbuf_img(np.ascontiguousarray(A.reshape(128, 4096).T), 128)
        shared[f"ball{L}"] = np.ascontiguousarray(Bm.transpose(0, 2, 1).reshape(128, 4096))
        shared[f"routT{L}"] = _sbuf_img(np.ascontiguousarray(Wr.T), 8)
        shared[f"bb{L}"] = np.ascontiguousarray(bb.reshape(32, 128).T)

    shared["idt"] = np.eye(128, dtype=f32)
    ind8 = np.zeros((8, 128), dtype=f32)
    for e in range(8):
        ind8[e, e * 16:(e + 1) * 16] = LORA_SCALING
    shared["ind8"] = ind8

    slopes = alibi_slopes(NH)
    x = np.asarray(inputs["L0_attn_x"], f32)

    in_maps, col_maps = [], []
    jj = np.arange(128)[:, None]
    ii = np.arange(128)[None, :]
    for core in range(8):
        b, r = core // 4, core % 4
        cols_seq = np.empty(TPC, dtype=np.int64)
        for s_ in range(8):
            m = ORDER[s_]
            cols_seq[s_ * 128:(s_ + 1) * 128] = 4 * (128 * m + np.arange(128)) + r
        col_maps.append((b, cols_seq))
        xs = np.ascontiguousarray(x[b].T[:, cols_seq])

        m0 = np.stack([((jj < ii) | ((jj == ii) & (rp <= r))).astype(f32) for rp in range(4)])
        mask0 = np.ascontiguousarray(m0.transpose(1, 0, 2).reshape(128, 512)).astype(ml_dtypes.bfloat16)

        dl = np.arange(8)[:, None, None, None]
        rp = np.arange(4)[None, :, None, None]
        jj4 = jj[None, None, :, :] * np.ones((1, 1, 1, 128))
        dmat = (512 * dl + 4 * (ii[None, None] - jj[None, None]) + (r - rp)).astype(np.int64)
        dmat_f = dmat.astype(f32)
        expb = np.empty((NH, 128, 4096), dtype=ml_dtypes.bfloat16)
        for h in range(NH):
            tab = np.where(dmat >= 0, np.exp(-slopes[h] * np.sqrt(np.maximum(dmat_f, 0.0))), 0.0)
            expb[h] = tab.transpose(2, 0, 1, 3).reshape(128, 4096).astype(ml_dtypes.bfloat16)

        m = {"x_in": xs, "mask0": mask0, "expb": expb}
        m.update(shared)
        in_maps.append(m)
    return in_maps, col_maps



class Plan:
    def __init__(self, cnt):
        self.cnt = cnt
        self.events = []

    def ev(self, eng, fn, waits=None, inc=None, n=1):
        self.events.append((eng, dict(waits or {}), fn, inc, n))
        if inc is not None:
            self.cnt[inc] += n

    def v(self, sem):
        return self.cnt[sem]


def emit_block(nc, plan, sems, name, trace=None):
    if trace is not None:
        trace.append(("BLOCK", name, list(plan.events)))
    with nc.Block(name, no_gpsimd_drain=True) as block:
        for eng_name in ("gpsimd", "sync", "scalar", "vector", "tensor"):
            evs = [e for e in plan.events if e[0] == eng_name]
            if not evs:
                continue

            def make(evs):
                def body(eng):
                    seen = {}
                    for _, waits, fn, inc, n in evs:
                        for semname, val in waits.items():
                            if val > 0 and seen.get(semname, -1) < val:
                                eng.wait_ge(sems[semname], val)
                                seen[semname] = val
                        instr = fn(eng) if fn is not None else None
                        if inc is not None:
                            if instr is None:
                                instr = eng.nop()
                            instr.then_inc(sems[inc], n)
                return body
            getattr(block, eng_name)(make(evs))



class Builder:
    _uid = 0

    @classmethod
    def uname(cls, name):
        cls._uid += 1
        return f"{name}_{cls._uid}"

    def __init__(self, debug=None, stop_after=None, att_parts="full", att_kh=NKV):
        self.dbg = debug or []
        self.stop_after = stop_after
        self.att_parts = att_parts
        self.att_kh = att_kh
        nc = self.nc = bass.Bass("TRN2", target_bir_lowering=False)
        self.C = defaultdict(int)

        self.x_in = nc.dram_tensor("x_in", [D, TPC], F32, kind="ExternalInput")
        self.out_fm = nc.dram_tensor("out_fm", [D, TPC], F32, kind="ExternalOutput")
        self.idt_d = nc.dram_tensor("idt", [128, 128], F32, kind="ExternalInput")
        self.ind8_d = nc.dram_tensor("ind8", [8, 128], F32, kind="ExternalInput")
        self.mask0_d = nc.dram_tensor("mask0", [128, 512], BF16, kind="ExternalInput")
        self.expb_d = nc.dram_tensor("expb", [NH, 128, 4096], BF16, kind="ExternalInput")
        self.W = {}
        for L in range(2):
            for nm, shape in [("wq", [NH, 128, 4096]), ("wkv", [4, 128, 16384]),
                              ("wo", [NC, 128, 4096]), ("wb", [NC, 128, 4096]),
                              ("aT", [128, 4096]), ("ball", [128, 4096]),
                              ("routT", [128, 256]), ("bb", [128, 32])]:
                self.W[f"{nm}{L}"] = nc.dram_tensor(f"{nm}{L}", shape, F32, kind="ExternalInput")
        self.x_cur = nc.dram_tensor("x_cur", [D, TPC], F32)
        self.g_buf = nc.dram_tensor("g_buf", [D, TPC], F32)
        self.kv_loc = [[nc.dram_tensor(f"kv_loc{L}_{s_}", [TPC, 512], BF16) for s_ in range(4)]
                       for L in range(2)]
        self.kv_full = [[nc.dram_tensor(f"kv_full{L}_{s_}", [4 * TPC, 512], BF16) for s_ in range(4)]
                        for L in range(2)]

        ctx = self.ctx = ExitStack()
        sbuf = lambda name, shape, dt: ctx.enter_context(nc.sbuf_tensor(name, shape, dt))
        semnames = (["W", "CC", "PE", "DVE", "ACT", "SM", "EXP", "MSK", "NPV",
                     "WQ", "EB", "KSQ", "RW", "GP"]
                    + [f"XB{i}" for i in range(4)]
                    + ["KVW0", "KVW1", "KST0", "KST1", "SLB0", "SLB1",
                       "OW0", "OW1", "OUT0", "OUT1", "GS0", "GS1", "GL0", "GL1"])
        self.sems = {s: ctx.enter_context(nc.semaphore(s)) for s in semnames}

        s = self
        s.sbuf = sbuf
        s.xn = sbuf("xn", [128, NC * TPC], BF16)
        s.onesb = sbuf("onesb", [128, 1], BF16)
        s.ones128 = sbuf("ones128", [128, 128], BF16)
        s.onesF = sbuf("onesF", [128, 1], F32)
        s.onesP = sbuf("onesP", [1, 128], F32)
        s.negP = sbuf("negP", [1, 128], F32)
        s.idt_s = sbuf("idt_s", [128, 128], F32)
        s.ind8_s = sbuf("ind8_s", [8, 128], F32)
        s.mask0_s = sbuf("mask0_s", [128, 512], BF16)
        s.bb_s = [sbuf(f"bb_s{L}", [128, 32], F32) for L in range(2)]
        s.hw_s = sbuf("hw_s", [128, TPC], BF16)
        s.constv = sbuf("constv", [128, 4], F32)
        self.const_vals = [EPS, math.log(SCALE * CH_SAFETY)]
        for i, v in enumerate(self.const_vals):
            nc.const_aps.register(F32, v, s.constv[:, i:i + 1])

        self.trace = []
        self.pending_negc = None
        self.pair_global = 0
        self.exp_of_pair = {}
        self.npv_pair_end = {}
        self.msk_pair_end = {}

    def phase_consts(self):
        C = self.C
        p = Plan(C)
        s = self
        p.ev("vector", lambda e: e.memset(s.onesb[:, :], 1.0))
        p.ev("vector", lambda e: e.memset(s.ones128[:, :], 1.0))
        p.ev("vector", lambda e: e.memset(s.onesF[:, :], 1.0))
        p.ev("vector", lambda e: e.memset(s.onesP[0:1, :], 1.0))
        p.ev("vector", lambda e: e.memset(s.negP[0:1, :], -1.0))
        for i, v in enumerate(self.const_vals):
            p.ev("vector", lambda e, i=i, v=v: e.memset(s.constv[:, i:i + 1], v))
        p.ev("vector", lambda e: e.drain(), inc="DVE")
        p.ev("gpsimd", lambda e: e.dma_start(out=s.idt_s[:, :], in_=s.idt_d[:, :]), inc="W", n=16)
        p.ev("gpsimd", lambda e: e.dma_start(out=s.ind8_s[:, :], in_=s.ind8_d[:, :]), inc="W", n=16)
        p.ev("gpsimd", lambda e: e.dma_start(out=s.mask0_s[:, :], in_=s.mask0_d[:, :]), inc="W", n=16)
        for L in range(2):
            p.ev("gpsimd", lambda e, L=L: e.dma_start(out=s.bb_s[L][:, :], in_=s.W[f"bb{L}"][:, :]), inc="W", n=16)
        p.ev("gpsimd", None, waits={"W": p.v("W")})
        emit_block(self.nc, p, self.sems, "consts", self.trace)

    def phase_ln_kv(self, L, xsrc, out_gate):
        nc, C, s = self.nc, self.C, self
        p = Plan(C)
        with ExitStack() as sc:
            sbuf = lambda name, shape, dt: sc.enter_context(nc.sbuf_tensor(Builder.uname(name), shape, dt))
            s.wkvs = [sbuf(f"wkvs{i}", [128, 16384], BF16) for i in range(2)]
            p.ev("gpsimd", lambda e, L=L: e.dma_start(out=s.wkvs[0][:, :], in_=s.W[f"wkv{L}"][0]), inc="KVW0", n=16)
            p.ev("gpsimd", lambda e, L=L: e.dma_start(out=s.wkvs[1][:, :], in_=s.W[f"wkv{L}"][1]), inc="KVW1", n=16)
            strip_ready = {0: ("KVW0", C["KVW0"]), 1: ("KVW1", C["KVW1"])}
            ln_ps = ExitStack()
            ps_s = ln_ps.enter_context(nc.psum_tensor(Builder.uname("ps_s"), [1, TPC], F32))
            ps_q = ln_ps.enter_context(nc.psum_tensor(Builder.uname("ps_q"), [1, TPC], F32))
            ps_A = ln_ps.enter_context(nc.psum_tensor(Builder.uname("ps_A"), [128, TPC], F32))
            ps_B = ln_ps.enter_context(nc.psum_tensor(Builder.uname("ps_B"), [128, TPC], F32))
            s.xbuf = [sbuf(f"xbuf{i}", [128, TPC], F32) for i in range(4)]
            s.xsqbuf = [sbuf(f"xsqbuf{i}", [128, TPC], F32) for i in range(2)]
            s.tmpf = [sbuf(f"tmpf{i}", [128, TPC], F32) for i in range(2)]
            st = [sbuf(f"st{i}", [1, TPC], F32) for i in range(4)]
            s.meanT, s.msqT, s.vqT, s.varT = st[0], st[1], st[2], st[3]
            s.lnvT, s.rstdT, s.mrstdT = st[1], st[2], st[3]
            s.AbcT = sbuf("AbcT", [128, TPC], F32)
            s.BbcT = sbuf("BbcT", [128, TPC], F32)
            base_PE, base_DVE, base_ACT = C["PE"], C["DVE"], C["ACT"]
            xb_ready = {}
            xn_dve = {}
            for c in range(NC):
                w = {}
                if c == 0 and out_gate:
                    w.update(out_gate)
                if c >= 4:
                    w.update({"ACT": base_ACT + (c - 3), "PE": base_PE + (c - 3)})
                p.ev("sync", lambda e, c=c: e.dma_start(
                    out=s.xbuf[c % 4][:, :], in_=xsrc[c * 128:(c + 1) * 128, :]),
                    waits=w, inc=f"XB{c % 4}", n=16)
                xb_ready[c] = C[f"XB{c % 4}"]
            for c in range(NC):
                w = {f"XB{c % 4}": xb_ready[c]}
                if c >= 2:
                    w["PE"] = base_PE + (c - 1)
                p.ev("scalar", lambda e, c=c: e.activation(s.xsqbuf[c % 2][:, :], s.xbuf[c % 4][:, :], AF.Square),
                     waits=w, inc="ACT")
            for c in range(NC):
                w = {"ACT": base_ACT + (c + 1)}
                if c >= 2:
                    w["PE"] = base_PE + (c - 1)
                for hf in range(2):
                    p.ev("tensor", lambda e, c=c, hf=hf: e.matmul(
                        ps_s[0:1, hf * 512:(hf + 1) * 512], s.onesF[:, :],
                        s.xbuf[c % 4][:, hf * 512:(hf + 1) * 512],
                        start=(c == 0), stop=(c == NC - 1)), waits=(w if hf == 0 else {}))
                    p.ev("tensor", lambda e, c=c, hf=hf: e.matmul(
                        ps_q[0:1, hf * 512:(hf + 1) * 512], s.onesF[:, :],
                        s.xsqbuf[c % 2][:, hf * 512:(hf + 1) * 512],
                        start=(c == 0), stop=(c == NC - 1)), inc=("PE" if hf == 1 else None))
            sq_act = base_ACT + NC
            p.ev("vector", lambda e: e.tensor_scalar(s.meanT[0:1, :], ps_s[0:1, :], 1.0 / D, None, OP.mult),
                 waits={"PE": p.v("PE")})
            p.ev("vector", lambda e: e.drain())
            p.ev("vector", lambda e: e.tensor_tensor(s.msqT[0:1, :], s.meanT[0:1, :], s.meanT[0:1, :], op=OP.mult))
            p.ev("vector", lambda e: e.drain())
            p.ev("vector", lambda e: e.tensor_scalar(s.vqT[0:1, :], ps_q[0:1, :], 1.0 / D, None, OP.mult))
            p.ev("vector", lambda e: e.drain())
            p.ev("vector", lambda e: e.tensor_tensor(s.varT[0:1, :], s.vqT[0:1, :], s.msqT[0:1, :], op=OP.subtract),
                 inc="DVE")
            p.ev("scalar", lambda e: e.activation(s.lnvT[0:1, :], s.varT[0:1, :], AF.Ln, bias=EPS),
                 waits={"DVE": p.v("DVE")})
            p.ev("scalar", lambda e: e.drain())
            p.ev("scalar", lambda e: e.activation(s.rstdT[0:1, :], s.lnvT[0:1, :], AF.Exp, scale=-0.5), inc="ACT")
            p.ev("vector", lambda e: e.tensor_tensor(s.mrstdT[0:1, :], s.meanT[0:1, :], s.rstdT[0:1, :], op=OP.mult),
                 waits={"ACT": p.v("ACT")})
            p.ev("vector", lambda e: e.drain(), inc="DVE")
            w = {"DVE": p.v("DVE")}
            for hf in range(2):
                p.ev("tensor", lambda e, hf=hf: e.matmul(
                    ps_A[:, hf * 512:(hf + 1) * 512], s.onesP[0:1, :], s.rstdT[0:1, hf * 512:(hf + 1) * 512],
                    start=True, stop=True), waits=(w if hf == 0 else {}))
                p.ev("tensor", lambda e, hf=hf: e.matmul(
                    ps_B[:, hf * 512:(hf + 1) * 512], s.negP[0:1, :], s.mrstdT[0:1, hf * 512:(hf + 1) * 512],
                    start=True, stop=True), inc=("PE" if hf == 1 else None))
            p.ev("vector", lambda e: e.tensor_copy(s.AbcT[:, :], ps_A[:, :]), waits={"PE": p.v("PE")})
            p.ev("vector", lambda e: e.tensor_copy(s.BbcT[:, :], ps_B[:, :]))
            p.ev("vector", lambda e: e.drain(), inc="DVE")
            dve_bc = p.v("DVE")
            for c in range(NC):
                xn_dve[c] = ("DVE", dve_bc + c + 1)
            for c in range(NC):
                w = {"DVE": dve_bc, "ACT": sq_act}
                if c >= 4:
                    sem4, val4 = xn_dve[c - 4]
                    w[sem4] = max(w.get(sem4, 0), val4)
                p.ev("sync", lambda e, c=c: e.dma_start(
                    out=s.xbuf[c % 4][:, :], in_=xsrc[c * 128:(c + 1) * 128, :]),
                    waits=w, inc=f"XB{c % 4}", n=16)
                xb_ready[c] = C[f"XB{c % 4}"]
            for c in range(NC):
                ww = {f"XB{c % 4}": xb_ready[c]}
                p.ev("vector", lambda e, c=c: e.tensor_tensor(
                    s.tmpf[c % 2][:, :], s.xbuf[c % 4][:, :], s.AbcT[:, :], op=OP.mult), waits=ww)
                p.ev("vector", lambda e: e.drain())
                p.ev("vector", lambda e, c=c: e.tensor_tensor(
                    s.xn[:, c * TPC:(c + 1) * TPC], s.tmpf[c % 2][:, :], s.BbcT[:, :], op=OP.add))
                p.ev("vector", lambda e: e.drain(), inc="DVE")
                assert xn_dve[c] == ("DVE", C["DVE"])
            ln_ps.close()
            pkv8 = [sc.enter_context(nc.psum_tensor(Builder.uname(f"ps_kv{i}"), [128, 512], F32))
                    for i in range(8)]
            s.kvst = [sbuf(f"kvst{i}", [128, 512], BF16) for i in range(2)]
            base_PE, base_DVE = C["PE"], C["DVE"]
            kst_ready = {}
            for s_ in range(4):
                if 1 <= s_ < 3:
                    w = {"PE": base_PE + 8 * s_}
                    slot = f"KVW{(s_ + 1) % 2}"
                    p.ev("gpsimd", lambda e, s_=s_: e.dma_start(
                        out=s.wkvs[(s_ + 1) % 2][:, :], in_=s.W[f"wkv{L}"][s_ + 1]), waits=w, inc=slot, n=16)
                    strip_ready[s_ + 1] = (slot, C[slot])
                if s_ == 0:
                    for c in range(NC):
                        semc, valc = xn_dve[c]
                        for tt in range(8):
                            w = {}
                            if tt == 0:
                                w = {semc: valc}
                                if c == 0:
                                    sl, val = strip_ready[0]
                                    w[sl] = val
                            p.ev("tensor", lambda e, tt=tt, c=c: e.matmul(
                                pkv8[tt][:, :], s.xn[:, c * TPC + tt * 128: c * TPC + (tt + 1) * 128],
                                s.wkvs[0][:, c * 512:(c + 1) * 512],
                                start=(c == 0), stop=(c == NC - 1)),
                                waits=w, inc=("PE" if (c == NC - 1) else None))
                    for tt in range(8):
                        it = tt
                        w2 = {"PE": base_PE + 8}
                        if it >= 2:
                            w2[f"KST{it % 2}"] = kst_ready[it - 2]
                        p.ev("vector", lambda e, tt=tt: e.tensor_copy(s.kvst[tt % 2][:, :], pkv8[tt][:, :]),
                             waits=w2, inc="DVE")
                        p.ev("sync", lambda e, tt=tt, L=L: e.dma_start(
                            out=self.kv_loc[L][0][tt * 128:(tt + 1) * 128, :],
                            in_=s.kvst[tt % 2][:, :]),
                            waits={"DVE": base_DVE + it + 1}, inc=f"KST{it % 2}", n=16)
                        kst_ready[it] = C[f"KST{it % 2}"]
                else:
                    for tt in range(8):
                        it = s_ * 8 + tt
                        for c in range(NC):
                            semc, valc = xn_dve[c]
                            w = {semc: valc}
                            if c == 0:
                                sl, val = strip_ready[s_]
                                w[sl] = val
                                if it >= 2:
                                    w["DVE"] = max(w.get("DVE", 0), base_DVE + (it - 1))
                            p.ev("tensor", lambda e, s_=s_, tt=tt, c=c: e.matmul(
                                pkv8[tt % 2][:, :], s.xn[:, c * TPC + tt * 128: c * TPC + (tt + 1) * 128],
                                s.wkvs[s_ % 2][:, c * 512:(c + 1) * 512],
                                start=(c == 0), stop=(c == NC - 1)),
                                waits=w, inc=("PE" if c == NC - 1 else None))
                        w2 = {"PE": base_PE + it + 1}
                        if it >= 2:
                            w2[f"KST{it % 2}"] = kst_ready[it - 2]
                        p.ev("vector", lambda e, tt=tt: e.tensor_copy(s.kvst[tt % 2][:, :], pkv8[tt % 2][:, :]),
                             waits=w2, inc="DVE")
                        p.ev("sync", lambda e, s_=s_, tt=tt, L=L: e.dma_start(
                            out=self.kv_loc[L][s_][tt * 128:(tt + 1) * 128, :],
                            in_=s.kvst[tt % 2][:, :]),
                            waits={"DVE": base_DVE + it + 1}, inc=f"KST{it % 2}", n=16)
                        kst_ready[it] = C[f"KST{it % 2}"]
                p.ev("gpsimd", lambda e, s_=s_, L=L: e.collective_compute(
                    "AllGather", OP.bypass, replica_groups=[[0, 1, 2, 3], [4, 5, 6, 7]],
                    ins=[self.kv_loc[L][s_].ap().opt()], outs=[self.kv_full[L][s_].ap().opt()]),
                    waits={"KST0": C["KST0"], "KST1": C["KST1"]}, inc="CC")
            emit_block(nc, p, self.sems, f"lnkv{L}", self.trace)

    def phase_att(self, L):
        nc, C, s = self.nc, self.C, self
        p = Plan(C)
        sm_head_end, msk_head_end, npv_head_end = {}, {}, {}
        npv_kh_end = {}

        with ExitStack() as sc:
            sbuf = lambda name, shape, dt: sc.enter_context(nc.sbuf_tensor(Builder.uname(name), shape, dt))
            pS0 = sc.enter_context(nc.psum_tensor(Builder.uname("ps_S0"), [128, 1024], F32))
            pS1 = sc.enter_context(nc.psum_tensor(Builder.uname("ps_S1"), [128, 1024], F32))
            pS2 = sc.enter_context(nc.psum_tensor(Builder.uname("ps_S2"), [128, 1024], F32))
            pAcc = sc.enter_context(nc.psum_tensor(Builder.uname("ps_acc"), [128, 512], F32))
            pn = sc.enter_context(nc.psum_tensor(Builder.uname("ps_n"), [128, 512], F32))
            pq = pn
            pnorm = pAcc
            s.kfm = [sbuf(f"kfm{i}", [128, 4096], BF16) for i in range(2)]
            s.vsl = [sbuf(f"vsl{i}", [128, 4096], BF16) for i in range(2)]
            s.ksq = sbuf("ksq", [128, 4096], BF16)
            s.Ubuf = [sbuf(f"Ubuf{i}", [128, 1024], BF16) for i in range(3)]
            s.expb_s = [sbuf("expb_s0", [128, 4096], BF16)]
            s.wqb = [sbuf("wqb0", [128, 4096], BF16)]
            s.qh = [sbuf(f"qh{i}", [128, TPC], BF16) for i in range(2)]
            s.qsq = sbuf("qsq", [128, TPC], BF16)
            s.kaccT = sbuf("kaccT", [128, 16], F32)
            s.kmaxT = sbuf("kmaxT", [128, 2], F32)
            s.qaccT = sbuf("qaccT", [128, 2], F32)
            s.qkT = sbuf("qkT", [128, 2], F32)
            s.lnqkT = sbuf("lnqkT", [128, 1], F32)
            s.chT = sbuf("chT", [128, 1], F32)
            s.cbias = sbuf("cbias", [128, 2], F32)
            s.lnnT = sbuf("lnnT", [128, 512], F32)
            s.rbcT = sbuf("rbcT", [128, 512], F32)
            pS = [pS0, pS1, pS2]
            slab_ready = {}
            ksq_done = {}
            wq_ready = {}
            expb_ready = {}
            qh_gate = {}
            qn_dve = {}
            state = {"epi_dve": 0}

            cc_base = C["CC"] - 4
            def slab_load(kh):
                bufk, bufv = s.kfm[kh % 2], s.vsl[kh % 2]
                kstrip = 0 if kh < 4 else 2
                vstrip = 1 if kh < 4 else 3
                col = (kh % 4) * 128
                slot = f"SLB{kh % 2}"
                w = {"CC": cc_base + kstrip + 1}
                if kh >= 2:
                    w["NPV"] = npv_kh_end[kh - 2]
                    w["KSQ"] = ksq_done[kh - 2]
                p.ev("sync", lambda e, bufk=bufk, kstrip=kstrip, col=col, L=L: e.dma_start_transpose(
                    bufk[:, :], self.kv_full[L][kstrip][:, col:col + 128]), waits=w, inc=slot, n=16)
                p.ev("sync", None, waits={"CC": cc_base + vstrip + 1})
                for q4 in range(4):
                    p.ev("sync", lambda e, q4=q4, bufv=bufv, vstrip=vstrip, col=col, L=L: e.dma_start(
                        out=bufv[:, q4 * 1024:(q4 + 1) * 1024].rearrange("p (t d) -> p t d", t=8),
                        in_=self.kv_full[L][vstrip][q4 * 1024:(q4 + 1) * 1024, col:col + 128]
                        .rearrange("(t p) d -> p t d", p=128)), inc=slot, n=16)
                slab_ready[kh] = C[slot]

            def knorm(kh):
                p.ev("vector", lambda e, kh=kh: e.tensor_tensor(
                    s.ksq[:, :], s.kfm[kh % 2][:, :], s.kfm[kh % 2][:, :], op=OP.mult),
                    waits={f"SLB{kh % 2}": slab_ready[kh]}, inc="KSQ")
                ksq_done[kh] = C["KSQ"]
                base_dve_k = C["DVE"]
                for j8 in range(8):
                    tgt = pnorm if j8 % 2 == 0 else pq
                    w = {"KSQ": C["KSQ"]}
                    if j8 < 2:
                        w["DVE"] = base_dve_k
                        w["ACT"] = C["ACT"]
                    else:
                        w["DVE"] = base_dve_k + (j8 - 1)
                    p.ev("tensor", lambda e, j8=j8, tgt=tgt: e.matmul(
                        tgt[:, :], s.ones128[:, :], s.ksq[:, j8 * 512:(j8 + 1) * 512], start=True, stop=True),
                        waits=w, inc="PE")
                for j8 in range(8):
                    tgt = pnorm if j8 % 2 == 0 else pq
                    p.ev("vector", lambda e, j8=j8, tgt=tgt: e.reduce_max(
                        s.kaccT[:, j8:j8 + 1], tgt[:, :], axis=AX.X),
                        waits={"PE": C["PE"] - 7 + j8}, inc="DVE")
                p.ev("vector", lambda e: e.drain())
                p.ev("vector", lambda e, kh=kh: e.reduce_max(s.kmaxT[:, kh % 2:kh % 2 + 1], s.kaccT[:, 0:8], axis=AX.X))
                p.ev("vector", lambda e: e.drain(), inc="DVE")

            def prepare_head(h, qproj_only=False, skip_qproj=False):
                kh = h // 4
                if skip_qproj:
                    pass
                else:
                  for hf in range(2):
                    tgt = pq if hf == 0 else pnorm
                    for c in range(NC):
                        ww = {}
                        if c == 0:
                            ww = {"WQ": wq_ready[h], "DVE": C["DVE"], "ACT": C["ACT"]}
                            if hf == 0 and h >= 2:
                                ww["NPV"] = npv_head_end[h - 2]
                        p.ev("tensor", lambda e, hf=hf, c=c, h=h, tgt=tgt: e.matmul(
                            tgt[:, :], s.wqb[0][:, c * 128:(c + 1) * 128],
                            s.xn[:, c * TPC + hf * 512: c * TPC + hf * 512 + 512],
                            start=(c == 0), stop=(c == NC - 1)), waits=ww,
                            inc=("PE" if c == NC - 1 else None))
                    pe_now = C["PE"]
                    p.ev("vector", lambda e, hf=hf, h=h, tgt=tgt: e.tensor_copy(
                        s.qh[h % 2][:, hf * 512:(hf + 1) * 512], tgt[:, :]), waits={"PE": pe_now}, inc="DVE")
                    p.ev("scalar", lambda e, hf=hf, tgt=tgt: e.activation(
                        s.qsq[:, hf * 512:(hf + 1) * 512], tgt[:, :], AF.Square),
                        waits={"PE": pe_now, "DVE": C["DVE"]}, inc="ACT")
                if h + 1 < NH and not skip_qproj:
                    p.ev("gpsimd", lambda e, h=h, L=L: e.dma_start(
                        out=s.wqb[0][:, :], in_=s.W[f"wq{L}"][h + 1]),
                        waits={"PE": C["PE"]}, inc="WQ", n=16)
                    wq_ready[h + 1] = C["WQ"]
                qh_gate[h] = C["DVE"]
                if qproj_only:
                    return
                for hf in range(2):
                    p.ev("tensor", lambda e, hf=hf: e.matmul(
                        pnorm[:, :] if hf == 0 else pq[:, :],
                        s.ones128[:, :], s.qsq[:, hf * 512:(hf + 1) * 512], start=True, stop=True),
                        waits=({"ACT": C["ACT"], "DVE": C["DVE"]} if hf == 0 else {"ACT": C["ACT"]}), inc="PE")
                pe_q = C["PE"]
                p.ev("vector", lambda e: e.reduce_max(s.qaccT[:, 0:1], pnorm[:, :], axis=AX.X),
                     waits={"PE": pe_q - 1})
                p.ev("vector", lambda e: e.drain())
                p.ev("vector", lambda e: e.reduce_max(s.qaccT[:, 1:2], pq[:, :], axis=AX.X),
                     waits={"PE": pe_q}, inc="DVE")
                p.ev("vector", lambda e: e.drain())
                p.ev("vector", lambda e, kh=kh: e.tensor_tensor(
                    s.qkT[:, 0:1], s.qaccT[:, 0:1], s.kmaxT[:, kh % 2:kh % 2 + 1], op=OP.mult))
                p.ev("vector", lambda e: e.drain())
                p.ev("vector", lambda e, kh=kh: e.tensor_tensor(
                    s.qkT[:, 1:2], s.qaccT[:, 1:2], s.kmaxT[:, kh % 2:kh % 2 + 1], op=OP.mult))
                p.ev("vector", lambda e: e.drain())
                p.ev("vector", lambda e: e.tensor_tensor(
                    s.lnqkT[:, 0:1], s.qkT[:, 0:1], s.qkT[:, 1:2], op=OP.max))
                p.ev("vector", lambda e: e.drain(), inc="DVE")
                qn_dve[h] = C["DVE"]
                p.ev("scalar", lambda e: e.activation(s.chT[:, 0:1], s.lnqkT[:, 0:1], AF.Ln, bias=0.0),
                     waits={"DVE": C["DVE"]})
                p.ev("scalar", lambda e: e.drain())
                p.ev("scalar", lambda e: e.activation(
                    s.chT[:, 0:1], s.chT[:, 0:1], AF.Exp, scale=0.5,
                    bias=math.log(SCALE * CH_SAFETY)))
                p.ev("scalar", lambda e: e.drain())
                p.ev("scalar", lambda e, h=h: e.mul(s.cbias[:, h % 2:h % 2 + 1], s.chT[:, 0:1], -1.0), inc="ACT")
                p.ev("scalar", lambda e: e.drain())

            def emit_attn_block(h, bi, tiles):
                kh = h // 4
                blk_col = bi * 512
                sched = att_sched(tiles)
                units = [(lg, Wd, bnd, act, rp) for (lg, Wd, bnd, act) in sched for rp in range(4)]
                first_unit = True
                pending = []
                gate = max(state["epi_dve"], qn_dve.get(h, 0))

                def emit_npv(arg):
                    pid, items = arg
                    for (buf_, ucol_, Wd_, T_idx_, ww_, fu_, lu_) in items:
                        p.ev("tensor", lambda e, buf=buf_, ucol=ucol_, Wd=Wd_, fu=fu_, lu=lu_: e.matmul(
                            pn[:, 0:Wd], s.ones128[:, :], s.Ubuf[buf][:, ucol:ucol + Wd],
                            start=fu, stop=lu), waits=ww_)
                        p.ev("tensor", lambda e, buf=buf_, ucol=ucol_, Wd=Wd_, T_idx=T_idx_, kh=kh,
                             fu=fu_, lu=lu_: e.matmul(
                                 pAcc[:, 0:Wd], s.vsl[kh % 2][:, T_idx * 128:(T_idx + 1) * 128],
                                 s.Ubuf[buf][:, ucol:ucol + Wd],
                                 start=fu, stop=lu), inc="NPV")
                    self.npv_pair_end[pid] = C["NPV"]

                for pu in range(0, len(units), 2):
                    pair = self.pair_global
                    self.pair_global += 1
                    buf = pair % 3
                    for slot in range(2):
                        lg, Wd, bnd, act, rp = units[pu + slot]
                        T_idx = 8 * rp + SLOT_OF[lg]
                        ww = {}
                        if pu in (0, 2) and slot == 0:
                            ww["DVE"] = qh_gate[h]
                        if pu == 0 and slot == 0 and L == 1:
                            ww["EB"] = expb_ready[h]
                        if slot == 0 and pair >= 3:
                            ww["EXP"] = self.exp_of_pair[pair - 3]
                        p.ev("tensor", lambda e, buf=buf, slot=slot, Wd=Wd, T_idx=T_idx, h=h,
                             blk_col=blk_col, kh=kh: e.matmul(
                                 pS[buf][:, slot * 512: slot * 512 + Wd],
                                 s.kfm[kh % 2][:, T_idx * 128:(T_idx + 1) * 128],
                                 s.qh[h % 2][:, blk_col:blk_col + Wd],
                                 start=True, stop=True), waits=ww, inc="SM")
                    lg, Wd, bnd, act, _ = units[pu]
                    ww = {"SM": C["SM"]}
                    if pair >= 3:
                        ww["NPV"] = self.npv_pair_end[pair - 3]
                        ww["MSK"] = self.msk_pair_end[pair - 3]
                    if Wd == 512:
                        p.ev("scalar", lambda e, buf=buf, h=h: e.activation(
                            s.Ubuf[buf][:, 0:1024], pS[buf][:, 0:1024], AF.Exp,
                            bias=s.cbias[:, h % 2:h % 2 + 1], scale=SCALE), waits=ww, inc="EXP")
                    else:
                        p.ev("scalar", lambda e, buf=buf, Wd=Wd, h=h: e.activation(
                            s.Ubuf[buf][:, :].rearrange("p (s w) -> p s w", s=2)[:, :, 0:Wd],
                            pS[buf][:, :].rearrange("p (s w) -> p s w", s=2)[:, :, 0:Wd],
                            AF.Exp, bias=s.cbias[:, h % 2:h % 2 + 1], scale=SCALE), waits=ww, inc="EXP")
                    self.exp_of_pair[pair] = C["EXP"]
                    items = []
                    for slot in range(2):
                        lg, Wd, bnd, act, rp = units[pu + slot]
                        last_unit = (pu + slot == len(units) - 1)
                        T_idx = 8 * rp + SLOT_OF[lg]
                        ucol = slot * 512
                        did_mask = False
                        if L == 0:
                            if bnd:
                                p.ev("vector", lambda e, buf=buf, ucol=ucol, Wd=Wd, rp=rp: e.tensor_tensor(
                                    s.Ubuf[buf][:, ucol + Wd - 128: ucol + Wd],
                                    s.Ubuf[buf][:, ucol + Wd - 128: ucol + Wd],
                                    s.mask0_s[:, rp * 128:(rp + 1) * 128], op=OP.mult),
                                    waits={"EXP": self.exp_of_pair[pair]}, inc="MSK")
                                did_mask = True
                        else:
                            nact = len(act)
                            dl0 = act[0] - lg
                            base_off = (dl0 * 4 + rp) * 128
                            tabap = bass.AP(s.expb_s[0].tensor if hasattr(s.expb_s[0], "tensor") else s.expb_s[0],
                                            base_off, [[4096, 128], [-1024, nact], [1, 128]])
                            uap = s.Ubuf[buf][:, ucol:ucol + Wd].rearrange("p (t n) -> p t n", n=128)
                            p.ev("vector", lambda e, uap=uap, tabap=tabap: e.tensor_tensor(
                                uap, uap, tabap, op=OP.mult),
                                waits={"EXP": self.exp_of_pair[pair]}, inc="MSK")
                            did_mask = True
                        ww = {"EXP": self.exp_of_pair[pair]}
                        if did_mask:
                            ww["MSK"] = C["MSK"]
                        if first_unit:
                            ww["DVE"] = gate
                        items.append((buf, ucol, Wd, T_idx, ww, first_unit, last_unit))
                        self.msk_pair_end[pair] = C["MSK"]
                        first_unit = False
                    pending.append((pair, items))
                    if len(pending) > 2:
                        emit_npv(pending.pop(0))
                for pd in pending:
                    emit_npv(pd)
                p.ev("scalar", lambda e: e.activation(s.lnnT[:, :], pn[:, 0:512], AF.Ln, bias=0.0),
                     waits={"NPV": C["NPV"]})
                p.ev("scalar", lambda e: e.drain())
                p.ev("scalar", lambda e: e.activation(s.rbcT[:, :], s.lnnT[:, :], AF.Exp, scale=-1.0),
                     inc="ACT")
                p.ev("vector", lambda e, h=h, blk_col=blk_col: e.tensor_tensor(
                    s.ao[:, h * TPC + blk_col: h * TPC + blk_col + 512],
                    pAcc[:, 0:512], s.rbcT[:, :], op=OP.mult),
                    waits={"ACT": C["ACT"]})
                p.ev("vector", lambda e: e.drain(), inc="DVE")
                state["epi_dve"] = C["DVE"]

            slab_load(0)
            if L == 1:
                p.ev("scalar", lambda e: e.dma_start(out=s.expb_s[0][:, :], in_=s.expb_d[0]), inc="EB", n=16)
                expb_ready[0] = C["EB"]
            p.ev("gpsimd", lambda e, L=L: e.dma_start(out=s.wqb[0][:, :], in_=s.W[f"wq{L}"][0]), inc="WQ", n=16)
            wq_ready[0] = C["WQ"]
            for kh in range(NKV):
                if kh == 0:
                    prepare_head(0, qproj_only=True)
                knorm(kh)
                if kh + 1 < NKV:
                    slab_load(kh + 1)
                for g_ in range(4):
                    h = 4 * kh + g_
                    prepare_head(h, skip_qproj=(h == 0))
                    emit_attn_block(h, 0, BLOCKS[0])
                    emit_attn_block(h, 1, BLOCKS[1])
                    sm_head_end[h] = C["SM"]
                    msk_head_end[h] = C["MSK"]
                    npv_head_end[h] = C["NPV"]
                    if L == 1 and h + 1 < NH:
                        p.ev("scalar", lambda e, h=h: e.dma_start(
                            out=s.expb_s[0][:, :], in_=s.expb_d[h + 1]),
                            waits={"MSK": C["MSK"]}, inc="EB", n=16)
                        expb_ready[h + 1] = C["EB"]
                npv_kh_end[kh] = C["NPV"]
            emit_block(nc, p, self.sems, f"att{L}", self.trace)
        return

    def phase_ffng(self, L):
        nc, C, s = self.nc, self.C, self
        p = Plan(C)
        with ExitStack() as sc:
            sbuf = lambda name, shape, dt: sc.enter_context(nc.sbuf_tensor(Builder.uname(name), shape, dt))
            rt_ps = ExitStack()
            plg = rt_ps.enter_context(nc.psum_tensor(Builder.uname("ps_lg"), [8, TPC], F32))
            pt8 = rt_ps.enter_context(nc.psum_tensor(Builder.uname("ps_t8"), [128, 64], F32))
            pcb = rt_ps.enter_context(nc.psum_tensor(Builder.uname("ps_cb"), [128, TPC], F32))
            ph = rt_ps.enter_context(nc.psum_tensor(Builder.uname("ps_h"), [128, TPC], F32))
            s.aT_s = sbuf("aT_s", [128, 4096], BF16)
            s.routT_s = sbuf("routT_s", [128, 256], BF16)
            s.logit_s = sbuf("logit_s", [8, TPC], F32)
            s.v8_s = sbuf("v8_s", [128, 64], F32)
            s.m8_s = sbuf("m8_s", [128, 8], F32)
            s.nm12_s = sbuf("nm12_s", [128, 8], F32)
            s.ge_s = sbuf("ge_s", [128, 8], F32)
            s.sg_s = sbuf("sg_s", [128, 8], F32)
            s.comb_s = sbuf("comb_s", [128, 64], F32)
            s.combT_s = sbuf("combT_s", [8, TPC], F32)
            s.h_s = sbuf("h_s", [128, TPC], F32)
            s.ball_s = sbuf("ball_s", [128, 4096], BF16)
            s.wbb = [sbuf(f"wbb{i}", [128, 4096], BF16) for i in range(2)]
            s.g_s = [sbuf(f"g_s{i}", [128, TPC], F32) for i in range(2)]
            p.ev("gpsimd", lambda e, L=L: e.dma_start(out=s.routT_s[:, :], in_=s.W[f"routT{L}"][:, :]), inc="RW", n=16)
            w_rt = C["RW"]
            p.ev("gpsimd", lambda e, L=L: e.dma_start(out=s.aT_s[:, :], in_=s.W[f"aT{L}"][:, :]), inc="GP", n=16)
            w_at = C["GP"]
            p.ev("gpsimd", lambda e, L=L: e.dma_start(out=s.ball_s[:, :], in_=s.W[f"ball{L}"][:, :]), inc="W", n=16)
            w_ready = C["W"]
            p.ev("gpsimd", lambda e, L=L: e.dma_start(out=s.wbb[0][:, :], in_=s.W[f"wb{L}"][0]), inc="OW0", n=16)
            ow_ready = {0: C["OW0"]}
            for c in range(NC):
                for hf in range(2):
                    p.ev("tensor", lambda e, c=c, hf=hf: e.matmul(
                        plg[0:8, hf * 512:(hf + 1) * 512], s.routT_s[:, c * 8:(c + 1) * 8],
                        s.xn[:, c * TPC + hf * 512: c * TPC + hf * 512 + 512],
                        start=(c == 0), stop=(c == NC - 1)),
                        waits=({"RW": w_rt} if (c == 0 and hf == 0) else {}),
                        inc=("PE" if (c == NC - 1 and hf == 1) else None))
            pe_lg = C["PE"]
            for c in range(NC):
                for hf in range(2):
                    p.ev("tensor", lambda e, c=c, hf=hf: e.matmul(
                        ph[:, hf * 512:(hf + 1) * 512], s.aT_s[:, c * 128:(c + 1) * 128],
                        s.xn[:, c * TPC + hf * 512: c * TPC + hf * 512 + 512],
                        start=(c == 0), stop=(c == NC - 1)),
                        waits=({"GP": w_at} if (c == 0 and hf == 0) else {}),
                        inc=("PE" if (c == NC - 1 and hf == 1) else None))
            pe_h = C["PE"]
            p.ev("vector", lambda e: e.tensor_copy(s.logit_s[0:8, :], plg[0:8, :]),
                 waits={"PE": pe_lg}, inc="DVE")
            for tt in range(8):
                p.ev("tensor", lambda e, tt=tt: e.transpose(
                    pt8[:, tt * 8:(tt + 1) * 8], s.logit_s[0:8, tt * 128:(tt + 1) * 128], s.idt_s[0:8, 0:8]),
                    waits=({"DVE": C["DVE"]} if tt == 0 else {}), inc=("PE" if tt == 7 else None))
            p.ev("vector", lambda e: e.tensor_copy(s.v8_s[:, :], pt8[:, :]), waits={"PE": C["PE"]})
            p.ev("vector", lambda e: e.drain())
            for tt in range(8):
                p.ev("vector", lambda e, tt=tt: e.max(s.m8_s[:, :], s.v8_s[:, tt * 8:(tt + 1) * 8]))
                p.ev("vector", lambda e: e.drain())
                p.ev("vector", lambda e, tt=tt: e.tensor_scalar(
                    s.nm12_s[:, tt:tt + 1], s.m8_s[:, 0:1], s.m8_s[:, 1:2], -1.0, OP.add, OP.mult))
                p.ev("vector", lambda e: e.drain())
                p.ev("vector", lambda e, tt=tt: e.tensor_scalar(
                    s.ge_s[:, :], s.v8_s[:, tt * 8:(tt + 1) * 8], s.m8_s[:, 1:2], None, OP.is_ge))
                p.ev("vector", lambda e: e.drain(), inc="DVE")
                p.ev("scalar", lambda e, tt=tt: e.activation(
                    s.sg_s[:, :], s.v8_s[:, tt * 8:(tt + 1) * 8], AF.Sigmoid,
                    bias=s.nm12_s[:, tt:tt + 1], scale=2.0),
                    waits={"DVE": C["DVE"]}, inc="ACT")
                p.ev("vector", lambda e, tt=tt: e.tensor_tensor(
                    s.comb_s[:, tt * 8:(tt + 1) * 8], s.sg_s[:, :], s.ge_s[:, :], op=OP.mult),
                    waits={"ACT": C["ACT"]})
                p.ev("vector", lambda e: e.drain(), inc="DVE")
                p.ev("tensor", lambda e, tt=tt: e.transpose(
                    plg[0:8, tt * 128:(tt + 1) * 128], s.comb_s[:, tt * 8:(tt + 1) * 8], s.idt_s[:, :]),
                    waits={"DVE": C["DVE"]}, inc="PE")
                p.ev("vector", lambda e, tt=tt: e.tensor_copy(
                    s.combT_s[0:8, tt * 128:(tt + 1) * 128], plg[0:8, tt * 128:(tt + 1) * 128]),
                    waits={"PE": C["PE"]}, inc="DVE")
            p.ev("vector", lambda e: e.drain(), inc="DVE")
            for hf in range(2):
                p.ev("tensor", lambda e, hf=hf: e.matmul(
                    pcb[:, hf * 512:(hf + 1) * 512], s.ind8_s[0:8, :], s.combT_s[0:8, hf * 512:(hf + 1) * 512],
                    start=True, stop=True),
                    waits=({"DVE": C["DVE"]} if hf == 0 else {}), inc=("PE" if hf == 1 else None))
            p.ev("vector", lambda e: e.tensor_copy(s.h_s[:, :], ph[:, :]), waits={"PE": pe_h})
            p.ev("vector", lambda e: e.drain())
            p.ev("vector", lambda e: e.tensor_tensor(s.hw_s[:, :], s.h_s[:, :], pcb[:, :], op=OP.mult),
                 waits={"PE": C["PE"]})
            p.ev("vector", lambda e: e.drain(), inc="DVE")
            hw_dve = C["DVE"]
            rt_ps.close()
            psB0 = sc.enter_context(nc.psum_tensor(Builder.uname("ps_B0"), [128, TPC], F32))
            psB1 = sc.enter_context(nc.psum_tensor(Builder.uname("ps_B1"), [128, TPC], F32))
            psB = [psB0, psB1]
            gs_ready = {}
            act_chunk_end = {}
            pe_chunk_end = {}
            for c in range(NC):
                if c + 1 < NC:
                    w = {}
                    if c >= 1:
                        w = {"PE": pe_chunk_end[c - 1]}
                    slot = f"OW{(c + 1) % 2}"
                    p.ev("gpsimd", lambda e, c=c, L=L: e.dma_start(
                        out=s.wbb[(c + 1) % 2][:, :], in_=s.W[f"wb{L}"][c + 1]), waits=w, inc=slot, n=16)
                    ow_ready[c + 1] = C[slot]
                ww = {f"OW{c % 2}": ow_ready[c], "DVE": hw_dve}
                if c >= 2:
                    ww["ACT"] = act_chunk_end[c - 2]
                for d_ in range(NC):
                    for hf in range(2):
                        p.ev("tensor", lambda e, c=c, d_=d_, hf=hf: e.matmul(
                            psB[c % 2][:, hf * 512:(hf + 1) * 512], s.wbb[c % 2][:, d_ * 128:(d_ + 1) * 128],
                            s.xn[:, d_ * TPC + hf * 512: d_ * TPC + hf * 512 + 512],
                            start=(d_ == 0), stop=False),
                            waits=(ww if (d_ == 0 and hf == 0) else {}))
                for hf in range(2):
                    p.ev("tensor", lambda e, c=c, hf=hf, L=L: e.matmul(
                        psB[c % 2][:, hf * 512:(hf + 1) * 512], s.ball_s[:, c * 128:(c + 1) * 128],
                        s.hw_s[:, hf * 512:(hf + 1) * 512], start=False, stop=True),
                        waits=({"W": w_ready} if (c == 0 and hf == 0) else {}),
                        inc=("PE" if hf == 1 else None))
                pe_chunk_end[c] = C["PE"]
                w = {"PE": C["PE"]}
                if c >= 2:
                    w[f"GS{c % 2}"] = gs_ready[c - 2]
                p.ev("scalar", lambda e, c=c, L=L: e.activation(
                    s.g_s[c % 2][:, :], psB[c % 2][:, :], AF.Silu, bias=s.bb_s[L][:, c:c + 1]),
                    waits=w, inc="ACT")
                act_chunk_end[c] = C["ACT"]
                p.ev("sync", lambda e, c=c: e.dma_start(
                    out=self.g_buf[c * 128:(c + 1) * 128, :], in_=s.g_s[c % 2][:, :]),
                    waits={"ACT": C["ACT"]}, inc=f"GS{c % 2}", n=16)
                gs_ready[c] = C[f"GS{c % 2}"]
            self.gs_done = (C["GS0"], C["GS1"])
            p.ev("sync", None, waits={"GS0": C["GS0"], "GS1": C["GS1"]})
            emit_block(nc, p, self.sems, f"ffng{L}", self.trace)

    def phase_out(self, L, xsrc, xdst, out_gate):
        nc, C, s = self.nc, self.C, self
        p = Plan(C)
        with ExitStack() as sc:
            sbuf = lambda name, shape, dt: sc.enter_context(nc.sbuf_tensor(Builder.uname(name), shape, dt))
            psA0 = sc.enter_context(nc.psum_tensor(Builder.uname("ps_A0"), [128, TPC], F32))
            psA1 = sc.enter_context(nc.psum_tensor(Builder.uname("ps_A1"), [128, TPC], F32))
            s.xbuf = [sbuf(f"xbuf{i}", [128, TPC], F32) for i in range(2)]
            s.gld = [sbuf(f"gld{i}", [128, TPC], F32) for i in range(2)]
            s.wob = [sbuf(f"wob{i}", [128, 4096], BF16) for i in range(2)]
            psA = [psA0, psA1]
            p.ev("gpsimd", lambda e, L=L: e.dma_start(out=s.wob[0][:, :], in_=s.W[f"wo{L}"][0]), inc="OW0", n=16)
            ow_ready = {0: C["OW0"]}
            xb_ready, gl_ready, out_ready = {}, {}, {}
            pe_chunk_end, dve_chunk_end = {}, {}
            gs0, gs1 = self.gs_done
            for c in range(NC):
                if c + 1 < NC:
                    w = {}
                    if c >= 1:
                        w = {"PE": pe_chunk_end[c - 1]}
                    slot = f"OW{(c + 1) % 2}"
                    p.ev("gpsimd", lambda e, c=c, L=L: e.dma_start(
                        out=s.wob[(c + 1) % 2][:, :], in_=s.W[f"wo{L}"][c + 1]), waits=w, inc=slot, n=16)
                    ow_ready[c + 1] = C[slot]
                w = {}
                if c == 0 and out_gate:
                    w.update(out_gate)
                if c >= 2:
                    w["DVE"] = dve_chunk_end[c - 2]
                    w[f"OUT{c % 2}"] = out_ready[c - 2]
                p.ev("sync", lambda e, c=c: e.dma_start(
                    out=s.xbuf[c % 2][:, :], in_=xsrc[c * 128:(c + 1) * 128, :]),
                    waits=w, inc=f"XB{c % 2}", n=16)
                xb_ready[c] = C[f"XB{c % 2}"]
                w = {f"GS{c % 2}": gs0 if c % 2 == 0 else gs1}
                if c >= 2:
                    w["DVE"] = dve_chunk_end[c - 2]
                p.ev("sync", lambda e, c=c: e.dma_start(
                    out=s.gld[c % 2][:, :], in_=self.g_buf[c * 128:(c + 1) * 128, :]),
                    waits=w, inc=f"GL{c % 2}", n=16)
                gl_ready[c] = C[f"GL{c % 2}"]
                ww = {f"OW{c % 2}": ow_ready[c]}
                if c >= 2:
                    ww["DVE"] = dve_chunk_end[c - 2]
                for d_ in range(NC):
                    for hf in range(2):
                        p.ev("tensor", lambda e, c=c, d_=d_, hf=hf: e.matmul(
                            psA[c % 2][:, hf * 512:(hf + 1) * 512], s.wob[c % 2][:, d_ * 128:(d_ + 1) * 128],
                            s.ao[:, d_ * TPC + hf * 512: d_ * TPC + hf * 512 + 512],
                            start=(d_ == 0), stop=(d_ == NC - 1)),
                            waits=(ww if (d_ == 0 and hf == 0) else {}),
                            inc=("PE" if (d_ == NC - 1 and hf == 1) else None))
                pe_chunk_end[c] = C["PE"]
                w = {"PE": C["PE"], f"GL{c % 2}": gl_ready[c], f"XB{c % 2}": xb_ready[c]}
                p.ev("vector", lambda e, c=c: e.tensor_tensor(
                    s.gld[c % 2][:, :], psA[c % 2][:, :], s.gld[c % 2][:, :], op=OP.mult), waits=w)
                p.ev("vector", lambda e: e.drain())
                p.ev("vector", lambda e, c=c: e.tensor_tensor(
                    s.xbuf[c % 2][:, :], s.gld[c % 2][:, :], s.xbuf[c % 2][:, :], op=OP.add))
                p.ev("vector", lambda e: e.drain(), inc="DVE")
                dve_chunk_end[c] = C["DVE"]
                p.ev("sync", lambda e, c=c: e.dma_start(
                    out=xdst[c * 128:(c + 1) * 128, :], in_=s.xbuf[c % 2][:, :]),
                    waits={"DVE": C["DVE"]}, inc=f"OUT{c % 2}", n=16)
                out_ready[c] = C[f"OUT{c % 2}"]
            p.ev("sync", None, waits={"OUT0": C["OUT0"], "OUT1": C["OUT1"]})
            emit_block(nc, p, self.sems, f"out{L}", self.trace)

    def phase_final(self):
        nc, C = self.nc, self.C
        p = Plan(C)
        dram_srcs = {"x_cur": self.x_cur}
        sbuf_srcs = {"xn": getattr(self, "xn", None), "ao": getattr(self, "ao", None),
                     "hw": getattr(self, "hw_s", None)}
        for name in self.dbg:
            if name in dram_srcs:
                src = dram_srcs[name]
                dd = nc.dram_tensor(f"dbg_{name}", list(src.shape), src.dtype, kind="ExternalOutput")
                p.ev("sync", lambda e, dd=dd, src=src: e.dma_start(out=dd[:, :], in_=src[:, :]),
                     waits={"OUT0": C["OUT0"], "OUT1": C["OUT1"], "CC": C["CC"]}, inc="OUT0", n=16)
            else:
                src = sbuf_srcs[name]
                shp = [int(x) for x in src.shape]
                dd = nc.dram_tensor(f"dbg_{name}", shp, src.dtype, kind="ExternalOutput")
                p.ev("sync", lambda e, dd=dd, src=src: e.dma_start(out=dd[:, :], in_=src[:, :]),
                     waits={"OUT0": C["OUT0"], "OUT1": C["OUT1"]}, inc="OUT0", n=16)
        p.ev("sync", None, waits={"OUT0": C["OUT0"], "OUT1": C["OUT1"]})
        emit_block(nc, p, self.sems, "fin", self.trace)

    def build(self):
        self.phase_consts()
        out_gate = None
        stop = self.stop_after

        def done(tag):
            return stop == tag

        for L in range(2):
            xsrc = self.x_in if L == 0 else self.x_cur
            xdst = self.x_cur if L == 0 else self.out_fm
            self.phase_ln_kv(L, xsrc, out_gate)
            if done(f"ln{L}") or done(f"kv{L}"):
                break
            self.phase_ffng(L)
            if done(f"rout{L}"):
                break
            with ExitStack() as lscope:
                self.ao = lscope.enter_context(
                    self.nc.sbuf_tensor(Builder.uname("ao"), [128, NC * TPC], BF16))
                self.phase_att(L)
                if done(f"att{L}"):
                    break
                self.phase_out(L, xsrc, xdst, out_gate)
            if done(f"out{L}"):
                break
            out_gate = {"OUT0": self.C["OUT0"], "OUT1": self.C["OUT1"]}
        self.phase_final()
        self.ctx.close()
        return self.nc


def build_program(debug=None, stop_after=None, att_parts="full", att_kh=NKV):
    return Builder(debug, stop_after, att_parts, att_kh).build()


def kernel(**inputs):
    in_maps, col_maps = host_prepare(inputs)
    nc = build_program()
    res = run_bass_kernel_spmd(nc, in_maps, core_ids=list(range(8)))
    out = np.empty((B, S, D), dtype=np.float32)
    for core in range(8):
        b, cols_seq = col_maps[core]
        out[b, cols_seq, :] = res.results[core]["out_fm"].T
    return out
